# version 05
# speedup vs baseline: 1.2800x; 1.2748x over previous
import contextlib
import numpy as np
import concourse.bass as bass
import concourse.mybir as mybir
from concourse.bass_utils import run_bass_kernel_spmd

F32 = mybir.dt.float32
BF16 = mybir.dt.bfloat16
U32 = mybir.dt.uint32
AF = mybir.ActivationFunctionType
ALU = mybir.AluOpType
AX = mybir.AxisListType

B, T, D = 8, 2048, 1024
KB = D // 128
TB = T // 512
NCORES = 8


class Sem:
    __slots__ = ("h", "n")
    def __init__(self, h):
        self.h = h
        self.n = 0


class _MockInstr:
    def then_inc(self, *a, **k):
        return self

    def annotate(self, *a, **k):
        return self


class _MockEngine:
    def __getattr__(self, name):
        def f(*a, **k):
            return _MockInstr()
        return f


class ChainedInstr:
    __slots__ = ("ins", "sem", "val")

    def __init__(self, ins, sem, val):
        self.ins = ins
        self.sem = sem
        self.val = val

    def then_inc(self, *a, **k):
        raise AssertionError("auto-chained instruction already carries its inc")


_SEQ_OPS = {"wait_ge", "nop", "dma_start", "dma_start_transpose", "sem_inc",
            "drain", "indirect_dma_start", "halt"}


class ChainedEngine:

    def __init__(self, c, eng, semname):
        self._c = c
        self._eng = eng
        self._semname = semname

    def __getattr__(self, name):
        real = getattr(self._eng, name)
        if name in _SEQ_OPS or name.startswith("reg") or name.startswith("_"):
            return real
        c = self._c
        semname = self._semname

        def f(*a, **k):
            s = c.sems[semname]
            if s.n > 0:
                self._eng.wait_ge(s.h, s.n)
            ins = real(*a, **k)
            s.n += 1
            ins.then_inc(s.h, 1)
            key = (semname, semname)
            c.marks[(c.ename, semname)] = s.n
            return ChainedInstr(ins, semname, s.n)
        return f


class Ctx:

    def __init__(self, nc):
        self.nc = nc
        self.sems = {}
        self.events = {}
        self.marks = {}
        self.queue = []
        self.dry = True
        self.pending = set()
        self.eng = None
        self.ename = None
        self.const_pool = None
        self.const_map = {}

    NCONST = 64

    def const(self, val):
        val = float(val)
        if val in self.const_map:
            return
        idx = len(self.const_map)
        assert idx < self.NCONST
        self.const_map[val] = idx
        self.nc.const_aps.register(F32, val,
                                   self.const_pool[:, idx:idx + 1])

    def add_sem(self, name, handle):
        self.sems[name] = Sem(handle)

    def body(self, which, fn):
        self.queue.append((which, fn))

    def done(self, label, instr, semname=None, k=1):
        if isinstance(instr, ChainedInstr):
            ev = (instr.sem, instr.val)
            if self.dry:
                self.events[label] = ev
            else:
                assert self.events.get(label) == ev, (
                    f"nondeterministic emission at {label}: "
                    f"{self.events.get(label)} vs {ev}")
            return instr.val
        s = self.sems[semname]
        s.n += k
        instr.then_inc(s.h, k)
        if self.dry:
            self.events[label] = (semname, s.n)
        else:
            assert self.events.get(label) == (semname, s.n), (
                f"nondeterministic emission at {label}: "
                f"{self.events.get(label)} vs {(semname, s.n)}")
        if semname not in ("qg", "qs", "qa"):
            key = (self.ename, semname)
            self.marks[key] = max(self.marks.get(key, 0), s.n)
        return s.n

    def alias(self, label, other):
        if self.dry:
            if other in self.events:
                self.events[label] = self.events[other]
            else:
                self.pending.add(other)
                self.events[label] = ("__alias__", other)
        else:
            pass

    def need(self, *labels):
        if self.dry:
            for lb in labels:
                if lb not in self.events:
                    self.pending.add(lb)
            return
        for lb in labels:
            semname, v = self.events[lb]
            while semname == "__alias__":
                semname, v = self.events[v]
            key = (self.ename, semname)
            if self.marks.get(key, 0) < v:
                self.eng.wait_ge(self.sems[semname].h, v)
                self.marks[key] = v

    _QSEM = {"gpsimd": "qg", "sync": "qs", "scalar": "qa"}

    def dma_batch(self, label, pairs, sem=None):
        if sem is None:
            sem = self._QSEM[self.ename]
        ins = None
        for out, in_ in pairs:
            ins = self.eng.dma_start(out=out, in_=in_)
            s = self.sems[sem]
            s.n += 16
            ins.then_inc(s.h, 16)
        self.events[label] = (sem, self.sems[sem].n) if self.dry else \
            self.events[label]
        if not self.dry:
            assert self.events[label] == (sem, self.sems[sem].n)
        return self.events[label]

    def run(self, blk):
        if self.const_map:
            cm = dict(self.const_map)

            def cinit(g):
                ins = None
                for val, idx in cm.items():
                    ins = g.memset(self.const_pool[:, idx:idx + 1], val)
                c.done("consts", ins, "gg")
            c = self
            engines_used = {w for w, _ in self.queue} - {"gpsimd"}

            def gate(e):
                c.need("consts")
            self.queue = ([("gpsimd", cinit)]
                          + [(w, gate) for w in sorted(engines_used)]
                          + self.queue)
        _CHAINSEM = {"vector": "vv", "scalar": "ss", "gpsimd": "gg"}

        def wrap_eng(eng, which):
            if which in _CHAINSEM:
                return ChainedEngine(self, eng, _CHAINSEM[which])
            return eng
        self.dry = True
        mock = _MockEngine()
        for which, fn in self.queue:
            self.eng, self.ename = mock, which
            fn(wrap_eng(mock, which))
        missing = self.pending - set(self.events)
        assert not missing, f"events never produced: {sorted(missing)[:10]}"
        self.dry = False
        self.marks = {}
        for s in self.sems.values():
            s.n = 0
        for which, fn in self.queue:
            def wrap(eng, fn=fn, which=which):
                self.eng, self.ename = eng, which
                fn(wrap_eng(eng, which))
            getattr(blk, which)(wrap)


def mm_accum(c, t, psum_ap, lhsT_list, rhs_list):
    n = len(lhsT_list)
    ins = None
    for i, (lh, rh) in enumerate(zip(lhsT_list, rhs_list)):
        ins = t.matmul(psum_ap, lh, rh, start=(i == 0), stop=(i == n - 1))
    return ins


def emit_ln2(c, X, XN, tag, kblocks=KB):
    nc = c.nc
    c.const(1e-5)
    with (
        nc.sbuf_tensor(f"{tag}_sq0", [128, T], F32) as SQ0,
        nc.sbuf_tensor(f"{tag}_sq1", [128, T], F32) as SQ1,
        nc.sbuf_tensor(f"{tag}_rows", [1, 4 * T], F32) as ROWS,
        nc.sbuf_tensor(f"{tag}_oc", [128, 1], F32) as ONESC,
        nc.sbuf_tensor(f"{tag}_or", [1, 128], F32) as ONESR,
        nc.psum_tensor(f"{tag}_pst", [1, T], F32) as PS_STAT,
        nc.psum_tensor(f"{tag}_psa", [128, T], F32) as PS_A,
    ):
        xsq = [SQ0, SQ1]

        def g_body(g):
            g.memset(ONESC[:], 1.0)
            ins = g.memset(ONESR[:], 1.0)
            c.done(f"{tag}ones", ins, "gg")
        c.body("gpsimd", g_body)

        def t_body(t):
            c.need(f"{tag}ones")
            for k in range(kblocks):
                c.need(f"{tag}in{k}")
            for nb in range(TB):
                ins = mm_accum(c, t, PS_STAT[:, nb * 512:nb * 512 + 512],
                               [ONESC[:, 0:1]] * kblocks,
                               [X[:, k * T + nb * 512: k * T + nb * 512 + 512]
                                for k in range(kblocks)])
            c.done(f"{tag}sxmm", ins, "tt")
            c.need(f"{tag}sxev")
            for k in range(kblocks):
                c.need(f"{tag}sq{k}")
                for nb in range(TB):
                    ins = t.matmul(PS_STAT[:, nb * 512:nb * 512 + 512],
                                   ONESC[:, 0:1],
                                   xsq[k % 2][:, nb * 512:nb * 512 + 512],
                                   start=(k == 0), stop=(k == kblocks - 1))
                c.done(f"{tag}qmm{k}", ins, "tt")
            c.need(f"{tag}rowa")
            for nb in range(TB):
                ins = t.matmul(PS_A[:, nb * 512:nb * 512 + 512], ONESR[:],
                               ROWS[:, 2 * T + nb * 512:2 * T + nb * 512 + 512],
                               start=True, stop=True)
            c.done(f"{tag}bca", ins, "tt")
            c.need(f"{tag}aev", f"{tag}rowb")
            for nb in range(TB):
                ins = t.matmul(PS_A[:, nb * 512:nb * 512 + 512], ONESR[:],
                               ROWS[:, 3 * T + nb * 512:3 * T + nb * 512 + 512],
                               start=True, stop=True)
            c.done(f"{tag}bcb", ins, "tt")
        c.body("tensor", t_body)

        def s_body(s):
            for k in range(kblocks):
                c.need(f"{tag}in{k}")
                if k >= 2:
                    c.need(f"{tag}qmm{k - 2}")
                ins = s.activation(xsq[k % 2][:], X[:, k * T:(k + 1) * T], AF.Square)
                c.done(f"{tag}sq{k}", ins, "ss")
            c.need(f"{tag}m")
            ins = s.activation(ROWS[:, 3 * T:4 * T], ROWS[:, T:2 * T], AF.Square)
            c.done(f"{tag}msq", ins, "ss")
            c.need(f"{tag}var")
            ins = s.activation(ROWS[:, 2 * T:3 * T], ROWS[:, 2 * T:3 * T],
                               AF.Sqrt, bias=1e-5)
            c.done(f"{tag}sd", ins, "ss")
        c.body("scalar", s_body)

        def v_body(v):
            c.need(f"{tag}sxmm")
            ins = v.tensor_copy(ROWS[:, 0:T], PS_STAT[:, 0:T])
            c.done(f"{tag}sxev", ins, "vv")
            ins = v.tensor_scalar_mul(ROWS[:, T:2 * T], ROWS[:, 0:T], 1.0 / D)
            c.done(f"{tag}m", ins, "vv")
            c.need(f"{tag}qmm{kblocks - 1}")
            ins = v.tensor_copy(ROWS[:, 0:T], PS_STAT[:, 0:T])
            c.need(f"{tag}msq")
            ins = v.scalar_tensor_tensor(ROWS[:, 2 * T:3 * T], ROWS[:, 0:T],
                                         1.0 / D, ROWS[:, 3 * T:4 * T],
                                         ALU.mult, ALU.subtract)
            c.done(f"{tag}var", ins, "vv")
            c.need(f"{tag}sd")
            ins = v.reciprocal(ROWS[:, 2 * T:3 * T], ROWS[:, 2 * T:3 * T])
            c.done(f"{tag}rowa", ins, "vv")
            ins = v.scalar_tensor_tensor(ROWS[:, 3 * T:4 * T], ROWS[:, T:2 * T],
                                         -1.0, ROWS[:, 2 * T:3 * T],
                                         ALU.mult, ALU.mult)
            c.done(f"{tag}rowb", ins, "vv")
            c.need(f"{tag}bca")
            ins = v.tensor_copy(SQ1[:], PS_A[:, 0:T])
            c.done(f"{tag}aev", ins, "vv")
            c.need(f"{tag}bcb")
            for k in range(kblocks):
                v.tensor_tensor(SQ0[:], X[:, k * T:(k + 1) * T], SQ1[:], ALU.mult)
                ins = v.tensor_tensor(XN[:, k * T:(k + 1) * T], SQ0[:],
                                      PS_A[:, 0:T], ALU.add)
                c.done(f"{tag}xn{k}", ins, "vv")
            c.alias(f"{tag}xn", f"{tag}xn{kblocks - 1}")
        c.body("vector", v_body)


def emit_ln3(c, X, XN, tag, kblocks=KB, in_bf16=False, alpha=None, beta=None,
             ddof1=False, eps=1e-5, eps_outside=False, after=None):
    nc = c.nc
    N = kblocks * 128
    c.const(float(eps))
    dt_in = BF16 if in_bf16 else F32
    with (
        nc.sbuf_tensor(f"{tag}_sq0", [128, T], F32) as SQ0,
        nc.sbuf_tensor(f"{tag}_sq1", [128, T], F32) as SQ1,
        nc.sbuf_tensor(f"{tag}_rows", [1, 4 * T], F32) as ROWS,
        nc.sbuf_tensor(f"{tag}_oc", [128, 1], dt_in) as ONESC,
        nc.sbuf_tensor(f"{tag}_ocf", [128, 1], F32) as ONESCF,
        nc.sbuf_tensor(f"{tag}_or", [1, 128], F32) as ONESR,
        nc.psum_tensor(f"{tag}_pst", [1, T], F32) as PS_STAT,
        nc.psum_tensor(f"{tag}_psa", [128, T], F32) as PS_A,
    ):
        xsq = [SQ0, SQ1]

        def g_body(g):
            if after is not None:
                c.need(after)
            g.memset(ONESC[:], 1.0)
            g.memset(ONESCF[:], 1.0)
            ins = g.memset(ONESR[:], 1.0)
            c.done(f"{tag}ones", ins, "gg")
        c.body("gpsimd", g_body)

        def t_body(t):
            c.need(f"{tag}ones")
            for k in range(kblocks):
                c.need(f"{tag}in{k}")
            for nb in range(TB):
                ins = mm_accum(c, t, PS_STAT[:, nb * 512:nb * 512 + 512],
                               [ONESC[:, 0:1]] * kblocks,
                               [X[:, k * T + nb * 512: k * T + nb * 512 + 512]
                                for k in range(kblocks)])
            c.done(f"{tag}sxmm", ins, "tt")
            c.need(f"{tag}sxev")
            for k in range(kblocks):
                c.need(f"{tag}sq{k}")
                for nb in range(TB):
                    ins = t.matmul(PS_STAT[:, nb * 512:nb * 512 + 512],
                                   ONESCF[:, 0:1],
                                   xsq[k % 2][:, nb * 512:nb * 512 + 512],
                                   start=(k == 0), stop=(k == kblocks - 1))
                c.done(f"{tag}qmm{k}", ins, "tt")
            c.need(f"{tag}rowa")
            for nb in range(TB):
                ins = t.matmul(PS_A[:, nb * 512:nb * 512 + 512], ONESR[:],
                               ROWS[:, 2 * T + nb * 512:2 * T + nb * 512 + 512],
                               start=True, stop=True)
            c.done(f"{tag}bca", ins, "tt")
            c.need(f"{tag}aev", f"{tag}rowb")
            for nb in range(TB):
                ins = t.matmul(PS_A[:, nb * 512:nb * 512 + 512], ONESR[:],
                               ROWS[:, 3 * T + nb * 512:3 * T + nb * 512 + 512],
                               start=True, stop=True)
            c.done(f"{tag}bcb", ins, "tt")
        c.body("tensor", t_body)

        def s_body(s):
            for k in range(kblocks):
                c.need(f"{tag}in{k}")
                if k >= 2:
                    c.need(f"{tag}qmm{k - 2}")
                ins = s.activation(xsq[k % 2][:], X[:, k * T:(k + 1) * T],
                                   AF.Square)
                c.done(f"{tag}sq{k}", ins, "ss")
            c.need(f"{tag}m")
            ins = s.activation(ROWS[:, 3 * T:4 * T], ROWS[:, T:2 * T], AF.Square)
            c.done(f"{tag}msq", ins, "ss")
            c.need(f"{tag}var")
            if eps_outside:
                ins = s.activation(ROWS[:, 2 * T:3 * T], ROWS[:, 2 * T:3 * T],
                                   AF.Sqrt)
            else:
                ins = s.activation(ROWS[:, 2 * T:3 * T], ROWS[:, 2 * T:3 * T],
                                   AF.Sqrt, bias=float(eps))
            c.done(f"{tag}sd", ins, "ss")
        c.body("scalar", s_body)

        def v_body(v):
            c.need(f"{tag}sxmm")
            ins = v.tensor_copy(ROWS[:, 0:T], PS_STAT[:, 0:T])
            c.done(f"{tag}sxev", ins, "vv")
            ins = v.tensor_scalar_mul(ROWS[:, T:2 * T], ROWS[:, 0:T], 1.0 / N)
            c.done(f"{tag}m", ins, "vv")
            c.need(f"{tag}qmm{kblocks - 1}")
            ins = v.tensor_copy(ROWS[:, 0:T], PS_STAT[:, 0:T])
            c.need(f"{tag}msq")
            vscale = (1.0 / N) * (N / (N - 1.0)) if ddof1 else (1.0 / N)
            msc = (N / (N - 1.0)) if ddof1 else 1.0
            v.scalar_tensor_tensor(ROWS[:, 2 * T:3 * T], ROWS[:, 0:T],
                                   vscale, ROWS[:, 3 * T:4 * T],
                                   ALU.mult, ALU.bypass)
            ins = v.scalar_tensor_tensor(ROWS[:, 2 * T:3 * T],
                                         ROWS[:, 3 * T:4 * T], -msc,
                                         ROWS[:, 2 * T:3 * T],
                                         ALU.mult, ALU.add)
            c.done(f"{tag}var", ins, "vv")
            c.need(f"{tag}sd")
            if eps_outside:
                v.tensor_scalar_add(ROWS[:, 2 * T:3 * T], ROWS[:, 2 * T:3 * T],
                                    float(eps))
            ins = v.reciprocal(ROWS[:, 2 * T:3 * T], ROWS[:, 2 * T:3 * T])
            c.done(f"{tag}rowa", ins, "vv")
            ins = v.scalar_tensor_tensor(ROWS[:, 3 * T:4 * T], ROWS[:, T:2 * T],
                                         -1.0, ROWS[:, 2 * T:3 * T],
                                         ALU.mult, ALU.mult)
            c.done(f"{tag}rowb", ins, "vv")
            c.need(f"{tag}bca")
            ins = v.tensor_copy(SQ1[:], PS_A[:, 0:T])
            c.done(f"{tag}aev", ins, "vv")
            c.need(f"{tag}bcb")
            for k in range(kblocks):
                v.tensor_tensor(SQ0[:], X[:, k * T:(k + 1) * T], SQ1[:],
                                ALU.mult)
                if alpha is not None:
                    v.tensor_tensor(SQ0[:], SQ0[:], PS_A[:, 0:T], ALU.add)
                    ins = v.tensor_scalar(XN[:, k * T:(k + 1) * T], SQ0[:],
                                          alpha[:, k:k + 1], beta[:, k:k + 1],
                                          ALU.mult, ALU.add)
                else:
                    ins = v.tensor_tensor(XN[:, k * T:(k + 1) * T], SQ0[:],
                                          PS_A[:, 0:T], ALU.add)
                c.done(f"{tag}xn{k}", ins, "vv")
            c.alias(f"{tag}xn", f"{tag}xn{kblocks - 1}")
        c.body("vector", v_body)


def emit_s1(c, W):
    nc = c.nc
    c.const(1.0 / 32.0)
    est = contextlib.ExitStack()
    with est:
        X = est.enter_context(nc.sbuf_tensor("s1X", [128, KB * T], F32))
        BQ = est.enter_context(nc.sbuf_tensor("s1bq", [128, 16], F32))
        BV = est.enter_context(nc.sbuf_tensor("s1bv", [1, D], BF16))
        EYE = est.enter_context(nc.sbuf_tensor("s1eye", [128, 128], BF16))
        ONESRB = est.enter_context(nc.sbuf_tensor("s1orb", [1, 128], BF16))
        SROW = est.enter_context(nc.sbuf_tensor("s1srow", [128, 8], F32))
        chain_gate = getattr(c, "chain_gate", None)

        def qload0(g):
            if chain_gate:
                c.need(chain_gate)
            pairs = [(X[:, k * T:(k + 1) * T],
                      W["xT"][k * 128:(k + 1) * 128, :]) for k in range(KB)]
            pairs += [(BQ[:], W["bq_col"]), (BV[:], W["bv_row"]),
                      (EYE[:], W["eye"])]
            c.dma_batch("in0", pairs)
            for k in range(KB):
                c.alias(f"s1_in{k}", "in0")
            for lb in ("ldbq", "ldbv", "ldeye"):
                c.alias(lb, "in0")
            ins = g.memset(ONESRB[:], 1.0)
            c.done("onesrb", ins, "gg")
        c.body("gpsimd", qload0)

        with nc.sbuf_tensor("s1xn", [128, KB * T], BF16) as XN:
            emit_ln2(c, X, XN, "s1_")

            QT = est.enter_context(
                nc.sbuf_tensor("s1qt", [128, KB * T], BF16, side="right"))
            KT = est.enter_context(
                nc.sbuf_tensor("s1kt", [128, KB * T], BF16, side="right"))
            VT = est.enter_context(
                nc.sbuf_tensor("s1vt", [128, 16 * 1024], BF16, side="right"))

            with (
                nc.sbuf_tensor("s1wb0", [128, KB * 256], BF16) as WB0,
                nc.sbuf_tensor("s1wb1", [128, KB * 256], BF16) as WB1,
                nc.psum_tensor("s1pj0", [128, 512], F32) as PJ0,
                nc.psum_tensor("s1pj1", [128, 512], F32) as PJ1,
                nc.psum_tensor("s1pj2", [128, 512], F32) as PJ2,
                nc.psum_tensor("s1pj3", [128, 512], F32) as PJ3,
            ):
                WBs = [WB0, WB1]
                PJ = [PJ0, PJ1, PJ2, PJ3]
                NG = 12

                def qload1(g):
                    c.need("s1_xn")
                    for grp in range(NG):
                        if grp >= 2:
                            c.need(f"pgrp{grp - 2}")
                        wb = WBs[grp % 2]
                        c.dma_batch(
                            f"ldw{grp}",
                            [(wb[:, k * 256:(k + 1) * 256],
                              W["wqkvT"][k * 128:(k + 1) * 128,
                                         grp * 256:(grp + 1) * 256])
                             for k in range(KB)],
                            sem=f"w{grp % 2}")
                c.body("gpsimd", qload1)

                def jobs_of(grp):
                    if grp < 8:
                        return [("qk", grp, m, tb) for m in range(2)
                                for tb in range(TB)]
                    else:
                        return [("v", grp, tc, 0) for tc in range(16)]

                def t_body2(t):
                    c.need("s1_xn", "onesrb", "ldbv")
                    gi = 0
                    for grp in range(NG):
                        c.need(f"ldw{grp}")
                        wb = WBs[grp % 2]
                        for job in jobs_of(grp):
                            if gi >= 4:
                                c.need(f"pjev{gi - 4}")
                            ps = PJ[gi % 4]
                            if job[0] == "qk":
                                _, _, m, tb = job
                                ins = mm_accum(
                                    c, t, ps[:],
                                    [wb[:, k * 256 + m * 128:
                                        k * 256 + m * 128 + 128]
                                     for k in range(KB)],
                                    [XN[:, k * T + tb * 512:
                                        k * T + tb * 512 + 512]
                                     for k in range(KB)])
                            else:
                                _, _, tc, _ = job
                                for k in range(KB):
                                    t.matmul(ps[:, 0:256],
                                             XN[:, k * T + tc * 128:
                                                k * T + tc * 128 + 128],
                                             wb[:, k * 256:(k + 1) * 256],
                                             start=(k == 0), stop=False)
                                ins = t.matmul(
                                    ps[:, 0:256], ONESRB[:],
                                    BV[:, (grp - 8) * 256:(grp - 8) * 256 + 256],
                                    start=False, stop=True)
                            c.done(f"pjmm{gi}", ins, "tt")
                            gi += 1
                        c.alias(f"pgrp{grp}", f"pjmm{gi - 1}")
                c.body("tensor", t_body2)

                def s_body2(s):
                    gi = 0
                    for grp in range(NG):
                        for job in jobs_of(grp):
                            if job[0] == "qk":
                                c.need(f"pjmm{gi}")
                                _, _, m, tb = job
                                mm = grp * 2 + m
                                dst = QT if mm < 8 else KT
                                mloc = mm % 8
                                ins = s.activation(
                                    dst[:, mloc * T + tb * 512:
                                        mloc * T + tb * 512 + 512],
                                    PJ[gi % 4][:], AF.Identity,
                                    bias=BQ[:, mm:mm + 1])
                                c.done(f"pjev{gi}", ins, "ss")
                                last_qk = gi
                            gi += 1
                    c.alias("qk_evac_all", f"pjev{last_qk}")
                c.body("scalar", s_body2)

                def v_body2(v):
                    gi = 0
                    ins = None
                    for grp in range(NG):
                        for job in jobs_of(grp):
                            if job[0] == "v":
                                c.need(f"pjmm{gi}")
                                _, _, tc, _ = job
                                eb = grp - 8
                                ins = v.tensor_copy(
                                    VT[:, tc * 1024 + eb * 256:
                                       tc * 1024 + eb * 256 + 256],
                                    PJ[gi % 4][:, 0:256])
                                c.done(f"pjev{gi}", ins, "vv")
                                last_v = gi
                            gi += 1
                    c.alias("v_all", f"pjev{last_v}")
                c.body("vector", v_body2)

        with (
            nc.sbuf_tensor("s1at", [128, 16 * 512], BF16) as ATT,
            nc.sbuf_tensor("s1aa", [128, 4 * T], BF16) as AA,
            nc.psum_tensor("s1ps_s", [128, T], F32) as PS_S,
            nc.psum_tensor("s1ps_t0", [128, 128], BF16) as PS_T0,
            nc.psum_tensor("s1ps_t1", [128, 128], BF16) as PS_T1,
            nc.psum_tensor("s1ps_o0", [128, 512], F32) as PS_O0,
            nc.psum_tensor("s1ps_o1", [128, 512], F32) as PS_O1,
        ):
            PS_T = [PS_T0, PS_T1]
            PS_O = [PS_O0, PS_O1]

            def t_body3(t):
                c.need("qk_evac_all", "v_all")
                c.need("ldeye")
                for qg in range(4):
                    for qb in range(4):
                        qi = qg * 4 + qb
                        if qi > 0:
                            c.need(f"anorm{qi - 1}")
                        for sb in range(TB):
                            ins = mm_accum(c, t,
                                           PS_S[:, sb * 512:sb * 512 + 512],
                                           [QT[:, k * T + qi * 128:
                                               k * T + qi * 128 + 128]
                                            for k in range(KB)],
                                           [KT[:, k * T + sb * 512:
                                               k * T + sb * 512 + 512]
                                            for k in range(KB)])
                        c.done(f"smm{qi}", ins, "tt")
                        c.need(f"anorm{qi}")
                        for sc in range(16):
                            if sc >= 2:
                                c.need(f"tpe{qi}_{sc - 2}")
                            ins = t.transpose(PS_T[sc % 2][:],
                                              AA[:, qb * T + sc * 128:
                                                 qb * T + sc * 128 + 128],
                                              EYE[:])
                            c.done(f"tp{qi}_{sc}", ins, "tt")
                    for e in range(KB):
                        gg = qg * KB + e
                        if gg >= 2:
                            pg, pe = divmod(gg - 2, KB)
                            c.need(f"oev{pg}_{pe}")
                        c.need(f"tpe{qg * 4 + 3}_15")
                        ins = mm_accum(c, t, PS_O[gg % 2][:],
                                       [VT[:, sc * 1024 + e * 128:
                                           sc * 1024 + e * 128 + 128]
                                        for sc in range(16)],
                                       [ATT[:, sc * 512:sc * 512 + 512]
                                        for sc in range(16)])
                        c.done(f"omm{qg}_{e}", ins, "tt")
            c.body("tensor", t_body3)

            def v_body3(v):
                ins = None
                for qg in range(4):
                    for qb in range(4):
                        qi = qg * 4 + qb
                        c.need(f"smm{qi}")
                        v.reduce_max(SROW[:, 0:1], PS_S[:, 0:T], AX.X)
                        ins = v.tensor_scalar_mul(SROW[:, 1:2], SROW[:, 0:1],
                                                  -1.0 / 32.0)
                        c.done(f"srow{qi}", ins, "vv")
                        c.need(f"sexp{qi}")
                        v.reciprocal(SROW[:, 4:5], SROW[:, 3:4])
                        ins = v.tensor_scalar_mul(AA[:, qb * T:(qb + 1) * T],
                                                  AA[:, qb * T:(qb + 1) * T],
                                                  SROW[:, 4:5])
                        c.done(f"anorm{qi}", ins, "vv")
                    for e in range(KB):
                        c.need(f"omm{qg}_{e}")
                        ins = v.tensor_tensor(
                            X[:, e * T + qg * 512: e * T + qg * 512 + 512],
                            X[:, e * T + qg * 512: e * T + qg * 512 + 512],
                            PS_O[(qg * KB + e) % 2][:], ALU.add)
                        c.done(f"oev{qg}_{e}", ins, "vv")
                c.alias("res_done", f"oev3_{KB - 1}")
            c.body("vector", v_body3)

            def s_body3(s):
                for qg in range(4):
                    for qb in range(4):
                        qi = qg * 4 + qb
                        c.need(f"srow{qi}")
                        ins = s.activation(AA[:, qb * T:(qb + 1) * T],
                                           PS_S[:, 0:T], AF.Exp,
                                           bias=SROW[:, 1:2], scale=1.0 / 32.0,
                                           accum_out=SROW[:, 3:4])
                        c.done(f"sexp{qi}", ins, "ss")
                        for sc in range(16):
                            c.need(f"tp{qi}_{sc}")
                            ins = s.activation(
                                ATT[:, sc * 512 + qb * 128:
                                    sc * 512 + qb * 128 + 128],
                                PS_T[sc % 2][:], AF.Copy)
                            c.done(f"tpe{qi}_{sc}", ins, "ss")
            c.body("scalar", s_body3)

        def st_body(sy):
            c.need("res_done")
            c.dma_batch("sty", [(W["yT"][k * 128:(k + 1) * 128, :],
                                 X[:, k * T:(k + 1) * T]) for k in range(KB)])
            sy.wait_ge(c.sems["qs"].h, c.sems["qs"].n)
        c.body("sync", st_body)


def emit_s2(c, W, tag):
    nc = c.nc
    est = contextlib.ExitStack()
    with est:
        X = est.enter_context(nc.sbuf_tensor(f"{tag}X", [128, KB * T], F32))
        GTW = est.enter_context(nc.sbuf_tensor(f"{tag}gt", [128, KB * 8], BF16))
        EYE = est.enter_context(nc.sbuf_tensor(f"{tag}eye", [128, 128], BF16))
        ONESRB = est.enter_context(nc.sbuf_tensor(f"{tag}orb", [1, 128], BF16))
        LG = est.enter_context(nc.sbuf_tensor(f"{tag}lg", [128, 128], F32))
        EXPS = est.enter_context(nc.sbuf_tensor(f"{tag}ex", [128, 128], F32))
        MM_ = est.enter_context(nc.sbuf_tensor(f"{tag}mm", [128, 128], F32))
        MMB = est.enter_context(nc.sbuf_tensor(f"{tag}mmb", [128, 128], BF16))
        TMPB = est.enter_context(nc.sbuf_tensor(f"{tag}tb", [128, 128], BF16))
        ST = est.enter_context(nc.sbuf_tensor(f"{tag}st", [128, 64], F32))
        MROW = est.enter_context(nc.sbuf_tensor(f"{tag}mr", [1, T], BF16))
        B1ROW = est.enter_context(nc.sbuf_tensor(f"{tag}b1r", [1, 4096], BF16))
        B2ROW = est.enter_context(nc.sbuf_tensor(f"{tag}b2r", [1, 1024], BF16))

        chain_gate = getattr(c, "chain_gate", None)

        def qload0(g):
            if chain_gate:
                c.need(chain_gate)
            pairs = [(X[:, k * T:(k + 1) * T],
                      W["xT"][k * 128:(k + 1) * 128, :]) for k in range(KB)]
            pairs += [(GTW[:, k * 8:(k + 1) * 8],
                       W[f"{tag}gateT"][k * 128:(k + 1) * 128, :])
                      for k in range(KB)]
            pairs += [(EYE[:], W["eye"])]
            c.dma_batch(f"{tag}in0", pairs)
            for k in range(KB):
                c.alias(f"{tag}in{k}", f"{tag}in0")
            ins = g.memset(ONESRB[:], 1.0)
            c.done(f"{tag}onesrb", ins, "gg")
        c.body("gpsimd", qload0)

        XN = est.enter_context(
            nc.sbuf_tensor(f"{tag}xnb", [128, KB * T], BF16))
        emit_ln2(c, X, XN, tag)

        with (
            nc.psum_tensor(f"{tag}pr0", [128, 8], F32) as PR0,
            nc.psum_tensor(f"{tag}pr1", [128, 8], F32) as PR1,
            nc.psum_tensor(f"{tag}ptb", [128, 128], BF16) as PTB,
        ):
            PR = [PR0, PR1]

            def t_rt(t):
                c.need(f"{tag}xn", f"{tag}in0")
                for tc in range(16):
                    if tc >= 2:
                        c.need(f"{tag}lgev{tc - 2}")
                    ins = mm_accum(c, t, PR[tc % 2][:],
                                   [XN[:, k * T + tc * 128:
                                       k * T + tc * 128 + 128]
                                    for k in range(KB)],
                                   [GTW[:, k * 8:(k + 1) * 8]
                                    for k in range(KB)])
                    c.done(f"{tag}lgmm{tc}", ins, "tt")
                c.need(f"{tag}mmb")
                ins = t.transpose(PTB[:], MMB[:], EYE[:])
                c.done(f"{tag}ptb", ins, "tt")
            c.body("tensor", t_rt)

            def v_rt(v):
                for tc in range(16):
                    c.need(f"{tag}lgmm{tc}")
                    ins = v.tensor_copy(LG[:, tc * 8:(tc + 1) * 8],
                                        PR[tc % 2][:])
                    c.done(f"{tag}lgev{tc}", ins, "vv")
                lg3 = LG.ap().rearrange("p (a b) -> p a b", a=16, b=8)
                v.tensor_reduce(ST[:, 0:16], lg3, AX.X, ALU.max)
                lmax_b = ST[:, 0:16].unsqueeze(2).broadcast_to([128, 16, 8])
                ex3 = EXPS.ap().rearrange("p (a b) -> p a b", a=16, b=8)
                ins = v.tensor_tensor(ex3, lg3, lmax_b, ALU.subtract)
                c.done(f"{tag}sub", ins, "vv")
                c.need(f"{tag}exp")
                v.tensor_reduce(ST[:, 16:32], ex3, AX.X, ALU.add)
                v.reciprocal(ST[:, 32:48], ST[:, 16:32])
                for e in range(8):
                    lg_sl = lg3[:, :, e:e + 1].squeeze(2)
                    v.tensor_tensor(ST[:, 48:64], lg_sl, ST[:, 0:16],
                                    ALU.is_equal)
                    v.tensor_tensor(MM_[:, e * 16:(e + 1) * 16],
                                    ST[:, 48:64],
                                    ST[:, 32:48], ALU.mult)
                ins = v.tensor_copy(MMB[:], MM_[:])
                c.done(f"{tag}mmb", ins, "vv")
                c.need(f"{tag}ptb")
                ins = v.tensor_copy(TMPB[:], PTB[:])
                c.done(f"{tag}tmpb", ins, "vv")
            c.body("vector", v_rt)

            MED = nc.dram_tensor(f"{tag}med", [128, 128], BF16,
                                 kind="Internal").ap()

            def st_med(sy):
                c.need(f"{tag}tmpb")
                c.dma_batch(f"{tag}medst", [(MED, TMPB[:])])
            c.body("sync", st_med)

            def s_rt(s):
                c.need(f"{tag}sub")
                ins = s.activation(EXPS[:], EXPS[:], AF.Exp)
                c.done(f"{tag}exp", ins, "ss")
            c.body("scalar", s_rt)

        with (
            nc.sbuf_tensor(f"{tag}xg", [128, KB * 512], BF16) as XG,
            nc.sbuf_tensor(f"{tag}w1c", [128, 2 * 8 * 512], BF16) as W1C,
            nc.sbuf_tensor(f"{tag}w2c", [128, 2 * 4 * 1024], BF16) as W2C,
            nc.sbuf_tensor(f"{tag}h1", [128, 32 * 512], BF16) as H1,
            contextlib.ExitStack() as pse,
        ):
            PS = [pse.enter_context(
                nc.psum_tensor(f"{tag}ps{i}", [128, 512], F32))
                for i in range(8)]

            seq = [(th, e) for th in range(4) for e in range(8)]

            def g_stream(g):
                med_flat = MED.rearrange("a b -> (a b)")
                for i, (th, e) in enumerate(seq):
                    p = f"{tag}{th}_{e}_"
                    if i == 0:
                        c.need(f"{tag}medst")
                    else:
                        pth, pe = seq[i - 1]
                        c.need(f"{tag}{pth}_{pe}_yemm7")
                    src = med_flat[e * 2048:(e + 1) * 2048].unsqueeze(0)
                    c.dma_batch(p + "rows", [
                        (MROW[0:1, :], src),
                        (B1ROW[:], W[f"{tag}b1"][e:e + 1, :]),
                        (B2ROW[:], W[f"{tag}b2"][e:e + 1, :]),
                    ])
                    for dc in range(8):
                        bi = i * 8 + dc
                        if bi >= 2:
                            pj = bi - 2
                            pth, pe = seq[pj // 8]
                            pdc = pj % 8
                            c.need(f"{tag}{pth}_{pe}_h1m{pdc * 4 + 3}")
                        else:
                            c.need(f"{tag}xn")
                        c.dma_batch(
                            p + f"ldw1_{dc}",
                            [(W1C[:, (bi % 2) * 4096 + k * 512:
                                  (bi % 2) * 4096 + (k + 1) * 512],
                              W[f"{tag}w1"][e * 1024 + k * 128:
                                            e * 1024 + (k + 1) * 128,
                                            dc * 512:(dc + 1) * 512])
                             for k in range(KB)],
                            sem=f"w{bi % 2}")
                    for sl in range(8):
                        bi = i * 8 + sl
                        if bi >= 2:
                            pj = bi - 2
                            pth, pe = seq[pj // 8]
                            psl = pj % 8
                            c.need(f"{tag}{pth}_{pe}_w2u{psl}")
                        else:
                            c.need(f"{tag}xn")
                        c.dma_batch(
                            p + f"ldw2_{sl}",
                            [(W2C[:, (bi % 2) * 4096 + kk * 1024:
                                  (bi % 2) * 4096 + (kk + 1) * 1024],
                              W[f"{tag}w2"][e * 4096 + sl * 512 + kk * 128:
                                            e * 4096 + sl * 512 + (kk + 1) * 128,
                                            :])
                             for kk in range(4)],
                            sem=f"u{bi % 2}")
            c.body("gpsimd", g_stream)

            def t_exp(t):
                for i, (th, e) in enumerate(seq):
                    p = f"{tag}{th}_{e}_"
                    c.need(p + "rows")
                    if i == 0:
                        c.need(f"{tag}tmpb")
                    else:
                        pth, pe = seq[i - 1]
                        c.need(f"{tag}{pth}_{pe}_oadd0")
                    ins = t.matmul(PS[0][:], ONESRB[:],
                                   MROW[0:1, th * 512:(th + 1) * 512],
                                   start=True, stop=True)
                    c.done(p + "pmbc", ins, "tt")
                    for dc in range(8):
                        c.need(p + f"ldw1_{dc}")
                        if dc == 0:
                            c.need(p + "xg")
                        wb = (i * 8 + dc) % 2
                        for m in range(4):
                            gi = dc * 4 + m
                            if gi >= 2:
                                c.need(p + f"h1e{gi - 2}")
                            elif gi == 1 and i > 0:
                                pth, pe = seq[i - 1]
                                c.need(f"{tag}{pth}_{pe}_oadd1")
                            for k in range(KB):
                                t.matmul(PS[gi % 2][:],
                                         W1C[:, wb * 4096 + k * 512 + m * 128:
                                             wb * 4096 + k * 512 + m * 128 + 128],
                                         XG[:, k * 512:(k + 1) * 512],
                                         start=(k == 0), stop=False)
                            ins = t.matmul(
                                PS[gi % 2][:],
                                B1ROW[0:1, dc * 512 + m * 128:
                                      dc * 512 + m * 128 + 128],
                                MROW[0:1, th * 512:(th + 1) * 512],
                                start=False, stop=True)
                            c.done(p + f"h1m{gi}", ins, "tt")
                    c.need(p + "h1e30", p + "h1e31")
                    if i > 0:
                        pth, pe = seq[i - 1]
                        c.need(f"{tag}{pth}_{pe}_oadd7")
                    for sl in range(8):
                        c.need(p + f"ldw2_{sl}")
                        c.need(p + f"h1e{sl * 4 + 3}")
                        wb = (i * 8 + sl) % 2
                        for kk in range(4):
                            k = sl * 4 + kk
                            for m in range(8):
                                ins = t.matmul(
                                    PS[m][:],
                                    W2C[:, wb * 4096 + kk * 1024 + m * 128:
                                        wb * 4096 + kk * 1024 + m * 128 + 128],
                                    H1[:, k * 512:(k + 1) * 512],
                                    start=(k == 0), stop=False)
                        c.done(p + f"w2u{sl}", ins, "tt")
                    for m in range(8):
                        ins = t.matmul(PS[m][:],
                                       B2ROW[0:1, m * 128:(m + 1) * 128],
                                       MROW[0:1, th * 512:(th + 1) * 512],
                                       start=False, stop=True)
                        c.done(p + f"yemm{m}", ins, "tt")
            c.body("tensor", t_exp)

            def v_exp(v):
                for i, (th, e) in enumerate(seq):
                    p = f"{tag}{th}_{e}_"
                    c.need(p + "pmbc")
                    ins = None
                    for k in range(KB):
                        ins = v.tensor_tensor(
                            XG[:, k * 512:(k + 1) * 512],
                            XN[:, k * T + th * 512: k * T + th * 512 + 512],
                            PS[0][:], ALU.mult)
                    c.done(p + "xg", ins, "vv")
                    for m in range(8):
                        c.need(p + f"yemm{m}")
                        ins = v.tensor_tensor(
                            X[:, m * T + th * 512: m * T + th * 512 + 512],
                            X[:, m * T + th * 512: m * T + th * 512 + 512],
                            PS[m][:], ALU.add)
                        c.done(p + f"oadd{m}", ins, "vv")
            c.body("vector", v_exp)

            def s_exp(s):
                for i, (th, e) in enumerate(seq):
                    p = f"{tag}{th}_{e}_"
                    for gi in range(32):
                        c.need(p + f"h1m{gi}")
                        ins = s.activation(H1[:, gi * 512:(gi + 1) * 512],
                                           PS[gi % 2][:], AF.Relu)
                        c.done(p + f"h1e{gi}", ins, "ss")
            c.body("scalar", s_exp)

        def st_body(sy):
            lth, le = 3, 7
            for m in range(8):
                c.need(f"{tag}{lth}_{le}_oadd{m}")
            c.dma_batch(f"{tag}sty",
                        [(W["yT"][k * 128:(k + 1) * 128, :],
                          X[:, k * T:(k + 1) * T]) for k in range(KB)])
            sy.wait_ge(c.sems["qs"].h, c.sems["qs"].n)
        c.body("sync", st_body)


SEMS = ("qg", "qs", "qa", "tt", "vv", "ss", "gg", "w0", "w1", "u0", "u1", "y0", "y1", "y2", "y3")


def build_single(emit, decls):
    nc = bass.Bass("TRN2", target_bir_lowering=False, debug=False)
    W = {}
    for name, shape, dtype, kind in decls:
        W[name] = nc.dram_tensor(name, shape, dtype, kind=kind).ap()
    with contextlib.ExitStack() as est:
        c = Ctx(nc)
        for name in SEMS:
            c.add_sem(name, est.enter_context(nc.semaphore(name)))
        c.const_pool = est.enter_context(
            nc.sbuf_tensor("cpool", [128, Ctx.NCONST], F32))
        emit(c, W)
        blk = est.enter_context(nc.Block())
        c.run(blk)
    return nc


def _prep_s1_inputs(ins):
    import ml_dtypes
    wqkvT = np.ascontiguousarray(ins["L0_attn_a_W_qkv"].T)
    b = ins["L0_attn_a_b_qkv"]
    bq_col = np.ascontiguousarray(b[:2048].reshape(16, 128).T)
    bv_row = np.ascontiguousarray(b[2048:].reshape(1, D))
    eye = np.eye(128, dtype=np.float32).astype(ml_dtypes.bfloat16)
    return {"wqkvT": wqkvT, "bq_col": bq_col, "bv_row": bv_row, "eye": eye}


S1_DECLS = [
    ("xT", [D, T], F32, "ExternalInput"),
    ("wqkvT", [D, 3072], F32, "ExternalInput"),
    ("bq_col", [128, 16], F32, "ExternalInput"),
    ("bv_row", [1, D], F32, "ExternalInput"),
    ("eye", [128, 128], BF16, "ExternalInput"),
    ("yT", [D, T], F32, "ExternalOutput"),
]


def run_s1(x, ins):
    nc = build_single(emit_s1, S1_DECLS)
    shared = _prep_s1_inputs(ins)
    in_maps = []
    for b in range(B):
        m = dict(shared)
        m["xT"] = np.ascontiguousarray(x[b].T)
        in_maps.append(m)
    res = run_bass_kernel_spmd(nc, in_maps, core_ids=list(range(NCORES)))
    return np.stack([np.ascontiguousarray(r["yT"].T) for r in res.results])



def _prep_s2_inputs(ins, pfx="L0_ffn"):
    gateT = np.ascontiguousarray(ins[f"{pfx}_gate_w"].T)
    w1 = np.ascontiguousarray(ins[f"{pfx}_w1"].reshape(8 * 1024, 4096))
    w2 = np.ascontiguousarray(ins[f"{pfx}_w2"].reshape(8 * 4096, 1024))
    b1 = np.ascontiguousarray(ins[f"{pfx}_b1"])
    b2 = np.ascontiguousarray(ins[f"{pfx}_b2"])
    return {"s2_gateT": gateT, "s2_w1": w1, "s2_w2": w2,
            "s2_b1": b1, "s2_b2": b2}


S2_DECLS = [
    ("xT", [D, T], F32, "ExternalInput"),
    ("s2_gateT", [D, 8], F32, "ExternalInput"),
    ("s2_w1", [8 * 1024, 4096], F32, "ExternalInput"),
    ("s2_w2", [8 * 4096, 1024], F32, "ExternalInput"),
    ("s2_b1", [8, 4096], F32, "ExternalInput"),
    ("s2_b2", [8, 1024], F32, "ExternalInput"),
    ("eye", [128, 128], BF16, "ExternalInput"),
    ("yT", [D, T], F32, "ExternalOutput"),
]


def run_s2(x, ins):
    import ml_dtypes
    nc = build_single(lambda c, W: emit_s2(c, W, "s2_"), S2_DECLS)
    shared = _prep_s2_inputs(ins)
    shared["eye"] = np.eye(128, dtype=np.float32).astype(ml_dtypes.bfloat16)
    in_maps = []
    for b in range(B):
        m = dict(shared)
        m["xT"] = np.ascontiguousarray(x[b].T)
        in_maps.append(m)
    res = run_bass_kernel_spmd(nc, in_maps, core_ids=list(range(NCORES)))
    return np.stack([np.ascontiguousarray(r["yT"].T) for r in res.results])


def emit_s6(c, W, tag):
    nc = c.nc
    est = contextlib.ExitStack()
    with est:
        X = est.enter_context(nc.sbuf_tensor(f"{tag}X", [128, KB * T], F32))
        GTW = est.enter_context(nc.sbuf_tensor(f"{tag}gt", [128, KB * 8], BF16))
        EYE = est.enter_context(nc.sbuf_tensor(f"{tag}eye", [128, 128], BF16))
        ONESRB = est.enter_context(nc.sbuf_tensor(f"{tag}orb", [1, 128], BF16))
        LG = est.enter_context(nc.sbuf_tensor(f"{tag}lg", [128, 128], F32))
        PRB = est.enter_context(nc.sbuf_tensor(f"{tag}prb", [128, 128], F32))
        MM_ = est.enter_context(nc.sbuf_tensor(f"{tag}mm", [128, 128], F32))
        MMB = est.enter_context(nc.sbuf_tensor(f"{tag}mmb", [128, 128], BF16))
        TMPB = est.enter_context(nc.sbuf_tensor(f"{tag}tb", [128, 128], BF16))
        ST = est.enter_context(nc.sbuf_tensor(f"{tag}st", [128, 128], F32))
        MROW = est.enter_context(nc.sbuf_tensor(f"{tag}mr", [1, T], BF16))

        chain_gate = getattr(c, "chain_gate", None)

        def qload0(g):
            if chain_gate:
                c.need(chain_gate)
            pairs = [(X[:, k * T:(k + 1) * T],
                      W["xT"][k * 128:(k + 1) * 128, :]) for k in range(KB)]
            pairs += [(GTW[:, k * 8:(k + 1) * 8],
                       W[f"{tag}gateT"][k * 128:(k + 1) * 128, :])
                      for k in range(KB)]
            pairs += [(EYE[:], W["eye"])]
            c.dma_batch(f"{tag}in0", pairs)
            for k in range(KB):
                c.alias(f"{tag}in{k}", f"{tag}in0")
            ins = g.memset(ONESRB[:], 1.0)
            c.done(f"{tag}onesrb", ins, "gg")
        c.body("gpsimd", qload0)

        XN = est.enter_context(
            nc.sbuf_tensor(f"{tag}xnb", [128, KB * T], BF16))
        emit_ln2(c, X, XN, tag)

        with (
            nc.psum_tensor(f"{tag}pr0", [128, 8], F32) as PR0,
            nc.psum_tensor(f"{tag}pr1", [128, 8], F32) as PR1,
            nc.psum_tensor(f"{tag}ptb", [128, 128], BF16) as PTB,
        ):
            PR = [PR0, PR1]

            def t_rt(t):
                c.need(f"{tag}xn", f"{tag}in0")
                for tc in range(16):
                    if tc >= 2:
                        c.need(f"{tag}lgev{tc - 2}")
                    ins = mm_accum(c, t, PR[tc % 2][:],
                                   [XN[:, k * T + tc * 128:
                                       k * T + tc * 128 + 128]
                                    for k in range(KB)],
                                   [GTW[:, k * 8:(k + 1) * 8]
                                    for k in range(KB)])
                    c.done(f"{tag}lgmm{tc}", ins, "tt")
                c.need(f"{tag}mmb")
                ins = t.transpose(PTB[:], MMB[:], EYE[:])
                c.done(f"{tag}ptb", ins, "tt")
            c.body("tensor", t_rt)

            def v_rt(v):
                for tc in range(16):
                    c.need(f"{tag}lgmm{tc}")
                    ins = v.tensor_copy(LG[:, tc * 8:(tc + 1) * 8],
                                        PR[tc % 2][:])
                    c.done(f"{tag}lgev{tc}", ins, "vv")
                c.need(f"{tag}sig")
                p3 = PRB.ap().rearrange("p (a b) -> p a b", a=16, b=8)
                v.tensor_reduce(ST[:, 0:16], p3, AX.X, ALU.max)
                v1b = ST[:, 0:16].unsqueeze(2).broadcast_to([128, 16, 8])
                mm3 = MM_.ap().rearrange("p (a b) -> p a b", a=16, b=8)
                v.tensor_tensor(mm3, p3, v1b, ALU.is_equal)
                v.scalar_tensor_tensor(mm3, mm3, -1e30, p3, ALU.mult, ALU.add)
                v.tensor_reduce(ST[:, 16:32], mm3, AX.X, ALU.max)
                v.tensor_tensor(ST[:, 32:48], ST[:, 0:16], ST[:, 16:32],
                                ALU.add)
                v.tensor_scalar_add(ST[:, 32:48], ST[:, 32:48], 1e-20)
                v.reciprocal(ST[:, 32:48], ST[:, 32:48])
                v.tensor_tensor(ST[:, 48:64], ST[:, 0:16], ST[:, 32:48],
                                ALU.mult)
                ins = v.tensor_tensor(ST[:, 64:80], ST[:, 16:32], ST[:, 32:48],
                                      ALU.mult)
                c.done(f"{tag}wraw", ins, "vv")
                c.need(f"{tag}sq")
                for e in range(8):
                    pe_sl = p3[:, :, e:e + 1].squeeze(2)
                    v.tensor_tensor(ST[:, 80:96], pe_sl, ST[:, 0:16],
                                    ALU.is_equal)
                    v.tensor_tensor(ST[:, 80:96], ST[:, 80:96], ST[:, 48:64],
                                    ALU.mult)
                    v.tensor_tensor(ST[:, 96:112], pe_sl, ST[:, 16:32],
                                    ALU.is_equal)
                    v.scalar_tensor_tensor(MM_[:, e * 16:(e + 1) * 16],
                                           ST[:, 96:112], 1.0, ST[:, 64:80],
                                           ALU.mult, ALU.mult)
                    v.tensor_tensor(MM_[:, e * 16:(e + 1) * 16],
                                    MM_[:, e * 16:(e + 1) * 16],
                                    ST[:, 80:96], ALU.add)
                ins = v.tensor_copy(MMB[:], MM_[:])
                c.done(f"{tag}mmb", ins, "vv")
                c.need(f"{tag}ptb")
                ins = v.tensor_copy(TMPB[:], PTB[:])
                c.done(f"{tag}tmpb", ins, "vv")
            c.body("vector", v_rt)

            def s_rt(s):
                c.need(f"{tag}lgev15")
                ins = s.activation(PRB[:], LG[:], AF.Sigmoid)
                c.done(f"{tag}sig", ins, "ss")
                c.need(f"{tag}wraw")
                ins = s.activation(ST[:, 48:64], ST[:, 48:64], AF.Sqrt)
                ins = s.activation(ST[:, 64:80], ST[:, 64:80], AF.Sqrt)
                c.done(f"{tag}sq", ins, "ss")
            c.body("scalar", s_rt)

            MED = nc.dram_tensor(f"{tag}med", [128, 128], BF16,
                                 kind="Internal").ap()

            def st_med(sy):
                c.need(f"{tag}tmpb")
                c.dma_batch(f"{tag}medst", [(MED, TMPB[:])])
            c.body("sync", st_med)

        with (
            nc.sbuf_tensor(f"{tag}xg", [128, KB * 512], BF16) as XG,
            nc.sbuf_tensor(f"{tag}w1c", [128, 2 * 8 * 512], BF16) as W1C,
            nc.sbuf_tensor(f"{tag}w2c", [128, 2 * 4 * 1024], BF16) as W2C,
            nc.sbuf_tensor(f"{tag}h1", [128, 8 * 512], BF16) as H1,
            nc.sbuf_tensor(f"{tag}hs0", [128, 512], F32) as HS0,
            nc.sbuf_tensor(f"{tag}hs1", [128, 512], F32) as HS1,
            contextlib.ExitStack() as pse,
        ):
            PS = [pse.enter_context(
                nc.psum_tensor(f"{tag}ps{i}", [128, 512], F32))
                for i in range(8)]
            HS = [HS0, HS1]

            seq = [(th, e) for th in range(4) for e in range(8)]

            def g_stream(g):
                med_flat = MED.rearrange("a b -> (a b)")
                for i, (th, e) in enumerate(seq):
                    p = f"{tag}{th}_{e}_"
                    if i == 0:
                        c.need(f"{tag}medst")
                    else:
                        pth, pe = seq[i - 1]
                        c.need(f"{tag}{pth}_{pe}_yemm7")
                    src = med_flat[e * 2048:(e + 1) * 2048].unsqueeze(0)
                    c.dma_batch(p + "rows", [(MROW[0:1, :], src)])
                    for dc in range(2):
                        bi = i * 2 + dc
                        if bi >= 2:
                            pj = bi - 2
                            pth, pe = seq[pj // 2]
                            pdc = pj % 2
                            c.need(f"{tag}{pth}_{pe}_h1m{pdc * 4 + 3}")
                        else:
                            c.need(f"{tag}xn")
                        c.dma_batch(
                            p + f"ldw1_{dc}",
                            [(W1C[:, (bi % 2) * 4096 + k * 512:
                                  (bi % 2) * 4096 + (k + 1) * 512],
                              W[f"{tag}w1"][k * 128:(k + 1) * 128,
                                            e * 1024 + dc * 512:
                                            e * 1024 + (dc + 1) * 512])
                             for k in range(KB)],
                            sem=f"w{bi % 2}")
                    for sl in range(2):
                        bi = i * 2 + sl
                        if bi >= 2:
                            pj = bi - 2
                            pth, pe = seq[pj // 2]
                            psl = pj % 2
                            c.need(f"{tag}{pth}_{pe}_w2u{psl}")
                        else:
                            c.need(f"{tag}xn")
                        c.dma_batch(
                            p + f"ldw2_{sl}",
                            [(W2C[:, (bi % 2) * 4096 + kk * 1024:
                                  (bi % 2) * 4096 + (kk + 1) * 1024],
                              W[f"{tag}w2"][e * 1024 + sl * 512 + kk * 128:
                                            e * 1024 + sl * 512 + (kk + 1) * 128,
                                            :])
                             for kk in range(4)],
                            sem=f"u{bi % 2}")
            c.body("gpsimd", g_stream)

            def t_exp(t):
                for i, (th, e) in enumerate(seq):
                    p = f"{tag}{th}_{e}_"
                    c.need(p + "rows")
                    if i == 0:
                        c.need(f"{tag}tmpb")
                    else:
                        pth, pe = seq[i - 1]
                        c.need(f"{tag}{pth}_{pe}_oadd0")
                    ins = t.matmul(PS[0][:], ONESRB[:],
                                   MROW[0:1, th * 512:(th + 1) * 512],
                                   start=True, stop=True)
                    c.done(p + "pmbc", ins, "tt")
                    for dc in range(2):
                        c.need(p + f"ldw1_{dc}")
                        if dc == 0:
                            c.need(p + "xg")
                        wb = (i * 2 + dc) % 2
                        for m in range(4):
                            gi = dc * 4 + m
                            if gi >= 2:
                                c.need(p + f"h1e{gi - 2}")
                            elif gi == 1 and i > 0:
                                pth, pe = seq[i - 1]
                                c.need(f"{tag}{pth}_{pe}_oadd1")
                            ins = mm_accum(
                                c, t, PS[gi % 2][:],
                                [W1C[:, wb * 4096 + k * 512 + m * 128:
                                     wb * 4096 + k * 512 + m * 128 + 128]
                                 for k in range(KB)],
                                [XG[:, k * 512:(k + 1) * 512]
                                 for k in range(KB)])
                            c.done(p + f"h1m{gi}", ins, "tt")
                    c.need(p + "h1e6", p + "h1e7")
                    if i > 0:
                        pth, pe = seq[i - 1]
                        c.need(f"{tag}{pth}_{pe}_oadd7")
                    for sl in range(2):
                        c.need(p + f"ldw2_{sl}")
                        c.need(p + f"h1e{sl * 4 + 3}")
                        wb = (i * 2 + sl) % 2
                        for kk in range(4):
                            k = sl * 4 + kk
                            for m in range(8):
                                ins = t.matmul(
                                    PS[m][:],
                                    W2C[:, wb * 4096 + kk * 1024 + m * 128:
                                        wb * 4096 + kk * 1024 + m * 128 + 128],
                                    H1[:, k * 512:(k + 1) * 512],
                                    start=(k == 0), stop=(k == 7))
                        c.done(p + f"w2u{sl}", ins, "tt")
                        if sl == 1:
                            for m in range(8):
                                c.alias(p + f"yemm{m}", p + "w2u1")
            c.body("tensor", t_exp)

            def v_exp(v):
                for i, (th, e) in enumerate(seq):
                    p = f"{tag}{th}_{e}_"
                    c.need(p + "pmbc")
                    ins = None
                    for k in range(KB):
                        ins = v.tensor_tensor(
                            XG[:, k * 512:(k + 1) * 512],
                            XN[:, k * T + th * 512: k * T + th * 512 + 512],
                            PS[0][:], ALU.mult)
                    c.done(p + "xg", ins, "vv")
                    for m in range(8):
                        c.need(p + f"yemm{m}")
                        ins = v.tensor_tensor(
                            X[:, m * T + th * 512: m * T + th * 512 + 512],
                            X[:, m * T + th * 512: m * T + th * 512 + 512],
                            PS[m][:], ALU.add)
                        c.done(p + f"oadd{m}", ins, "vv")
            c.body("vector", v_exp)

            def s_exp(s):
                for i, (th, e) in enumerate(seq):
                    p = f"{tag}{th}_{e}_"
                    for gi in range(8):
                        c.need(p + f"h1m{gi}")
                        s.activation(HS[gi % 2][:], PS[gi % 2][:], AF.Relu)
                        ins = s.activation(H1[:, gi * 512:(gi + 1) * 512],
                                           HS[gi % 2][:], AF.Square)
                        c.done(p + f"h1e{gi}", ins, "ss")
            c.body("scalar", s_exp)

        def st_body(sy):
            lth, le = 3, 7
            for m in range(8):
                c.need(f"{tag}{lth}_{le}_oadd{m}")
            c.dma_batch(f"{tag}sty",
                        [(W["yT"][k * 128:(k + 1) * 128, :],
                          X[:, k * T:(k + 1) * T]) for k in range(KB)])
            sy.wait_ge(c.sems["qs"].h, c.sems["qs"].n)
        c.body("sync", st_body)


def _prep_s6_inputs(ins, tag):
    gateT = np.ascontiguousarray(ins["L1_ffn_router_w"].T)
    w1 = np.ascontiguousarray(ins["L1_ffn_w1"])
    w2 = np.ascontiguousarray(ins["L1_ffn_w2"])
    return {f"{tag}gateT": gateT, f"{tag}w1": w1, f"{tag}w2": w2}


def s6_decls(tag):
    return [
        ("xT", [D, T], F32, "ExternalInput"),
        (f"{tag}gateT", [D, 8], F32, "ExternalInput"),
        (f"{tag}w1", [D, 8192], F32, "ExternalInput"),
        (f"{tag}w2", [8192, D], F32, "ExternalInput"),
        ("eye", [128, 128], BF16, "ExternalInput"),
        ("yT", [D, T], F32, "ExternalOutput"),
    ]


def run_s6(x, ins):
    import ml_dtypes
    nc = build_single(lambda c, W: emit_s6(c, W, "s6_"), s6_decls("s6_"))
    shared = _prep_s6_inputs(ins, "s6_")
    shared["eye"] = np.eye(128, dtype=np.float32).astype(ml_dtypes.bfloat16)
    in_maps = []
    for b in range(B):
        m = dict(shared)
        m["xT"] = np.ascontiguousarray(x[b].T)
        in_maps.append(m)
    res = run_bass_kernel_spmd(nc, in_maps, core_ids=list(range(NCORES)))
    return np.stack([np.ascontiguousarray(r["yT"].T) for r in res.results])


def emit_s7(c, W, tag):
    nc = c.nc
    c.const(1e-6)
    PI = float(np.pi)
    for v in (-PI, PI, PI / 2, -1.0, 0.125, -0.5, 1.0, 1.0 / 64.0):
        c.const(v)
    PROJS = ("qa", "ka", "qp", "kp")

    def qi_label(g):
        hg, r = divmod(g, 32)
        h2, qi = divmod(r, 16)
        return f"{tag}h{hg}_anorm{h2}_{qi}"

    def og_label(g):
        hg, r = divmod(g, 8)
        h2, qg = divmod(r, 4)
        return f"{tag}h{hg}_oev{h2}_{qg}"

    TPL = []
    for hg in range(8):
        for h2 in range(2):
            for qg in range(4):
                for qi4 in range(4):
                    qi = qg * 4 + qi4
                    for sc in range((qg + 1) * 4):
                        TPL.append(f"{tag}h{hg}_tpe{h2}_{qi}_{sc}")
    TPOF = {}
    for hg in range(8):
        TPOF[hg] = 320 * hg

    est = contextlib.ExitStack()
    with est:
        X = est.enter_context(nc.sbuf_tensor(f"{tag}X", [128, KB * T], F32))
        EYE = est.enter_context(nc.sbuf_tensor(f"{tag}eye", [128, 128], BF16))
        IND = est.enter_context(nc.sbuf_tensor(f"{tag}inds", [128, 2], BF16))
        INDT = est.enter_context(nc.sbuf_tensor(f"{tag}indts", [2, 128], BF16))
        MSK = est.enter_context(nc.sbuf_tensor(f"{tag}msks", [128, 4 * 512], BF16))
        SLS = est.enter_context(nc.sbuf_tensor(f"{tag}slss", [1, 16], F32))
        ESC = est.enter_context(nc.sbuf_tensor(f"{tag}esc", [1, 16], F32))
        ESCC = est.enter_context(nc.sbuf_tensor(f"{tag}escc", [128, 16], F32))
        ONESR = est.enter_context(nc.sbuf_tensor(f"{tag}onr", [1, 128], F32))
        SROW = est.enter_context(nc.sbuf_tensor(f"{tag}srow", [128, 8], F32))
        RST = est.enter_context(nc.sbuf_tensor(f"{tag}rst", [2, 2 * T], BF16))
        OD = nc.dram_tensor(f"{tag}od", [D, T], BF16, kind="Internal").ap()

        chain_gate = getattr(c, "chain_gate", None)

        def qload0(g):
            if chain_gate:
                c.need(chain_gate)
            pairs = [(X[:, k * T:(k + 1) * T],
                      W["xT"][k * 128:(k + 1) * 128, :]) for k in range(KB)]
            pairs += [(EYE[:], W["eye"]), (IND[:], W[f"{tag}ind"]),
                      (INDT[:], W[f"{tag}indt"]), (MSK[:], W[f"{tag}msk"]),
                      (SLS[:], W[f"{tag}sls"])]
            c.dma_batch(f"{tag}in0", pairs)
            for k in range(KB):
                c.alias(f"{tag}in{k}", f"{tag}in0")
            ins = g.memset(ONESR[:], 1.0)
            c.done(f"{tag}ones2", ins, "gg")
        c.body("gpsimd", qload0)

        stack2 = contextlib.ExitStack()
        XN = stack2.enter_context(nc.sbuf_tensor(f"{tag}xnb", [128, KB * T], BF16))
        emit_ln2(c, X, XN, tag)
        VT = stack2.enter_context(
            nc.sbuf_tensor(f"{tag}vt", [128, 16 * 1024], BF16, side="right"))

        with nc.psum_tensor(f"{tag}pesc", [128, 16], F32) as PESC:
            def s_esc(s):
                c.need(f"{tag}in0")
                ins = s.activation(ESC[:], SLS[:], AF.Exp, scale=1.0)
                c.done(f"{tag}esc1", ins, "ss")
            c.body("scalar", s_esc)

            def v_esc(v):
                c.need(f"{tag}esc1")
                ins = v.tensor_scalar_mul(ESC[:], ESC[:], 0.125)
                c.done(f"{tag}esc2", ins, "vv")
                c.need(f"{tag}escmm")
                ins = v.tensor_copy(ESCC[:], PESC[:])
                c.done(f"{tag}escc", ins, "vv")
            c.body("vector", v_esc)

            def t_esc(t):
                c.need(f"{tag}esc2", f"{tag}ones2")
                ins = t.matmul(PESC[:], ONESR[:], ESC[:], start=True, stop=True)
                c.done(f"{tag}escmm", ins, "tt")
            c.body("tensor", t_esc)

        with (
            nc.sbuf_tensor(f"{tag}wv0", [128, KB * 256], BF16) as WV0,
            nc.sbuf_tensor(f"{tag}wv1", [128, KB * 256], BF16) as WV1,
            nc.psum_tensor(f"{tag}pj0", [128, 256], F32) as VPJ0,
            nc.psum_tensor(f"{tag}pj1", [128, 256], F32) as VPJ1,
        ):
            WVs = [WV0, WV1]
            VPJ = [VPJ0, VPJ1]

            def g_v(g):
                c.need(f"{tag}xn")
                for grp in range(4):
                    if grp >= 2:
                        c.need(f"{tag}vgrp{grp - 2}")
                    c.dma_batch(
                        f"{tag}ldv{grp}",
                        [(WVs[grp % 2][:, k * 256:(k + 1) * 256],
                          W[f"{tag}wvT"][k * 128:(k + 1) * 128,
                                         grp * 256:(grp + 1) * 256])
                         for k in range(KB)],
                        sem=f"w{grp % 2}")
            c.body("gpsimd", g_v)

            def t_v(t):
                c.need(f"{tag}xn", f"{tag}escc")
                gi = 0
                for grp in range(4):
                    c.need(f"{tag}ldv{grp}")
                    for tc in range(16):
                        if gi >= 2:
                            c.need(f"{tag}vev{gi - 2}")
                        ins = mm_accum(c, t, VPJ[gi % 2][:],
                                       [XN[:, k * T + tc * 128:
                                           k * T + tc * 128 + 128]
                                        for k in range(KB)],
                                       [WVs[grp % 2][:, k * 256:(k + 1) * 256]
                                        for k in range(KB)])
                        c.done(f"{tag}vmm{gi}", ins, "tt")
                        gi += 1
                    c.alias(f"{tag}vgrp{grp}", f"{tag}vmm{gi - 1}")
            c.body("tensor", t_v)

            def v_v(v):
                gi = 0
                for grp in range(4):
                    for tc in range(16):
                        c.need(f"{tag}vmm{gi}")
                        ins = v.tensor_copy(
                            VT[:, tc * 1024 + grp * 256:
                               tc * 1024 + grp * 256 + 256],
                            VPJ[gi % 2][:])
                        c.done(f"{tag}vev{gi}", ins, "vv")
                        gi += 1
                c.alias(f"{tag}vall", f"{tag}vev{gi - 1}")
            c.body("vector", v_v)

        QA = stack2.enter_context(nc.sbuf_tensor(f"{tag}qa", [128, T], BF16))
        KA = stack2.enter_context(nc.sbuf_tensor(f"{tag}ka", [128, T], BF16))
        CQ = stack2.enter_context(nc.sbuf_tensor(f"{tag}cq", [128, T], BF16))
        SQ = stack2.enter_context(nc.sbuf_tensor(f"{tag}sq", [128, T], BF16))
        CK = stack2.enter_context(nc.sbuf_tensor(f"{tag}ck", [128, T], BF16))
        SK = stack2.enter_context(nc.sbuf_tensor(f"{tag}sk", [128, T], BF16))
        AA = stack2.enter_context(nc.sbuf_tensor(f"{tag}aa", [128, T], BF16))
        ATB = stack2.enter_context(
            nc.sbuf_tensor(f"{tag}atb", [128, 16 * 512], BF16, side="right"))
        OTH = stack2.enter_context(nc.sbuf_tensor(f"{tag}oth", [128, T], BF16))
        WH = stack2.enter_context(
            nc.sbuf_tensor(f"{tag}wh", [128, 2 * KB * 128], BF16))
        SCR = [stack2.enter_context(
            nc.sbuf_tensor(f"{tag}scr{jj}", [128, 2 * 512], F32))
            for jj in range(2)]
        SQB = stack2.enter_context(
            nc.sbuf_tensor(f"{tag}sqb", [128, 4 * 512], BF16))

        def g_hg(g):
            for hg in range(8):
                p = f"{tag}h{hg}_"
                for pi, nm in enumerate(PROJS):
                    bi = hg * 4 + pi
                    if bi >= 2:
                        pbi = bi - 2
                        phg, ppi = divmod(pbi, 4)
                        c.need(f"{tag}h{phg}_{PROJS[ppi]}mm3")
                    else:
                        c.need(f"{tag}vall")
                    wn = {"qa": "wqaT", "ka": "wkaT",
                          "qp": "wqpT", "kp": "wkpT"}[nm]
                    c.dma_batch(
                        p + f"ld{nm}",
                        [(WH[:, (bi % 2) * 1024 + k * 128:
                             (bi % 2) * 1024 + (k + 1) * 128],
                          W[f"{tag}{wn}"][k * 128:(k + 1) * 128,
                                          hg * 128:(hg + 1) * 128])
                         for k in range(KB)],
                        sem=f"w{bi % 2}")
                c.need(p + "oall")
                c.dma_batch(p + "ostore", [(OD[hg * 128:(hg + 1) * 128, :],
                                            OTH[:])])
        c.body("gpsimd", g_hg)

        PS_S_guard = nc.psum_tensor(f"{tag}ps_s", [128, T], F32)
        PS_S = stack2.enter_context(PS_S_guard)

        for hg in range(8):
            p = f"{tag}h{hg}_"
            with (
                nc.psum_tensor(p + "pja", [128, 512], F32) as PJA,
                nc.psum_tensor(p + "pjb", [128, 512], F32) as PJB,
                nc.psum_tensor(p + "prs", [2, 512], F32) as PRS,
            ):
                PJs = [PJA, PJB]

                def t_proj(t, p=p, hg=hg, PJs=PJs, PRS=PRS):
                    for pi, nm in enumerate(PROJS):
                        c.need(p + f"ld{nm}")
                        bi = hg * 4 + pi
                        for tb in range(TB):
                            gi = pi * 4 + tb
                            if gi >= 2:
                                pnm = PROJS[(gi - 2) // 4]
                                c.need(p + f"{pnm}done{(gi - 2) % 4}")
                            elif hg > 0:
                                c.need(f"{tag}h{hg - 1}_oall")
                            else:
                                c.need(f"{tag}vall")
                            ins = mm_accum(
                                c, t, PJs[gi % 2][:],
                                [WH[:, (bi % 2) * 1024 + k * 128:
                                    (bi % 2) * 1024 + (k + 1) * 128]
                                 for k in range(KB)],
                                [XN[:, k * T + tb * 512:
                                    k * T + tb * 512 + 512]
                                 for k in range(KB)])
                            c.done(p + f"{nm}mm{tb}", ins, "tt")
                    for si, side in enumerate(("q", "k")):
                        for tb in range(TB):
                            c.need(p + f"sq{side}{tb}")
                            idx = si * 4 + tb
                            if idx >= 1:
                                ps_, pt_ = divmod(idx - 1, 4)
                                c.need(p + "rsev" +
                                       ("q" if ps_ == 0 else "k") + str(pt_))
                            ins = t.matmul(PRS[:], IND[:, 0:2],
                                           SQB[:, tb * 512:(tb + 1) * 512],
                                           start=True, stop=True)
                            c.done(p + f"rsmm{side}{tb}", ins, "tt")
                    bi2 = 0
                    for si, side in enumerate(("q", "k")):
                        co = 0 if side == "q" else T
                        c.need(p + f"rstrow{side}")
                        if bi2 == 0:
                            c.need(p + "kpdone3", p + "qpdone3")
                        for tb in range(TB):
                            if bi2 >= 2:
                                pb = bi2 - 2
                                c.need(p + "bcev" +
                                       ("q" if pb < 4 else "k") + str(pb % 4))
                            ins = t.matmul(
                                PJs[bi2 % 2][:], INDT[0:2, :],
                                RST[0:2, co + tb * 512:co + (tb + 1) * 512],
                                start=True, stop=True)
                            c.done(p + f"bcmm{side}{tb}", ins, "tt")
                            bi2 += 1
                c.body("tensor", t_proj)

                def v_proj(v, p=p, hg=hg, PJs=PJs, PRS=PRS):
                    for pi, nm in enumerate(PROJS):
                        kind = "amp" if pi < 2 else "phase"
                        for tb in range(TB):
                            gi = pi * 4 + tb
                            jj = gi % 2
                            sc = SCR[jj]
                            c.need(p + f"{nm}mm{tb}")
                            if gi >= 2:
                                pnm = PROJS[(gi - 2) // 4]
                                c.need(p + f"{pnm}sdone{(gi - 2) % 4}")
                            v.tensor_scalar_mul(
                                sc[:, 0:512], PJs[jj][:], -1.0)
                            ins = v.tensor_tensor(
                                sc[:, 0:512], sc[:, 0:512], PJs[jj][:],
                                ALU.max)
                            c.done(p + f"{nm}abs{tb}", ins, "vv")
                            if kind == "amp":
                                c.need(p + f"{nm}ln{tb}")
                                dst_t = QA if nm == "qa" else KA
                                ins = v.scalar_tensor_tensor(
                                    dst_t[:, tb * 512:(tb + 1) * 512],
                                    PJs[jj][:], 0.0, sc[:, 512:1024],
                                    ALU.max, ALU.add)
                                c.done(p + f"{nm}done{tb}", ins, "vv")
                            else:
                                c.need(p + f"{nm}sin{tb}", p + f"{nm}sgn{tb}")
                                st_ = SQ if nm == "qp" else SK
                                ins = v.tensor_tensor(
                                    st_[:, tb * 512:(tb + 1) * 512],
                                    sc[:, 0:512], sc[:, 512:1024], ALU.mult)
                                c.done(p + f"{nm}done{tb}", ins, "vv")
                    for si, side in enumerate(("q", "k")):
                        co = 0 if side == "q" else T
                        for tb in range(TB):
                            c.need(p + f"rsmm{side}{tb}")
                            ins = v.tensor_copy(
                                RST[0:2, co + tb * 512:co + (tb + 1) * 512],
                                PRS[:])
                            c.done(p + f"rsev{side}{tb}", ins, "vv")
                        ins = v.tensor_scalar_mul(
                            RST[0:2, co:co + T], RST[0:2, co:co + T],
                            1.0 / 64.0)
                        c.done(p + f"rsmean{side}", ins, "vv")
                    bi2 = 0
                    lastev = None
                    for si, side in enumerate(("q", "k")):
                        amp = QA if side == "q" else KA
                        ct = CQ if side == "q" else CK
                        st_ = SQ if side == "q" else SK
                        for tb in range(TB):
                            c.need(p + f"bcmm{side}{tb}")
                            sl = slice(tb * 512, (tb + 1) * 512)
                            v.tensor_tensor(SCR[bi2 % 2][:, 0:512], amp[:, sl],
                                            PJs[bi2 % 2][:], ALU.mult)
                            v.tensor_tensor(ct[:, sl], ct[:, sl],
                                            SCR[bi2 % 2][:, 0:512], ALU.mult)
                            ins = v.tensor_tensor(st_[:, sl], st_[:, sl],
                                                  SCR[bi2 % 2][:, 0:512],
                                                  ALU.mult)
                            c.done(p + f"bcev{side}{tb}", ins, "vv")
                            lastev = p + f"bcev{side}{tb}"
                            bi2 += 1
                    c.alias(p + "csdone", lastev)
                c.body("vector", v_proj)

                def s_proj(s, p=p, hg=hg, PJs=PJs):
                    for pi, nm in enumerate(PROJS):
                        kind = "amp" if pi < 2 else "phase"
                        for tb in range(TB):
                            gi = pi * 4 + tb
                            jj = gi % 2
                            sc = SCR[jj]
                            c.need(p + f"{nm}abs{tb}")
                            if kind == "amp":
                                s.activation(sc[:, 512:1024], sc[:, 0:512],
                                             AF.Exp, scale=-1.0)
                                ins = s.activation(sc[:, 512:1024],
                                                   sc[:, 512:1024],
                                                   AF.Ln, bias=1.0)
                                c.done(p + f"{nm}ln{tb}", ins, "ss")
                                c.alias(p + f"{nm}sdone{tb}",
                                        p + f"{nm}ln{tb}")
                            else:
                                ct = CQ if nm == "qp" else CK
                                s.activation(sc[:, 512:1024], sc[:, 0:512],
                                             AF.Tanh)
                                ins = s.activation(
                                    ct[:, tb * 512:(tb + 1) * 512],
                                    sc[:, 512:1024], AF.Sin,
                                    scale=-PI, bias=PI / 2)
                                c.done(p + f"{nm}cos{tb}", ins, "ss")
                                ins = s.activation(sc[:, 512:1024],
                                                   sc[:, 512:1024],
                                                   AF.Sin, scale=PI)
                                c.done(p + f"{nm}sin{tb}", ins, "ss")
                                ins = s.activation(sc[:, 0:512], PJs[jj][:],
                                                   AF.Sign)
                                c.done(p + f"{nm}sgn{tb}", ins, "ss")
                                c.alias(p + f"{nm}sdone{tb}",
                                        p + f"{nm}sgn{tb}")
                    for si, side in enumerate(("q", "k")):
                        amp_t = QA if side == "q" else KA
                        for tb in range(TB):
                            c.need(p + ("qa" if side == "q" else "ka") +
                                   f"done{tb}")
                            if side == "k":
                                c.need(p + f"rsmmq{tb}")
                            elif hg > 0:
                                c.need(f"{tag}h{hg - 1}_rsmmk{tb}")
                            ins = s.activation(
                                SQB[:, tb * 512:(tb + 1) * 512],
                                amp_t[:, tb * 512:(tb + 1) * 512], AF.Square)
                            c.done(p + f"sq{side}{tb}", ins, "ss")
                    for side in ("q", "k"):
                        co = 0 if side == "q" else T
                        c.need(p + f"rsmean{side}")
                        s.activation(RST[0:2, co:co + T], RST[0:2, co:co + T],
                                     AF.Ln, bias=1e-6)
                        ins = s.activation(RST[0:2, co:co + T],
                                           RST[0:2, co:co + T],
                                           AF.Exp, scale=-0.5)
                        c.done(p + f"rstrow{side}", ins, "ss")
                c.body("scalar", s_proj)

            with (
                nc.psum_tensor(p + "pt0", [128, 128], BF16) as PT0,
                nc.psum_tensor(p + "pt1", [128, 128], BF16) as PT1,
                nc.psum_tensor(p + "po", [64, 512], F32) as PO,
            ):
                PTs = [PT0, PT1]

                def t_att(t, p=p, hg=hg, PTs=PTs, PO=PO):
                    c.need(p + "csdone")
                    ti = TPOF[hg]
                    for h2 in range(2):
                        hoff = h2 * 64
                        for qg in range(4):
                            nsc = (qg + 1) * 4
                            for qi4 in range(4):
                                qi = qg * 4 + qi4
                                q0 = qi * 128
                                g = hg * 32 + h2 * 16 + qi
                                if g > 0:
                                    c.need(qi_label(g - 1))
                                ins = None
                                for sb in range(qg + 1):
                                    ssl = slice(sb * 512, sb * 512 + 512)
                                    t.matmul(PS_S[:, ssl],
                                             CQ[hoff:hoff + 64, q0:q0 + 128],
                                             CK[hoff:hoff + 64, ssl],
                                             start=True, stop=False)
                                    ins = t.matmul(
                                        PS_S[:, ssl],
                                        SQ[hoff:hoff + 64, q0:q0 + 128],
                                        SK[hoff:hoff + 64, ssl],
                                        start=False, stop=True)
                                c.done(p + f"smm{h2}_{qi}", ins, "tt")
                                c.need(p + f"anorm{h2}_{qi}")
                                for sc in range(nsc):
                                    if ti >= 2:
                                        c.need(TPL[ti - 2])
                                    ins = t.transpose(
                                        PTs[sc % 2][:],
                                        AA[:, sc * 128:(sc + 1) * 128],
                                        EYE[:])
                                    c.done(p + f"tp{h2}_{qi}_{sc}", ins, "tt")
                                    ti += 1
                            c.need(p + f"tpq{h2}_{qg}")
                            og = hg * 8 + h2 * 4 + qg
                            if og > 0:
                                c.need(og_label(og - 1))
                            ins = mm_accum(
                                c, t, PO[:],
                                [VT[:, sc2 * 1024 + hg * 128 + hoff:
                                    sc2 * 1024 + hg * 128 + hoff + 64]
                                 for sc2 in range(nsc)],
                                [ATB[:, sc2 * 512:sc2 * 512 + 512]
                                 for sc2 in range(nsc)])
                            c.done(p + f"omm{h2}_{qg}", ins, "tt")
                c.body("tensor", t_att)

                def v_att(v, p=p, hg=hg, PO=PO):
                    for h2 in range(2):
                        h = hg * 2 + h2
                        hoff = h2 * 64
                        for qg in range(4):
                            ns = (qg + 1) * 512
                            for qi4 in range(4):
                                qi = qg * 4 + qi4
                                c.need(p + f"smm{h2}_{qi}")
                                v.tensor_tensor(
                                    PS_S[:, qg * 512:qg * 512 + 512],
                                    PS_S[:, qg * 512:qg * 512 + 512],
                                    MSK[:, qi4 * 512:qi4 * 512 + 512],
                                    ALU.add)
                                v.tensor_reduce(SROW[:, 0:1], PS_S[:, 0:ns],
                                                AX.X, ALU.max)
                                ins = v.scalar_tensor_tensor(
                                    SROW[:, 1:2], SROW[:, 0:1], -1.0,
                                    ESCC[:, h:h + 1], ALU.mult, ALU.mult)
                                c.done(p + f"srow{h2}_{qi}", ins, "vv")
                                c.need(p + f"sexp{h2}_{qi}")
                                v.reciprocal(SROW[:, 4:5], SROW[:, 3:4])
                                ins = v.tensor_scalar_mul(
                                    AA[:, 0:ns], AA[:, 0:ns], SROW[:, 4:5])
                                c.done(p + f"anorm{h2}_{qi}", ins, "vv")
                            c.need(p + f"omm{h2}_{qg}")
                            if h2 == 0 and qg == 0 and hg > 0:
                                c.need(f"{tag}h{hg - 1}_ostore")
                            ins = v.tensor_copy(
                                OTH[hoff:hoff + 64, qg * 512:(qg + 1) * 512],
                                PO[:])
                            c.done(p + f"oev{h2}_{qg}", ins, "vv")
                    c.alias(p + "oall", p + "oev1_3")
                c.body("vector", v_att)

                def s_att(s, p=p, hg=hg, PTs=PTs):
                    for h2 in range(2):
                        h = hg * 2 + h2
                        for qg in range(4):
                            ns = (qg + 1) * 512
                            for qi4 in range(4):
                                qi = qg * 4 + qi4
                                c.need(p + f"srow{h2}_{qi}")
                                ins = s.activation(
                                    AA[:, 0:ns], PS_S[:, 0:ns], AF.Exp,
                                    bias=SROW[:, 1:2], scale=ESCC[:, h:h + 1],
                                    accum_out=SROW[:, 3:4])
                                c.done(p + f"sexp{h2}_{qi}", ins, "ss")
                                for sc in range((qg + 1) * 4):
                                    c.need(p + f"tp{h2}_{qi}_{sc}")
                                    ins = s.activation(
                                        ATB[:, sc * 512 + qi4 * 128:
                                            sc * 512 + qi4 * 128 + 128],
                                        PTs[sc % 2][:], AF.Copy)
                                    c.done(p + f"tpe{h2}_{qi}_{sc}", ins, "ss")
                            c.alias(p + f"tpq{h2}_{qg}",
                                    p + f"tpe{h2}_{qg * 4 + 3}"
                                    f"_{(qg + 1) * 4 - 1}")
                c.body("scalar", s_att)

        def v_close(v):
            c.need(f"{tag}h7_oall")
        c.body("vector", v_close)
        stack2.close()

        with (
            nc.sbuf_tensor(f"{tag}ob", [128, KB * T], BF16) as OB,
            nc.sbuf_tensor(f"{tag}wo0", [128, KB * 256], BF16) as WO0,
            nc.sbuf_tensor(f"{tag}wo1", [128, KB * 256], BF16) as WO1,
            nc.psum_tensor(f"{tag}pf0", [128, 512], F32) as PF0,
            nc.psum_tensor(f"{tag}pf1", [128, 512], F32) as PF1,
        ):
            WOs = [WO0, WO1]
            PF = [PF0, PF1]

            def g_wo(g):
                for grp in range(4):
                    if grp == 0:
                        c.need(f"{tag}h7_ostore", f"{tag}h7_oall")
                    if grp >= 2:
                        c.need(f"{tag}fgrp{grp - 2}")
                    c.dma_batch(
                        f"{tag}ldo{grp}",
                        [(WOs[grp % 2][:, k * 256:(k + 1) * 256],
                          W[f"{tag}woT"][k * 128:(k + 1) * 128,
                                         grp * 256:(grp + 1) * 256])
                         for k in range(KB)],
                        sem=f"u{grp % 2}")
            c.body("gpsimd", g_wo)

            def g_ob(sy):
                c.need(f"{tag}h7_ostore")
                c.dma_batch(f"{tag}ldob",
                            [(OB[:, k * T:(k + 1) * T],
                              OD[k * 128:(k + 1) * 128, :])
                             for k in range(KB)], sem="qs")
            c.body("sync", g_ob)

            def t_wo(t):
                c.need(f"{tag}ldob")
                gi = 0
                for grp in range(4):
                    c.need(f"{tag}ldo{grp}")
                    for m2 in range(2):
                        for tb in range(TB):
                            if gi >= 2:
                                c.need(f"{tag}fev{gi - 2}")
                            ins = mm_accum(
                                c, t, PF[gi % 2][:],
                                [WOs[grp % 2][:, k * 256 + m2 * 128:
                                     k * 256 + m2 * 128 + 128]
                                 for k in range(KB)],
                                [OB[:, k * T + tb * 512: k * T + tb * 512 + 512]
                                 for k in range(KB)])
                            c.done(f"{tag}fmm{gi}", ins, "tt")
                            gi += 1
                    c.alias(f"{tag}fgrp{grp}", f"{tag}fmm{gi - 1}")
            c.body("tensor", t_wo)

            def v_wo(v):
                gi = 0
                ins = None
                for grp in range(4):
                    for m2 in range(2):
                        m = grp * 2 + m2
                        for tb in range(TB):
                            c.need(f"{tag}fmm{gi}")
                            sl = slice(m * T + tb * 512, m * T + tb * 512 + 512)
                            ins = v.tensor_tensor(X[:, sl], X[:, sl],
                                                  PF[gi % 2][:], ALU.add)
                            c.done(f"{tag}fev{gi}", ins, "vv")
                            gi += 1
                c.alias(f"{tag}res", f"{tag}fev{gi - 1}")
            c.body("vector", v_wo)

        def st_body(sy):
            c.need(f"{tag}res")
            c.dma_batch(f"{tag}sty",
                        [(W["yT"][k * 128:(k + 1) * 128, :],
                          X[:, k * T:(k + 1) * T]) for k in range(KB)])
            sy.wait_ge(c.sems["qs"].h, c.sems["qs"].n)
        c.body("sync", st_body)


def _prep_s7_inputs(ins, tag):
    out = {}
    for nm, key in (("wqaT", "L1_attn_b_Wq_amp"), ("wkaT", "L1_attn_b_Wk_amp"),
                    ("wqpT", "L1_attn_b_Wq_phi"), ("wkpT", "L1_attn_b_Wk_phi"),
                    ("wvT", "L1_attn_b_Wv"), ("woT", "L1_attn_b_Wo")):
        out[f"{tag}{nm}"] = np.ascontiguousarray(ins[key].T)
    out[f"{tag}sls"] = np.ascontiguousarray(
        ins["L1_attn_b_score_log_scale"].reshape(1, 16))
    ind = np.zeros((128, 2), np.float32)
    ind[0:64, 0] = 1.0
    ind[64:128, 1] = 1.0
    out[f"{tag}ind"] = ind
    out[f"{tag}indt"] = np.ascontiguousarray(ind.T)
    msk = np.zeros((128, 4 * 512), np.float32)
    for v in range(4):
        for p in range(128):
            msk[p, v * 512 + v * 128 + p + 1: (v + 1) * 512] = -1e30
    out[f"{tag}msk"] = msk
    return out


def s7_decls(tag):
    d = [("xT", [D, T], F32, "ExternalInput"),
         ("eye", [128, 128], BF16, "ExternalInput"),
         (f"{tag}sls", [1, 16], F32, "ExternalInput"),
         (f"{tag}ind", [128, 2], F32, "ExternalInput"),
         (f"{tag}indt", [2, 128], F32, "ExternalInput"),
         (f"{tag}msk", [128, 4 * 512], F32, "ExternalInput"),
         ("yT", [D, T], F32, "ExternalOutput")]
    for nm in ("wqaT", "wkaT", "wqpT", "wkpT", "wvT", "woT"):
        d.append((f"{tag}{nm}", [D, D], F32, "ExternalInput"))
    return d


def run_s7(x, ins):
    import ml_dtypes
    nc = build_single(lambda c, W: emit_s7(c, W, "s7_"), s7_decls("s7_"))
    shared = _prep_s7_inputs(ins, "s7_")
    shared["eye"] = np.eye(128, dtype=np.float32).astype(ml_dtypes.bfloat16)
    in_maps = []
    for b in range(B):
        m = dict(shared)
        m["xT"] = np.ascontiguousarray(x[b].T)
        in_maps.append(m)
    res = run_bass_kernel_spmd(nc, in_maps, core_ids=list(range(NCORES)))
    return np.stack([np.ascontiguousarray(r["yT"].T) for r in res.results])



def emit_s3(c, W, tag):
    nc = c.nc
    NBK = 16
    NS = 16
    SEG = 16
    WIN = 128
    NWIN = T // WIN
    NSEG = WIN // SEG
    for v in (1e-4, 10.0, 9.357622968840175e-14, 1e-6, -1.0, 1.0):
        c.const(v)
    est = contextlib.ExitStack()
    with est:
        X = est.enter_context(nc.sbuf_tensor(f"{tag}X", [128, KB * T], F32))
        INBX = est.enter_context(nc.sbuf_tensor(f"{tag}ibxc", [128, 16], F32))
        INBZ = est.enter_context(nc.sbuf_tensor(f"{tag}ibzc", [128, 16], F32))
        CVW = est.enter_context(nc.sbuf_tensor(f"{tag}cvwc", [128, 64], F32))
        CVB = est.enter_context(nc.sbuf_tensor(f"{tag}cvbc", [128, 16], F32))
        XPBC = est.enter_context(nc.sbuf_tensor(f"{tag}xpbc", [32, 1], F32))
        DTBC = est.enter_context(nc.sbuf_tensor(f"{tag}dtbc", [128, 16], F32))
        ALOG = est.enter_context(nc.sbuf_tensor(f"{tag}alogc", [128, 256], F32))
        AMAT = est.enter_context(nc.sbuf_tensor(f"{tag}amatc", [128, 256], F32))
        DCOL = est.enter_context(nc.sbuf_tensor(f"{tag}dcolc", [128, 16], F32))
        OPBC = est.enter_context(nc.sbuf_tensor(f"{tag}opbc", [128, 8], F32))
        INAC = est.enter_context(nc.sbuf_tensor(f"{tag}inac", [128, 8], F32))
        INB2 = est.enter_context(nc.sbuf_tensor(f"{tag}inb2c", [128, 8], F32))
        NAC = est.enter_context(nc.sbuf_tensor(f"{tag}nac", [128, 8], F32))
        NBCC = est.enter_context(nc.sbuf_tensor(f"{tag}nbcc", [128, 8], F32))
        ONESRB = est.enter_context(nc.sbuf_tensor(f"{tag}orb", [1, 128], BF16))
        ONESCF = est.enter_context(nc.sbuf_tensor(f"{tag}ocf", [128, 1], F32))
        ONESRF = est.enter_context(nc.sbuf_tensor(f"{tag}orf", [1, 128], F32))
        XCD = nc.dram_tensor(f"{tag}xcd", [NBK * 128, T], BF16,
                             kind="Internal").ap()
        ZD = nc.dram_tensor(f"{tag}zd", [NBK * 128, T], BF16,
                            kind="Internal").ap()
        DTD = nc.dram_tensor(f"{tag}dtd", [NBK * 128, T], F32,
                             kind="Internal").ap()
        BCD = nc.dram_tensor(f"{tag}bcd", [32, T], F32, kind="Internal").ap()
        X0D = nc.dram_tensor(f"{tag}x0d", [D, T], BF16, kind="Internal").ap()

        chain_gate = getattr(c, "chain_gate", None)

        def qload0(g):
            if chain_gate:
                c.need(chain_gate)
            pairs = [(X[:, k * T:(k + 1) * T],
                      W["xT"][k * 128:(k + 1) * 128, :]) for k in range(KB)]
            pairs += [(INBX[:], W[f"{tag}ibx"]), (INBZ[:], W[f"{tag}ibz"]),
                      (CVW[:], W[f"{tag}cvw"]), (CVB[:], W[f"{tag}cvb"]),
                      (XPBC[:], W[f"{tag}xpb"]), (DTBC[:], W[f"{tag}dtb"]),
                      (ALOG[:], W[f"{tag}alog"]), (DCOL[:], W[f"{tag}dcol"]),
                      (OPBC[:], W[f"{tag}opb"]), (INAC[:], W[f"{tag}ina"]),
                      (INB2[:], W[f"{tag}inb"]), (NAC[:], W[f"{tag}na"]),
                      (NBCC[:], W[f"{tag}nb"])]
            c.dma_batch(f"{tag}in0", pairs)
            for k in range(KB):
                c.alias(f"{tag}o_in{k}", f"{tag}in0")
            g.memset(ONESRB[:], 1.0)
            g.memset(ONESCF[:], 1.0)
            ins = g.memset(ONESRF[:], 1.0)
            c.done(f"{tag}ones1", ins, "gg")
        c.body("gpsimd", qload0)

        def s_amat(s):
            c.need(f"{tag}in0")
            ins = s.activation(AMAT[:], ALOG[:], AF.Exp)
            c.done(f"{tag}aexp", ins, "ss")
        c.body("scalar", s_amat)

        def v_amat(v):
            c.need(f"{tag}aexp")
            ins = v.tensor_scalar_mul(AMAT[:], AMAT[:], -1.0)
            c.done(f"{tag}amat", ins, "vv")
        c.body("vector", v_amat)

        stAB = contextlib.ExitStack()
        with contextlib.ExitStack() as stA:
            XN0 = stA.enter_context(
                nc.sbuf_tensor(f"{tag}xn0", [128, KB * T], BF16))
            XN = stAB.enter_context(
                nc.sbuf_tensor(f"{tag}xnn", [128, KB * T], BF16, side="right"))
            emit_ln3(c, X, XN0, f"{tag}o_")
            for k in range(KB):
                c.alias(f"{tag}i_in{k}", f"{tag}o_xn")
            emit_ln3(c, XN0, XN, f"{tag}i_", in_bf16=True, alpha=INAC,
                     beta=INB2, ddof1=True, eps=1e-6, eps_outside=True,
                     after=f"{tag}o_xn")

            def st_xn0(sy):
                c.need(f"{tag}i_xn")
                c.dma_batch(f"{tag}stx0",
                            [(X0D[k * 128:(k + 1) * 128, :],
                              XN0[:, k * T:(k + 1) * T]) for k in range(KB)])
            c.body("sync", st_xn0)

        with (
            nc.sbuf_tensor(f"{tag}wg0", [128, KB * 256], BF16) as WG0,
            nc.sbuf_tensor(f"{tag}wg1", [128, KB * 256], BF16) as WG1,
            nc.sbuf_tensor(f"{tag}xin", [128, 2 * (8 + T)], F32) as XIN,
            nc.sbuf_tensor(f"{tag}xcf", [128, T], F32) as XCF,
            nc.sbuf_tensor(f"{tag}xcbb", [128, 2 * T], BF16) as XCB,
            nc.sbuf_tensor(f"{tag}zbb", [128, 2 * T], BF16) as ZB,
            nc.sbuf_tensor(f"{tag}xpw", [128, 32], BF16) as XPW,
            nc.sbuf_tensor(f"{tag}bcs", [32, T], F32) as BCS,
            nc.psum_tensor(f"{tag}pg0", [128, 512], F32) as PG0,
            nc.psum_tensor(f"{tag}pg1", [128, 512], F32) as PG1,
            nc.psum_tensor(f"{tag}pbc0", [32, 512], F32) as PBC0,
            nc.psum_tensor(f"{tag}pbc1", [32, 512], F32) as PBC1,
            nc.psum_tensor(f"{tag}pbc2", [32, 512], F32) as PBC2,
            nc.psum_tensor(f"{tag}pbc3", [32, 512], F32) as PBC3,
        ):
            WGs = [WG0, WG1]
            PG = [PG0, PG1]
            PBC = [PBC0, PBC1, PBC2, PBC3]

            def g_b(g):
                c.need(f"{tag}i_xn")
                g.wait_ge(c.sems["qs"].h, c.events[f"{tag}stx0"][1])
                for grp in range(16):
                    if grp >= 2:
                        c.need(f"{tag}bgrp{grp - 2}")
                    c.dma_batch(
                        f"{tag}ldb{grp}",
                        [(WGs[grp % 2][:, k * 256:(k + 1) * 256],
                          W[f"{tag}winT"][k * 128:(k + 1) * 128,
                                          grp * 256:(grp + 1) * 256])
                         for k in range(KB)],
                        sem=f"w{grp % 2}")
                    if grp < 8:
                        for m2 in range(2):
                            dm = grp * 2 + m2
                            if dm > 0:
                                c.need(f"{tag}bcmm{dm - 1}")
                            c.dma_batch(f"{tag}ldxp{dm}",
                                        [(XPW[:], W[f"{tag}xprojT"]
                                          [dm * 128:(dm + 1) * 128, :])],
                                        sem="u0")
            c.body("gpsimd", g_b)

            def g_bst(sy):
                for dm in range(NBK):
                    c.need(f"{tag}xc{dm}")
                    c.dma_batch(f"{tag}stxc{dm}",
                                [(XCD[dm * 128:(dm + 1) * 128, :],
                                  XCB[:, (dm % 2) * T:(dm % 2 + 1) * T])],
                                sem=f"y{dm % 2}")
                for zi in range(NBK):
                    c.need(f"{tag}zev{zi}")
                    c.dma_batch(f"{tag}stz{zi}",
                                [(ZD[zi * 128:(zi + 1) * 128, :],
                                  ZB[:, (zi % 2) * T:(zi % 2 + 1) * T])],
                                sem=f"y{2 + zi % 2}")
                c.need(f"{tag}bcall")
                c.dma_batch(f"{tag}stbc", [(BCD, BCS[:])])
            c.body("sync", g_bst)

            def t_b(t):
                c.need(f"{tag}i_xn")
                gi = 0
                for grp in range(16):
                    c.need(f"{tag}ldb{grp}")
                    for m2 in range(2):
                        for tb in range(TB):
                            if gi >= 2:
                                c.need(f"{tag}bev{gi - 2}")
                            ins = mm_accum(
                                c, t, PG[gi % 2][:],
                                [WGs[grp % 2][:, k * 256 + m2 * 128:
                                    k * 256 + m2 * 128 + 128]
                                 for k in range(KB)],
                                [XN[:, k * T + tb * 512:
                                    k * T + tb * 512 + 512]
                                 for k in range(KB)])
                            c.done(f"{tag}bmm{gi}", ins, "tt")
                            gi += 1
                    c.alias(f"{tag}bgrp{grp}", f"{tag}bmm{gi - 1}")
                    if grp < 8:
                        for m2 in range(2):
                            dm = grp * 2 + m2
                            c.need(f"{tag}xc{dm}", f"{tag}ldxp{dm}")
                            for tb in range(TB):
                                ins = t.matmul(
                                    PBC[tb][:], XPW[:],
                                    XCB[:, (dm % 2) * T + tb * 512:
                                        (dm % 2) * T + tb * 512 + 512],
                                    start=(dm == 0), stop=(dm == NBK - 1))
                            c.done(f"{tag}bcmm{dm}", ins, "tt")
            c.body("tensor", t_b)

            def v_b(v):
                gi = 0
                for grp in range(16):
                    for m2 in range(2):
                        for tb in range(TB):
                            c.need(f"{tag}bmm{gi}")
                            if grp < 8:
                                dm = grp * 2 + m2
                                base = (dm % 2) * (8 + T)
                                if tb == 0 and dm >= 2:
                                    c.need(f"{tag}cv{dm - 2}")
                                ins = v.tensor_scalar(
                                    XIN[:, base + 8 + tb * 512:
                                        base + 8 + tb * 512 + 512],
                                    PG[gi % 2][:],
                                    INBX[:, dm:dm + 1], None, ALU.add)
                            else:
                                zi = (grp - 8) * 2 + m2
                                if tb == 0 and zi >= 2:
                                    c.need(f"{tag}stz{zi - 2}")
                                ins = v.tensor_scalar(
                                    ZB[:, (zi % 2) * T + tb * 512:
                                       (zi % 2) * T + tb * 512 + 512],
                                    PG[gi % 2][:],
                                    INBZ[:, zi:zi + 1], None, ALU.add)
                                if tb == TB - 1:
                                    c.done(f"{tag}zev{zi}", ins, "vv")
                            c.done(f"{tag}bev{gi}", ins, "vv")
                            gi += 1
                    if grp < 8:
                        for m2 in range(2):
                            dm = grp * 2 + m2
                            base = (dm % 2) * (8 + T)
                            if dm >= 2:
                                c.need(f"{tag}stxc{dm - 2}")
                            if dm < 2:
                                v.memset(XIN[:, base + 5:base + 8], 0.0)
                            v.tensor_scalar(
                                XCF[:], XIN[:, base + 5:base + 5 + T],
                                CVW[:, dm * 4:dm * 4 + 1],
                                CVB[:, dm:dm + 1], ALU.mult, ALU.add)
                            for j in range(1, 4):
                                ins = v.scalar_tensor_tensor(
                                    XCF[:],
                                    XIN[:, base + 5 + j:base + 5 + j + T],
                                    CVW[:, dm * 4 + j:dm * 4 + j + 1],
                                    XCF[:], ALU.mult, ALU.add)
                            c.done(f"{tag}cv{dm}", ins, "vv")
                            c.need(f"{tag}sil{dm}")
                    if grp == 15:
                        c.need(f"{tag}bcmm{NBK - 1}")
                        for tb in range(TB):
                            ins = v.tensor_scalar(
                                BCS[:, tb * 512:tb * 512 + 512],
                                PBC[tb][:], XPBC[:, 0:1], None, ALU.add)
                        c.done(f"{tag}bcall", ins, "vv")
            c.body("vector", v_b)

            def s_b(s):
                for dm in range(NBK):
                    c.need(f"{tag}cv{dm}")
                    if dm >= 2:
                        c.need(f"{tag}bcmm{dm - 2}", f"{tag}stxc{dm - 2}")
                    ins = s.activation(XCB[:, (dm % 2) * T:(dm % 2 + 1) * T],
                                       XCF[:], AF.Silu)
                    c.done(f"{tag}sil{dm}", ins, "ss")
                    c.alias(f"{tag}xc{dm}", f"{tag}sil{dm}")
            c.body("scalar", s_b)

        stAB.close()

        with (
            nc.sbuf_tensor(f"{tag}dwa", [128, 2 * 16 * 128], BF16) as DWB,
            nc.sbuf_tensor(f"{tag}xct", [128, NBK * 512], BF16) as XCT,
            nc.sbuf_tensor(f"{tag}dtt", [128, 2 * 512], F32) as DTT,
            nc.sbuf_tensor(f"{tag}dts", [128, 2 * 512], F32) as DTS,
            nc.psum_tensor(f"{tag}pd0", [128, 512], F32) as PD0,
            nc.psum_tensor(f"{tag}pd1", [128, 512], F32) as PD1,
        ):
            PD = [PD0, PD1]

            def g_c(g):
                bi = 0
                for tb in range(TB):
                    if tb > 0:
                        c.need(f"{tag}dmm{tb - 1}_15")
                    else:
                        c.need(f"{tag}stbc", f"{tag}stxc14", f"{tag}stxc15")
                    c.dma_batch(f"{tag}ldxct{tb}",
                                [(XCT[:, dm * 512:(dm + 1) * 512],
                                  XCD[dm * 128:(dm + 1) * 128,
                                      tb * 512:(tb + 1) * 512])
                                 for dm in range(NBK)], sem="u1")
                    for mg in range(16):
                        if bi >= 2:
                            pj = bi - 2
                            c.need(f"{tag}dmm{pj // 16}_{pj % 16}")
                        c.dma_batch(
                            f"{tag}lddw{tb}_{mg}",
                            [(DWB[:, (bi % 2) * 2048 + k * 128:
                                  (bi % 2) * 2048 + (k + 1) * 128],
                              W[f"{tag}dtT"][k * 128:(k + 1) * 128,
                                             mg * 128:(mg + 1) * 128])
                             for k in range(NBK)],
                            sem=f"w{bi % 2}")
                        bi += 1
            c.body("gpsimd", g_c)

            def t_c(t):
                bi = 0
                for tb in range(TB):
                    c.need(f"{tag}ldxct{tb}")
                    for mg in range(16):
                        c.need(f"{tag}lddw{tb}_{mg}")
                        if bi >= 2:
                            pj = bi - 2
                            c.need(f"{tag}dtab{pj // 16}_{pj % 16}")
                        ins = mm_accum(
                            c, t, PD[bi % 2][:],
                            [DWB[:, (bi % 2) * 2048 + k * 128:
                                 (bi % 2) * 2048 + (k + 1) * 128]
                             for k in range(NBK)],
                            [XCT[:, k * 512:(k + 1) * 512]
                             for k in range(NBK)])
                        c.done(f"{tag}dmm{tb}_{mg}", ins, "tt")
                        bi += 1
            c.body("tensor", t_c)

            def v_c(v):
                bi = 0
                for tb in range(TB):
                    for mg in range(16):
                        c.need(f"{tag}dmm{tb}_{mg}")
                        jj = bi % 2
                        if bi >= 2:
                            c.need(f"{tag}stdt{(bi - 2) // 16}_{(bi - 2) % 16}")
                        v.tensor_scalar(DTS[:, jj * 512:(jj + 1) * 512],
                                        PD[jj][:], DTBC[:, mg:mg + 1], None,
                                        ALU.add)
                        v.tensor_scalar_mul(
                            DTT[:, jj * 512:(jj + 1) * 512],
                            DTS[:, jj * 512:(jj + 1) * 512], -1.0)
                        ins = v.tensor_tensor(
                            DTT[:, jj * 512:(jj + 1) * 512],
                            DTT[:, jj * 512:(jj + 1) * 512],
                            DTS[:, jj * 512:(jj + 1) * 512], ALU.max)
                        c.done(f"{tag}dtab{tb}_{mg}", ins, "vv")
                        c.need(f"{tag}dtln{tb}_{mg}")
                        v.scalar_tensor_tensor(
                            DTS[:, jj * 512:(jj + 1) * 512],
                            DTS[:, jj * 512:(jj + 1) * 512], 0.0,
                            DTT[:, jj * 512:(jj + 1) * 512],
                            ALU.max, ALU.add)
                        ins = v.tensor_scalar(
                            DTS[:, jj * 512:(jj + 1) * 512],
                            DTS[:, jj * 512:(jj + 1) * 512],
                            1e-4, 10.0, ALU.max, ALU.min)
                        c.done(f"{tag}dtcl{tb}_{mg}", ins, "vv")
                        bi += 1
            c.body("vector", v_c)

            def s_c(s):
                for tb in range(TB):
                    for mg in range(16):
                        jj = (tb * 16 + mg) % 2
                        c.need(f"{tag}dtab{tb}_{mg}")
                        s.activation(DTT[:, jj * 512:(jj + 1) * 512],
                                     DTT[:, jj * 512:(jj + 1) * 512],
                                     AF.Exp, scale=-1.0)
                        ins = s.activation(DTT[:, jj * 512:(jj + 1) * 512],
                                           DTT[:, jj * 512:(jj + 1) * 512],
                                           AF.Ln, bias=1.0)
                        c.done(f"{tag}dtln{tb}_{mg}", ins, "ss")
            c.body("scalar", s_c)

            def g_cst(sy):
                for tb in range(TB):
                    for mg in range(16):
                        jj = (tb * 16 + mg) % 2
                        c.need(f"{tag}dtcl{tb}_{mg}")
                        c.dma_batch(f"{tag}stdt{tb}_{mg}",
                                    [(DTD[mg * 128:(mg + 1) * 128,
                                          tb * 512:(tb + 1) * 512],
                                      DTS[:, jj * 512:(jj + 1) * 512])],
                                    sem=f"y{(tb * 16 + mg) % 2}")
            c.body("sync", g_cst)

        OPWF = est.enter_context(
            nc.sbuf_tensor(f"{tag}opwf", [128, NBK * 1024], BF16,
                           side="right"))

        def g_dop(g):
            c.need(f"{tag}bgrp15")
            c.dma_batch(f"{tag}ldopf",
                        [(OPWF[:, k * 1024:(k + 1) * 1024],
                          W[f"{tag}outT"][k * 128:(k + 1) * 128, :])
                         for k in range(NBK)])
        c.body("gpsimd", g_dop)

        with contextlib.ExitStack() as psD:
            DTW = psD.enter_context(
                nc.sbuf_tensor(f"{tag}dtw", [128, NBK * WIN], F32))
            XCW = psD.enter_context(
                nc.sbuf_tensor(f"{tag}xcw", [128, NBK * WIN], BF16))
            ZW = psD.enter_context(
                nc.sbuf_tensor(f"{tag}zw", [128, NBK * WIN], BF16))
            BROW = psD.enter_context(
                nc.sbuf_tensor(f"{tag}brw", [1, NS * WIN], BF16))
            CROW = psD.enter_context(
                nc.sbuf_tensor(f"{tag}crw", [1, NS * WIN], BF16))
            DAT = psD.enter_context(
                nc.sbuf_tensor(f"{tag}dat", [128, 4096], F32))
            DBX = psD.enter_context(
                nc.sbuf_tensor(f"{tag}dbx", [128, 4096], F32))
            HOUT = psD.enter_context(
                nc.sbuf_tensor(f"{tag}hout", [128, 4096], F32))
            YW = psD.enter_context(
                nc.sbuf_tensor(f"{tag}yw", [128, NBK * WIN], F32))
            BRS = psD.enter_context(
                nc.sbuf_tensor(f"{tag}brs", [128, 256], F32))
            CRS = psD.enter_context(
                nc.sbuf_tensor(f"{tag}crs", [128, 256], F32))
            CAR = psD.enter_context(
                nc.sbuf_tensor(f"{tag}car", [128, 256], F32))
            DXS = psD.enter_context(
                nc.sbuf_tensor(f"{tag}dxs", [128, 256], F32))
            HSQ = psD.enter_context(
                nc.sbuf_tensor(f"{tag}hsq", [128, 256], F32))
            TMPB = psD.enter_context(
                nc.sbuf_tensor(f"{tag}tmp", [128, 256], F32))
            HNR = psD.enter_context(
                nc.sbuf_tensor(f"{tag}hnr", [128, 16], F32))
            YG = psD.enter_context(
                nc.sbuf_tensor(f"{tag}ygb", [128, NBK * WIN], BF16))
            X0W = psD.enter_context(
                nc.sbuf_tensor(f"{tag}x0w", [128, KB * WIN], BF16))
            SMB = psD.enter_context(
                nc.sbuf_tensor(f"{tag}smb", [128, KB * WIN], F32))
            SQW = psD.enter_context(
                nc.sbuf_tensor(f"{tag}sqw", [128, 2 * WIN], F32))
            LNR = psD.enter_context(
                nc.sbuf_tensor(f"{tag}lnr", [1, 4 * WIN], F32))
            PS = [psD.enter_context(
                nc.psum_tensor(f"{tag}psd{i}", [128, 256], F32))
                for i in range(8)]
            d3 = DAT.ap().rearrange("p (dm n t) -> p dm n t",
                                    dm=16, n=16, t=16)
            b3 = DBX.ap().rearrange("p (dm n t) -> p dm n t",
                                    dm=16, n=16, t=16)
            h3 = HOUT.ap().rearrange("p (dm n t) -> p dm n t",
                                     dm=16, n=16, t=16)
            brow3 = BROW.ap().rearrange("p (n t) -> p n t", n=NS, t=WIN)
            crow3 = CROW.ap().rearrange("p (n t) -> p n t", n=NS, t=WIN)

            def g_d(g):
                for w in range(NWIN):
                    lb = f"{tag}w{w}_"
                    if w == 0:
                        c.need(f"{tag}stdt{TB - 1}_14", f"{tag}stdt{TB - 1}_15",
                               f"{tag}stz14", f"{tag}stz15",
                               f"{tag}stxc14", f"{tag}stxc15",
                               f"{tag}stbc", f"{tag}stx0")
                    else:
                        c.need(f"{tag}w{w - 1}_sumb")
                    wc = slice(w * WIN, (w + 1) * WIN)
                    pairs = [(DTW[:, dm * WIN:(dm + 1) * WIN],
                              DTD[dm * 128:(dm + 1) * 128, wc])
                             for dm in range(NBK)]
                    pairs += [(XCW[:, dm * WIN:(dm + 1) * WIN],
                               XCD[dm * 128:(dm + 1) * 128, wc])
                              for dm in range(NBK)]
                    pairs += [(ZW[:, dm * WIN:(dm + 1) * WIN],
                               ZD[dm * 128:(dm + 1) * 128, wc])
                              for dm in range(NBK)]
                    pairs += [(brow3, BCD[0:16, wc].unsqueeze(0)),
                              (crow3, BCD[16:32, wc].unsqueeze(0))]
                    pairs += [(X0W[:, k * WIN:(k + 1) * WIN],
                               X0D[k * 128:(k + 1) * 128, wc])
                              for k in range(KB)]
                    c.dma_batch(lb + "ld", pairs, sem=f"u{w % 2}")
            c.body("gpsimd", g_d)

            def t_d(t):
                for w in range(NWIN):
                    lb = f"{tag}w{w}_"
                    c.need(lb + "ld")
                    if w == 0:
                        c.need(f"{tag}amat", f"{tag}ones1", f"{tag}ldopf")
                    for s_ in range(NSEG):
                        sl = f"{lb}s{s_}_"
                        if s_ > 0:
                            c.need(f"{lb}s{s_ - 1}_bcev")
                        elif w > 0:
                            c.need(f"{tag}w{w - 1}_used")
                        ins = t.matmul(
                            PS[0][:].rearrange("p (a b) -> p a b",
                                               a=16, b=16),
                            ONESRB[:],
                            brow3[:, :, s_ * SEG:(s_ + 1) * SEG],
                            start=True, stop=True)
                        c.done(sl + "brep", ins, "tt")
                        ins = t.matmul(
                            PS[1][:].rearrange("p (a b) -> p a b",
                                               a=16, b=16),
                            ONESRB[:],
                            crow3[:, :, s_ * SEG:(s_ + 1) * SEG],
                            start=True, stop=True)
                        c.done(sl + "crep", ins, "tt")
                    c.need(lb + "yg")
                    for m in range(KB):
                        if w > 0:
                            c.need(f"{tag}w{w - 1}_oev{m}")
                        ins = mm_accum(
                            c, t, PS[m][:, 0:WIN],
                            [OPWF[:, k * 1024 + m * 128:
                                  k * 1024 + m * 128 + 128]
                             for k in range(NBK)],
                            [YG[:, k * WIN:(k + 1) * WIN]
                             for k in range(NBK)])
                        c.done(lb + f"omm{m}", ins, "tt")
                    c.need(lb + "sumb")
                    for k in range(KB):
                        ins = t.matmul(PS[0][0:1, 0:WIN], ONESCF[:, 0:1],
                                       SMB[:, k * WIN:(k + 1) * WIN],
                                       start=(k == 0), stop=(k == KB - 1))
                    c.done(lb + "lnsx", ins, "tt")
                    c.need(lb + "sxev")
                    for k in range(KB):
                        c.need(lb + f"lsq{k}")
                        ins = t.matmul(PS[0][0:1, 0:WIN], ONESCF[:, 0:1],
                                       SQW[:, (k % 2) * WIN:
                                           (k % 2 + 1) * WIN],
                                       start=(k == 0), stop=(k == KB - 1))
                        c.done(lb + f"lnsqk{k}", ins, "tt")
                    c.alias(lb + "lnsq", lb + f"lnsqk{KB - 1}")
                    c.need(lb + "rowab")
                    ins = t.matmul(PS[0][:, 0:WIN], ONESRF[:],
                                   LNR[:, 2 * WIN:3 * WIN],
                                   start=True, stop=True)
                    c.done(lb + "bca", ins, "tt")
                    ins = t.matmul(PS[1][:, 0:WIN], ONESRF[:],
                                   LNR[:, 3 * WIN:4 * WIN],
                                   start=True, stop=True)
                    c.done(lb + "bcb", ins, "tt")
            c.body("tensor", t_d)

            def v_d(v):
                v.memset(CAR[:], 0.0)
                for w in range(NWIN):
                    lb = f"{tag}w{w}_"
                    for s_ in range(NSEG):
                        sl = f"{lb}s{s_}_"
                        ts0 = s_ * SEG
                        if s_ > 0:
                            c.need(f"{lb}s{s_ - 1}_dbx0",
                                   f"{lb}s{s_ - 1}_cmul")
                        elif w > 0:
                            c.need(f"{tag}w{w - 1}_s{NSEG - 1}_dbx0",
                                   f"{tag}w{w - 1}_s{NSEG - 1}_cmul")
                        c.need(sl + "brep")
                        v.tensor_copy(BRS[:], PS[0][:])
                        c.need(sl + "crep")
                        ins = v.tensor_copy(CRS[:], PS[1][:])
                        c.done(sl + "bcev", ins, "vv")
                        dtb_ap = DTW.ap().rearrange(
                            "p (dm t) -> p dm t", dm=16, t=WIN)[
                            :, :, ts0:ts0 + SEG].unsqueeze(2).broadcast_to(
                            [128, 16, 16, 16])
                        amat_ap = AMAT.ap().rearrange(
                            "p (dm n) -> p dm n", dm=16,
                            n=16).unsqueeze(3).broadcast_to([128, 16, 16, 16])
                        ins = v.tensor_tensor(d3, dtb_ap, amat_ap, ALU.mult)
                        c.done(sl + "da0", ins, "vv")
                        xcw_ap = XCW.ap().rearrange(
                            "p (dm t) -> p dm t", dm=16, t=WIN)[
                            :, :, ts0:ts0 + SEG]
                        dtw_ap = DTW.ap().rearrange(
                            "p (dm t) -> p dm t", dm=16, t=WIN)[
                            :, :, ts0:ts0 + SEG]
                        dxs3 = DXS.ap().rearrange("p (a b) -> p a b",
                                                  a=16, b=16)
                        ins = v.tensor_tensor(dxs3, dtw_ap, xcw_ap, ALU.mult)
                        c.done(sl + "dx", ins, "vv")
                        dx_ap = DXS.ap().rearrange(
                            "p (dm t) -> p dm t", dm=16, t=16).unsqueeze(
                            2).broadcast_to([128, 16, 16, 16])
                        brs_ap = BRS.ap().rearrange(
                            "p (n t) -> p n t", n=16, t=16).unsqueeze(
                            1).broadcast_to([128, 16, 16, 16])
                        ins = v.tensor_tensor(b3, dx_ap, brs_ap, ALU.mult)
                        c.done(sl + "dbx0", ins, "vv")
                        c.need(sl + "daexp")
                        v.tensor_tensor(TMPB.ap().rearrange(
                            "p (a b) -> p a b", a=16, b=16),
                            d3[:, :, :, 0:1].squeeze(3),
                            CAR.ap().rearrange("p (a b) -> p a b",
                                               a=16, b=16),
                            ALU.mult)
                        v.tensor_tensor(
                            b3[:, :, :, 0:1].squeeze(3),
                            b3[:, :, :, 0:1].squeeze(3),
                            TMPB.ap().rearrange("p (a b) -> p a b",
                                                a=16, b=16),
                            ALU.add)
                        ins = v.memset(d3[:, :, :, 0:1].squeeze(3), 0.0)
                        c.done(sl + "da_z", ins, "vv")
                        ins = v.tensor_tensor_scan(HOUT[:], DAT[:], DBX[:],
                                                   0.0, ALU.mult, ALU.add)
                        c.done(sl + "scan", ins, "vv")
                        c.need(sl + "hsq")
                        hsq3 = HSQ.ap().rearrange("p (dm n) -> p dm n",
                                                  dm=16, n=16)
                        ins = v.tensor_reduce(HNR[:, 0:16], hsq3, AX.X,
                                              ALU.add)
                        c.done(sl + "hn2", ins, "vv")
                        c.need(sl + "hnsq")
                        v.reciprocal(HNR[:, 0:16], HNR[:, 0:16])
                        v.tensor_scalar(HNR[:, 0:16], HNR[:, 0:16], 10.0, 1.0,
                                        ALU.mult, ALU.min)
                        scl_ap = HNR.ap()[:, 0:16].unsqueeze(2).broadcast_to(
                            [128, 16, 16])
                        ins = v.tensor_tensor(
                            CAR.ap().rearrange("p (a b) -> p a b",
                                               a=16, b=16),
                            h3[:, :, :, 15:16].squeeze(3),
                            scl_ap, ALU.mult)
                        c.done(sl + "car", ins, "vv")
                        crs_ap = CRS.ap().rearrange(
                            "p (n t) -> p n t", n=16, t=16).unsqueeze(
                            1).broadcast_to([128, 16, 16, 16])
                        ins = v.tensor_tensor(b3, h3, crs_ap, ALU.mult)
                        c.done(sl + "cmul", ins, "vv")
                        v.tensor_tensor(b3[:, :, 0:8, :], b3[:, :, 0:8, :],
                                        b3[:, :, 8:16, :], ALU.add)
                        v.tensor_tensor(b3[:, :, 0:4, :], b3[:, :, 0:4, :],
                                        b3[:, :, 4:8, :], ALU.add)
                        v.tensor_tensor(b3[:, :, 0:2, :], b3[:, :, 0:2, :],
                                        b3[:, :, 2:4, :], ALU.add)
                        v.tensor_tensor(b3[:, :, 0:1, :], b3[:, :, 0:1, :],
                                        b3[:, :, 1:2, :], ALU.add)
                        yw_ap = YW.ap().rearrange("p (dm t) -> p dm t",
                                                  dm=16, t=WIN)[
                            :, :, ts0:ts0 + SEG]
                        dcol_ap = DCOL.ap().unsqueeze(2).broadcast_to(
                            [128, 16, 16])
                        v.tensor_tensor(TMPB.ap().rearrange(
                            "p (a b) -> p a b", a=16, b=16),
                            xcw_ap, dcol_ap, ALU.mult)
                        ins = v.tensor_tensor(
                            yw_ap, b3[:, :, 0:1, :].squeeze(2),
                            TMPB.ap().rearrange("p (a b) -> p a b",
                                                a=16, b=16),
                            ALU.add)
                        c.done(sl + "y", ins, "vv")
                    c.need(lb + "zsil")
                    ins = v.tensor_tensor(YG[:], YW[:], ZW[:], ALU.mult)
                    c.done(lb + "yg", ins, "vv")
                    for m in range(KB):
                        c.need(lb + f"omm{m}")
                        v.tensor_scalar(SMB[:, m * WIN:(m + 1) * WIN],
                                        PS[m][:, 0:WIN], OPBC[:, m:m + 1],
                                        None, ALU.add)
                        ins = v.tensor_tensor(SMB[:, m * WIN:(m + 1) * WIN],
                                              SMB[:, m * WIN:(m + 1) * WIN],
                                              X0W[:, m * WIN:(m + 1) * WIN],
                                              ALU.add)
                        c.done(lb + f"oev{m}", ins, "vv")
                    c.alias(lb + "sumb", lb + f"oev{KB - 1}")
                    c.need(lb + "lnsx")
                    ins = v.tensor_copy(LNR[:, 0:WIN], PS[0][0:1, 0:WIN])
                    c.done(lb + "sxev", ins, "vv")
                    ins = v.tensor_scalar_mul(LNR[:, WIN:2 * WIN],
                                              LNR[:, 0:WIN], 1.0 / D)
                    c.done(lb + "lnm", ins, "vv")
                    c.need(lb + "lnsq", lb + "lmsq")
                    ins = v.scalar_tensor_tensor(
                        LNR[:, 2 * WIN:3 * WIN], PS[0][0:1, 0:WIN], 1.0 / D,
                        LNR[:, 3 * WIN:4 * WIN], ALU.mult, ALU.subtract)
                    c.done(lb + "lvar", ins, "vv")
                    c.need(lb + "lsd")
                    ins = v.reciprocal(LNR[:, 2 * WIN:3 * WIN],
                                       LNR[:, 2 * WIN:3 * WIN])
                    c.done(lb + "lra", ins, "vv")
                    ins = v.scalar_tensor_tensor(
                        LNR[:, 3 * WIN:4 * WIN], LNR[:, WIN:2 * WIN], -1.0,
                        LNR[:, 2 * WIN:3 * WIN], ALU.mult, ALU.mult)
                    c.done(lb + "rowab", ins, "vv")
                    c.need(lb + "bca", lb + "bcb")
                    for k in range(KB):
                        v.tensor_tensor(SQW[:, 0:WIN],
                                        SMB[:, k * WIN:(k + 1) * WIN],
                                        PS[0][:, 0:WIN], ALU.mult)
                        v.tensor_tensor(SQW[:, 0:WIN], SQW[:, 0:WIN],
                                        PS[1][:, 0:WIN], ALU.add)
                        v.tensor_scalar(SQW[:, 0:WIN], SQW[:, 0:WIN],
                                        NAC[:, k:k + 1], NBCC[:, k:k + 1],
                                        ALU.mult, ALU.add)
                        ins = v.tensor_tensor(
                            X[:, k * T + w * WIN:k * T + w * WIN + WIN],
                            X[:, k * T + w * WIN:k * T + w * WIN + WIN],
                            SQW[:, 0:WIN], ALU.add)
                        c.done(lb + f"xadd{k}", ins, "vv")
                    c.alias(lb + "used", lb + f"xadd{KB - 1}")
            c.body("vector", v_d)



            def s_d(s):
                for w in range(NWIN):
                    lb = f"{tag}w{w}_"
                    for s_ in range(NSEG):
                        sl = f"{lb}s{s_}_"
                        c.need(sl + "da0")
                        ins = s.activation(DAT[:], DAT[:], AF.Exp)
                        c.done(sl + "daexp", ins, "ss")
                        c.need(sl + "scan")
                        ins = s.activation(
                            HSQ[:],
                            h3[:, :, :, 15:16].squeeze(3).rearrange(
                                "p a b -> p (a b)"),
                            AF.Square)
                        c.done(sl + "hsq", ins, "ss")
                        c.need(sl + "hn2")
                        ins = s.activation(HNR[:, 0:16], HNR[:, 0:16],
                                           AF.Sqrt)
                        c.done(sl + "hnsq", ins, "ss")
                    c.need(lb + "ld")
                    if w > 0:
                        c.need(f"{tag}w{w - 1}_yg")
                    ins = s.activation(ZW[:], ZW[:], AF.Silu)
                    c.done(lb + "zsil", ins, "ss")
                    for k in range(KB):
                        c.need(lb + f"oev{k}")
                        if k >= 2:
                            c.need(lb + f"lnsqk{k - 2}")
                        ins = s.activation(SQW[:, (k % 2) * WIN:
                                               (k % 2 + 1) * WIN],
                                           SMB[:, k * WIN:(k + 1) * WIN],
                                           AF.Square)
                        c.done(lb + f"lsq{k}", ins, "ss")
                    c.need(lb + "lnm")
                    ins = s.activation(LNR[:, 3 * WIN:4 * WIN],
                                       LNR[:, WIN:2 * WIN], AF.Square)
                    c.done(lb + "lmsq", ins, "ss")
                    c.need(lb + "lvar")
                    ins = s.activation(LNR[:, 2 * WIN:3 * WIN],
                                       LNR[:, 2 * WIN:3 * WIN], AF.Sqrt,
                                       bias=1e-5)
                    c.done(lb + "lsd", ins, "ss")
            c.body("scalar", s_d)

        def st_body(sy):
            for w in range(NWIN):
                for k in range(KB):
                    c.need(f"{tag}w{w}_xadd{k}")
            c.dma_batch(f"{tag}sty",
                        [(W["yT"][k * 128:(k + 1) * 128, :],
                          X[:, k * T:(k + 1) * T]) for k in range(KB)])
            sy.wait_ge(c.sems["qs"].h, c.sems["qs"].n)
        c.body("sync", st_body)


def _prep_s3_inputs(ins, tag):
    def colt(v, nb):
        return np.ascontiguousarray(v.reshape(nb, 128).T)
    out = {
        f"{tag}winT": np.ascontiguousarray(ins["L0_attn_b_in_proj_w"].T),
        f"{tag}ibx": colt(ins["L0_attn_b_in_proj_b"][:2048], 16),
        f"{tag}ibz": colt(ins["L0_attn_b_in_proj_b"][2048:], 16),
        f"{tag}cvw": np.ascontiguousarray(
            ins["L0_attn_b_conv_w"].reshape(16, 128, 4).transpose(1, 0, 2)
            .reshape(128, 64)),
        f"{tag}cvb": colt(ins["L0_attn_b_conv_b"], 16),
        f"{tag}xprojT": np.ascontiguousarray(ins["L0_attn_b_x_proj_w"].T),
        f"{tag}xpb": np.ascontiguousarray(
            ins["L0_attn_b_x_proj_b"].reshape(32, 1)),
        f"{tag}dtT": np.ascontiguousarray(ins["L0_attn_b_dt_proj_w"].T),
        f"{tag}dtb": colt(ins["L0_attn_b_dt_proj_b"], 16),
        f"{tag}alog": np.ascontiguousarray(
            ins["L0_attn_b_A_log"].reshape(16, 128, 16).transpose(1, 0, 2)
            .reshape(128, 256)),
        f"{tag}dcol": colt(ins["L0_attn_b_D"], 16),
        f"{tag}outT": np.ascontiguousarray(ins["L0_attn_b_out_proj_w"].T),
        f"{tag}opb": colt(ins["L0_attn_b_out_proj_b"], 8),
        f"{tag}ina": colt(ins["L0_attn_b_in_norm_alpha"], 8),
        f"{tag}inb": colt(ins["L0_attn_b_in_norm_bias"], 8),
        f"{tag}na": colt(ins["L0_attn_b_norm_alpha"], 8),
        f"{tag}nb": colt(ins["L0_attn_b_norm_bias"], 8),
    }
    return out


def s3_decls(tag):
    return [
        ("xT", [D, T], F32, "ExternalInput"),
        (f"{tag}winT", [D, 4096], F32, "ExternalInput"),
        (f"{tag}ibx", [128, 16], F32, "ExternalInput"),
        (f"{tag}ibz", [128, 16], F32, "ExternalInput"),
        (f"{tag}cvw", [128, 64], F32, "ExternalInput"),
        (f"{tag}cvb", [128, 16], F32, "ExternalInput"),
        (f"{tag}xprojT", [2048, 32], F32, "ExternalInput"),
        (f"{tag}xpb", [32, 1], F32, "ExternalInput"),
        (f"{tag}dtT", [2048, 2048], F32, "ExternalInput"),
        (f"{tag}dtb", [128, 16], F32, "ExternalInput"),
        (f"{tag}alog", [128, 256], F32, "ExternalInput"),
        (f"{tag}dcol", [128, 16], F32, "ExternalInput"),
        (f"{tag}outT", [2048, 1024], F32, "ExternalInput"),
        (f"{tag}opb", [128, 8], F32, "ExternalInput"),
        (f"{tag}ina", [128, 8], F32, "ExternalInput"),
        (f"{tag}inb", [128, 8], F32, "ExternalInput"),
        (f"{tag}na", [128, 8], F32, "ExternalInput"),
        (f"{tag}nb", [128, 8], F32, "ExternalInput"),
        ("yT", [D, T], F32, "ExternalOutput"),
    ]


def run_s3(x, ins):
    nc = build_single(lambda c, W: emit_s3(c, W, "s3_"), s3_decls("s3_"))
    shared = _prep_s3_inputs(ins, "s3_")
    in_maps = []
    for b in range(B):
        m = dict(shared)
        m["xT"] = np.ascontiguousarray(x[b].T)
        in_maps.append(m)
    res = run_bass_kernel_spmd(nc, in_maps, core_ids=list(range(NCORES)))
    return np.stack([np.ascontiguousarray(r["yT"].T) for r in res.results])



def emit_s5(c, W, tag):
    nc = c.nc
    NH, HD, NS, Q, NCH = 32, 64, 16, 128, 16
    for v in (-1.0, 1.0, 1e-5, -0.5, 1.0 / 2048.0):
        c.const(v)
    est = contextlib.ExitStack()
    with est:
        X = est.enter_context(nc.sbuf_tensor(f"{tag}X", [128, KB * T], F32))
        EYE = est.enter_context(nc.sbuf_tensor(f"{tag}eyeb", [128, 128], BF16))
        EYEF = est.enter_context(nc.sbuf_tensor(f"{tag}eyefs", [128, 128], F32))
        MSK01 = est.enter_context(nc.sbuf_tensor(f"{tag}m01", [128, 128], BF16))
        ONESRB = est.enter_context(nc.sbuf_tensor(f"{tag}orb", [1, 128], BF16))
        ONESRF = est.enter_context(nc.sbuf_tensor(f"{tag}orf", [1, 128], F32))
        LSC = est.enter_context(nc.sbuf_tensor(f"{tag}lscs", [128, 8], F32))
        XN = est.enter_context(
            nc.sbuf_tensor(f"{tag}xnb", [128, KB * T], BF16, side="right"))
        WCD = {dd: nc.dram_tensor(f"{tag}wc{dd}", [2048, 1024], BF16,
                                  kind="Internal").ap() for dd in "fb"}
        YND = {dd: nc.dram_tensor(f"{tag}yn{dd}", [2048, T], BF16,
                                  kind="Internal").ap() for dd in "fb"}

        chain_gate = getattr(c, "chain_gate", None)

        def qload0(g):
            if chain_gate:
                c.need(chain_gate)
            pairs = [(X[:, k * T:(k + 1) * T],
                      W["xT"][k * 128:(k + 1) * 128, :]) for k in range(KB)]
            pairs += [(EYE[:], W["eye"]), (EYEF[:], W[f"{tag}eyef"]),
                      (MSK01[:], W[f"{tag}msk01"]), (LSC[:], W[f"{tag}lsc"])]
            c.dma_batch(f"{tag}in0", pairs)
            for k in range(KB):
                c.alias(f"{tag}ln_in{k}", f"{tag}in0")
            g.memset(ONESRB[:], 1.0)
            ins = g.memset(ONESRF[:], 1.0)
            c.done(f"{tag}ones", ins, "gg")
        c.body("gpsimd", qload0)

        emit_ln3(c, X, XN, f"{tag}ln_")

        def v_addxn(v):
            c.need(f"{tag}ln_xn")
            ins = None
            for k in range(KB):
                ins = v.tensor_tensor(X[:, k * T:(k + 1) * T],
                                      X[:, k * T:(k + 1) * T],
                                      XN[:, k * T:(k + 1) * T], ALU.add)
            c.done(f"{tag}xadd", ins, "vv")
        c.body("vector", v_addxn)

        def wcn(ge):
            dd2, e2 = ("f", ge) if ge < 16 else ("b", ge - 16)
            return f"{tag}wcev_{dd2}{e2}", f"{tag}stwc_{dd2}{e2}"

        with contextlib.ExitStack() as wcs:
            OPN = wcs.enter_context(
                nc.sbuf_tensor(f"{tag}opn", [128, KB * 2048], BF16))
            LOT = wcs.enter_context(
                nc.sbuf_tensor(f"{tag}lot", [128, KB * 1024], BF16))
            WCS = wcs.enter_context(
                nc.sbuf_tensor(f"{tag}wcs", [128, 2 * 1024], BF16))
            PWC0 = wcs.enter_context(
                nc.psum_tensor(f"{tag}pwc0", [128, 1024], F32))
            PWC1 = wcs.enter_context(
                nc.psum_tensor(f"{tag}pwc1", [128, 1024], F32))
            PWC = [PWC0, PWC1]

            def g_wc(g):
                for di, dd in enumerate("fb"):
                    if di == 0:
                        c.need(f"{tag}ln_xn")
                    else:
                        c.need(f"{tag}wcev_f15")
                    c.dma_batch(f"{tag}ldop_{dd}",
                                [(OPN[:, k * 2048:(k + 1) * 2048],
                                  W[f"{tag}op_{dd}"][k * 128:(k + 1) * 128, :])
                                 for k in range(KB)], sem="u0")
                    c.dma_batch(f"{tag}ldlo_{dd}",
                                [(LOT[:, k * 1024:(k + 1) * 1024],
                                  W[f"{tag}loT"][di * 1024 + k * 128:
                                                 di * 1024 + (k + 1) * 128, :])
                                 for k in range(KB)], sem="u1")
            c.body("gpsimd", g_wc)

            def sy_wc(sy):
                for ge in range(32):
                    dd, e = ("f", ge) if ge < 16 else ("b", ge - 16)
                    c.need(f"{tag}wcev_{dd}{e}")
                    c.dma_batch(f"{tag}stwc_{dd}{e}",
                                [(WCD[dd][e * 128:(e + 1) * 128, :],
                                  WCS[:, (ge % 2) * 1024:
                                      (ge % 2 + 1) * 1024])],
                                sem=f"y{ge % 2}")
            c.body("sync", sy_wc)

            def t_wc(t):
                for ge in range(32):
                    dd, e = ("f", ge) if ge < 16 else ("b", ge - 16)
                    if e == 0:
                        c.need(f"{tag}ldop_{dd}", f"{tag}ldlo_{dd}")
                    if ge >= 2:
                        c.need(wcn(ge - 2)[0])
                    ins = None
                    for nb in range(2):
                        for k in range(KB):
                            ins = t.matmul(
                                PWC[ge % 2][:, nb * 512:(nb + 1) * 512],
                                OPN[:, k * 2048 + e * 128:
                                    k * 2048 + e * 128 + 128],
                                LOT[:, k * 1024 + nb * 512:
                                    k * 1024 + nb * 512 + 512],
                                start=(k == 0), stop=(k == KB - 1))
                    c.done(f"{tag}wcmm_{dd}{e}", ins, "tt")
            c.body("tensor", t_wc)

            def v_wc(v):
                for ge in range(32):
                    dd, e = ("f", ge) if ge < 16 else ("b", ge - 16)
                    c.need(f"{tag}wcmm_{dd}{e}")
                    if ge >= 2:
                        c.need(wcn(ge - 2)[1])
                    ins = v.tensor_copy(
                        WCS[:, (ge % 2) * 1024:(ge % 2 + 1) * 1024],
                        PWC[ge % 2][:])
                    c.done(f"{tag}wcev_{dd}{e}", ins, "vv")
            c.body("vector", v_wc)

        def _dir(di, dd):
            p = f"{tag}{dd}_"
            SHDm = nc.dram_tensor(p + "shd", [32, T], F32, kind="Internal").ap()
            LEDm = nc.dram_tensor(p + "led", [32, 16], F32, kind="Internal").ap()
            ZTDm = nc.dram_tensor(p + "ztd", [T, 2048], BF16,
                                  kind="Internal").ap()
            XSDm = nc.dram_tensor(p + "xsd", [T, 2048], BF16,
                                  kind="Internal").ap()

            st2 = contextlib.ExitStack()
            SHT = st2.enter_context(
                nc.sbuf_tensor(p + "sht", [128, NCH * 32], F32, side="right"))
            EXPST = st2.enter_context(
                nc.sbuf_tensor(p + "exst", [128, NCH * 32], F32, side="right"))
            EXPWT = st2.enter_context(
                nc.sbuf_tensor(p + "exwt", [128, NCH * 32], F32, side="right"))
            DTT = st2.enter_context(
                nc.sbuf_tensor(p + "dttk", [128, NCH * 32], BF16, side="right"))
            BSB = st2.enter_context(
                nc.sbuf_tensor(p + "bsbr", [16, T], BF16, side="right"))
            CSB = st2.enter_context(
                nc.sbuf_tensor(p + "csbr", [16, T], BF16, side="right"))
            LAMC = st2.enter_context(
                nc.sbuf_tensor(p + "lamc", [16, 512], F32, side="right"))
            DPREP = st2.enter_context(
                nc.sbuf_tensor(p + "dpr", [128, 2048], BF16, side="right"))
            NWREP = st2.enter_context(
                nc.sbuf_tensor(p + "nwr", [128, 2048], BF16, side="right"))

            s1e = contextlib.ExitStack()
            with s1e:
                if dd == "b":
                    REV = s1e.enter_context(
                        nc.sbuf_tensor(p + "rev", [128, 2048], BF16))

                    def v_rev(v):
                        c.need(f"{tag}f_dirdone", f"{tag}f_c15_ynspill")
                        ins = None
                        for k in range(KB):
                            v.tensor_copy(REV[:], XN[:, k * T:(k + 1) * T])
                            ins = v.tensor_copy(XN[:, k * T:(k + 1) * T],
                                                REV[:, ::-1])
                        c.done(f"{tag}revdone", ins, "vv")
                    c.body("vector", v_rev)
                WG0 = s1e.enter_context(
                    nc.sbuf_tensor(p + "wg0", [128, KB * 256], BF16))
                WG1 = s1e.enter_context(
                    nc.sbuf_tensor(p + "wg1", [128, KB * 256], BF16))
                XIN = s1e.enter_context(
                    nc.sbuf_tensor(p + "xin", [128, 2 * (8 + T)], BF16))
                XCF = s1e.enter_context(nc.sbuf_tensor(p + "xcf", [128, T], F32))
                XSB = s1e.enter_context(
                    nc.sbuf_tensor(p + "xsb", [128, 2 * T], BF16))
                ZTB = s1e.enter_context(
                    nc.sbuf_tensor(p + "ztb", [128, 2 * 256], BF16))
                XTT = s1e.enter_context(
                    nc.sbuf_tensor(p + "xtt", [128, 2 * 128], BF16))
                BCRB = s1e.enter_context(
                    nc.sbuf_tensor(p + "bcrb", [16, 8 + T], BF16))
                BCRC = s1e.enter_context(
                    nc.sbuf_tensor(p + "bcrc", [16, 8 + T], BF16))
                DTR = s1e.enter_context(nc.sbuf_tensor(p + "dtr", [32, T], F32))
                DTH = s1e.enter_context(nc.sbuf_tensor(p + "dth", [32, T], F32))
                SH = s1e.enter_context(nc.sbuf_tensor(p + "sh", [32, T], F32))
                EXPS = s1e.enter_context(nc.sbuf_tensor(p + "exs", [32, T], F32))
                EXPW = DTR
                ONES32 = s1e.enter_context(
                    nc.sbuf_tensor(p + "o32", [32, 128], F32))
                SRO = s1e.enter_context(nc.sbuf_tensor(p + "sro", [32, 32], F32))
                ACOL = s1e.enter_context(
                    nc.sbuf_tensor(p + "acol", [32, 1], F32))
                CVWB = s1e.enter_context(
                    nc.sbuf_tensor(p + "cvwb", [128, 64], F32))
                CVBB = s1e.enter_context(
                    nc.sbuf_tensor(p + "cvbb", [128, 16], F32))
                CVW2B = s1e.enter_context(
                    nc.sbuf_tensor(p + "cvw2b", [16, 4], F32))
                CVW2C = s1e.enter_context(
                    nc.sbuf_tensor(p + "cvw2c", [16, 4], F32))
                CVB2B = s1e.enter_context(
                    nc.sbuf_tensor(p + "cvb2b", [16, 1], F32))
                CVB2C = s1e.enter_context(
                    nc.sbuf_tensor(p + "cvb2c", [16, 1], F32))
                DTBC = s1e.enter_context(
                    nc.sbuf_tensor(p + "dtbi", [32, 1], F32))
                PG0 = s1e.enter_context(
                    nc.psum_tensor(p + "pg0", [128, 512], F32))
                PG1 = s1e.enter_context(
                    nc.psum_tensor(p + "pg1", [128, 512], F32))
                PTX0 = s1e.enter_context(
                    nc.psum_tensor(p + "ptx0", [128, 128], BF16))
                PTX1 = s1e.enter_context(
                    nc.psum_tensor(p + "ptx1", [128, 128], BF16))
                PTF0 = s1e.enter_context(
                    nc.psum_tensor(p + "ptf0", [128, 32], F32))
                PTF1 = s1e.enter_context(
                    nc.psum_tensor(p + "ptf1", [128, 32], F32))
                PRW = s1e.enter_context(
                    nc.psum_tensor(p + "prw", [128, 512], F32))
                scA = contextlib.ExitStack()
                P1 = scA.enter_context(nc.psum_tensor(p + "p1", [32, 512], F32))
                scA.close()
                PLM = s1e.enter_context(
                    nc.psum_tensor(p + "plm", [16, 512], F32))
                WGs = [WG0, WG1]
                PG = [PG0, PG1]
                PTXs = [PTX0, PTX1]
                PTFs = [PTF0, PTF1]

                def g_s1(g):
                    if dd == "f":
                        c.need(f"{tag}wcev_b15",
                               f"{tag}stwc_b14", f"{tag}stwc_b15")
                    else:
                        c.need(f"{tag}revdone", f"{tag}f_c15_ynspill")
                    c.dma_batch(p + "ldcv", [
                        (CVWB[:], W[f"{tag}cvw_{dd}"]),
                        (CVBB[:], W[f"{tag}cvb_{dd}"]),
                        (CVW2B[:], W[f"{tag}cvw2b_{dd}"]),
                        (CVW2C[:], W[f"{tag}cvw2c_{dd}"]),
                        (CVB2B[:], W[f"{tag}cvb2b_{dd}"]),
                        (CVB2C[:], W[f"{tag}cvb2c_{dd}"]),
                        (DTBC[:], W[f"{tag}dtb_{dd}"]),
                        (ACOL[:], W[f"{tag}alog_{dd}"]),
                        (DPREP[0:1, 0:2048], W[f"{tag}dprow_{dd}"]),
                        (NWREP[0:1, 0:2048], W[f"{tag}nwrow_{dd}"]),
                    ])
                    ins = g.memset(ONES32[:], 1.0)
                    c.done(p + "o32", ins, "gg")
                    for zg in range(8):
                        if zg >= 2:
                            c.need(p + f"pgev{zg * 16 - 17}")
                        c.dma_batch(
                            p + f"ldz{zg}",
                            [(WGs[zg % 2][:, k * 256:(k + 1) * 256],
                              W[f"{tag}inT_{dd}"][k * 128:(k + 1) * 128,
                                                  zg * 256:(zg + 1) * 256])
                             for k in range(KB)], sem=f"w{zg % 2}")
                    for xg in range(8):
                        bi = 8 + xg
                        if xg == 0:
                            c.need(p + "pgev111")
                        elif xg == 1:
                            c.need(p + "pgev127")
                        else:
                            c.need(p + f"pgev{128 + (xg - 2) * 8 + 7}")
                        c.dma_batch(
                            p + f"ldxs{xg}",
                            [(WGs[bi % 2][:, k * 256:(k + 1) * 256],
                              W[f"{tag}inT_{dd}"][k * 128:(k + 1) * 128,
                                                  2048 + xg * 256:
                                                  2048 + (xg + 1) * 256])
                             for k in range(KB)], sem=f"w{bi % 2}")
                    c.need(p + "pgev183")
                    c.dma_batch(
                        p + "ldbcdt",
                        [(WG0[:, k * 256:k * 256 + 64],
                          W[f"{tag}inT_{dd}"][k * 128:(k + 1) * 128,
                                              4096:4160])
                         for k in range(KB)], sem="w0")
                    c.need(p + "stle")
                    c.dma_batch(p + "ldlam",
                                [(LAMC[0:1, 0:512],
                                  LEDm.rearrange("a b -> (a b)").unsqueeze(0))])
                c.body("gpsimd", g_s1)

                def sy_s1(sy):
                    for zg in range(8):
                        for tc in range(16):
                            zi = zg * 16 + tc
                            c.need(p + f"pgev{zi}")
                            c.dma_batch(
                                p + f"stz{zi}",
                                [(ZTDm[tc * 128:(tc + 1) * 128,
                                       zg * 256:(zg + 1) * 256],
                                  ZTB[:, (zi % 2) * 256:(zi % 2 + 1) * 256])],
                                sem=f"y{zi % 2}")
                    for dm in range(16):
                        for ch in range(NCH):
                            xi = dm * NCH + ch
                            c.need(p + f"xte{dm}_{ch}")
                            c.dma_batch(
                                p + f"stxt{dm}_{ch}",
                                [(XSDm[ch * 128:(ch + 1) * 128,
                                       dm * 128:(dm + 1) * 128],
                                  XTT[:, (xi % 2) * 128:(xi % 2 + 1) * 128])],
                                sem=f"y{2 + xi % 2}")
                    c.need(p + "shdone")
                    c.dma_batch(p + "stsh", [(SHDm, SH[:])])
                    c.need(p + "lend")
                    c.dma_batch(p + "stle", [(LEDm, SRO[:, 16:32])])
                c.body("sync", sy_s1)

                def emit_xtp(t, dm):
                    c.need(p + f"sil{dm}")
                    for ch in range(NCH):
                        xi = dm * NCH + ch
                        if xi >= 2:
                            pi2 = xi - 2
                            c.need(p + f"xte{pi2 // NCH}_{pi2 % NCH}")
                        ins = t.transpose(
                            PTXs[xi % 2][:],
                            XSB[:, (dm % 2) * T + ch * 128:
                                (dm % 2) * T + ch * 128 + 128],
                            EYE[:])
                        c.done(p + f"xtp{dm}_{ch}", ins, "tt")

                def t_s1(t):
                    if dd == "f":
                        c.need(f"{tag}ln_xn", f"{tag}wcev_b15")
                    else:
                        c.need(f"{tag}revdone")
                    gi = 0
                    for zg in range(8):
                        c.need(p + f"ldz{zg}")
                        for tc in range(16):
                            if gi >= 2:
                                c.need(p + f"pgev{gi - 2}")
                            ins = mm_accum(
                                c, t, PG[gi % 2][:, 0:256],
                                [XN[:, k * T + tc * 128:
                                    k * T + tc * 128 + 128]
                                 for k in range(KB)],
                                [WGs[zg % 2][:, k * 256:(k + 1) * 256]
                                 for k in range(KB)])
                            c.done(p + f"pgr{gi}", ins, "tt")
                            gi += 1
                    for xg in range(8):
                        bi = 8 + xg
                        c.need(p + f"ldxs{xg}")
                        for m2 in range(2):
                            dm = xg * 2 + m2
                            for tb in range(TB):
                                if gi >= 2:
                                    c.need(p + f"pgev{gi - 2}")
                                ins = mm_accum(
                                    c, t, PG[gi % 2][:],
                                    [WGs[bi % 2][:, k * 256 + m2 * 128:
                                        k * 256 + m2 * 128 + 128]
                                     for k in range(KB)],
                                    [XN[:, k * T + tb * 512:
                                        k * T + tb * 512 + 512]
                                     for k in range(KB)])
                                c.done(p + f"pgr{gi}", ins, "tt")
                                gi += 1
                            if dm >= 2:
                                emit_xtp(t, dm - 2)
                    emit_xtp(t, 14)
                    emit_xtp(t, 15)
                    c.need(p + "ldbcdt")
                    for ph, (c0, c1, lbl) in enumerate(
                            ((0, 16, "B"), (16, 32, "C"), (32, 64, "D"))):
                        for tb in range(TB):
                            if tb >= 1:
                                c.need(p + f"bdev{lbl}{tb - 1}")
                            elif ph >= 1:
                                plbl = ("B", "C", "D")[ph - 1]
                                c.need(p + f"bdev{plbl}{TB - 1}")
                            ins = mm_accum(
                                c, t, P1[0:c1 - c0, :],
                                [WG0[:, k * 256 + c0:k * 256 + c1]
                                 for k in range(KB)],
                                [XN[:, k * T + tb * 512:
                                    k * T + tb * 512 + 512]
                                 for k in range(KB)])
                            c.done(p + f"bdmm{lbl}{tb}", ins, "tt")
                    for si, srcname in enumerate(("SH", "EXPS", "EXPW", "DTH")):
                        src = {"SH": SH, "EXPS": EXPS, "EXPW": EXPW,
                               "DTH": DTH}[srcname]
                        ev = {"SH": "shdone", "EXPS": "expsdone",
                              "EXPW": "expwdone", "DTH": "dthdone"}[srcname]
                        c.need(p + ev)
                        for ch in range(NCH):
                            hi = si * NCH + ch
                            if hi >= 2:
                                pi2 = hi - 2
                                c.need(p + f"hte{pi2 // NCH}_{pi2 % NCH}")
                            ins = t.transpose(
                                PTFs[hi % 2][:],
                                src[:, ch * 128:(ch + 1) * 128],
                                EYEF[0:32, 0:32])
                            c.done(p + f"htp{si}_{ch}", ins, "tt")
                    c.need(p + "ldcv", f"{tag}ones")
                    for nb in range(4):
                        if nb >= 1:
                            c.need(p + f"dpev{nb - 1}")
                        ins = t.matmul(PRW[:], ONESRB[:],
                                       DPREP[0:1, nb * 512:(nb + 1) * 512],
                                       start=True, stop=True)
                        c.done(p + f"dpbc{nb}", ins, "tt")
                    for nb in range(4):
                        c.need(p + (f"nwev{nb - 1}" if nb >= 1 else "dpev3"))
                        ins = t.matmul(PRW[:], ONESRB[:],
                                       NWREP[0:1, nb * 512:(nb + 1) * 512],
                                       start=True, stop=True)
                        c.done(p + f"nwbc{nb}", ins, "tt")
                    c.need(p + "ldlam")
                    ins = t.matmul(PLM[:], ONESRF[0:1, 0:16],
                                   LAMC[0:1, 0:512], start=True, stop=True)
                    c.done(p + "lambc", ins, "tt")
                c.body("tensor", t_s1)

                def v_xte(v, dm):
                    for ch in range(NCH):
                        xi = dm * NCH + ch
                        c.need(p + f"xtp{dm}_{ch}")
                        if xi >= 2:
                            pi2 = xi - 2
                            c.need(p + f"stxt{pi2 // NCH}_{pi2 % NCH}")
                        ins = v.tensor_copy(
                            XTT[:, (xi % 2) * 128:(xi % 2 + 1) * 128],
                            PTXs[xi % 2][:])
                        c.done(p + f"xte{dm}_{ch}", ins, "vv")

                def v_s1(v):
                    c.need(p + "ldcv")
                    gi = 0
                    for zg in range(8):
                        for tc in range(16):
                            zi = zg * 16 + tc
                            c.need(p + f"pgr{gi}")
                            if zi >= 2:
                                c.need(p + f"stz{zi - 2}")
                            ins = v.tensor_copy(
                                ZTB[:, (zi % 2) * 256:(zi % 2 + 1) * 256],
                                PG[gi % 2][:, 0:256])
                            c.done(p + f"pgev{gi}", ins, "vv")
                            gi += 1
                    for xg in range(8):
                        for m2 in range(2):
                            dm = xg * 2 + m2
                            base = (dm % 2) * (8 + T)
                            for tb in range(TB):
                                c.need(p + f"pgr{gi}")
                                ins = v.tensor_copy(
                                    XIN[:, base + 8 + tb * 512:
                                        base + 8 + tb * 512 + 512],
                                    PG[gi % 2][:])
                                c.done(p + f"pgev{gi}", ins, "vv")
                                gi += 1
                            if dm >= 2:
                                v_xte(v, dm - 2)
                            if dm >= 1:
                                c.need(p + f"sil{dm - 1}")
                            if dm < 2:
                                v.memset(XIN[:, base + 5:base + 8], 0.0)
                            v.tensor_scalar(
                                XCF[:], XIN[:, base + 5:base + 5 + T],
                                CVWB[:, dm * 4:dm * 4 + 1],
                                CVBB[:, dm:dm + 1], ALU.mult, ALU.add)
                            ins = None
                            for j in range(1, 4):
                                ins = v.scalar_tensor_tensor(
                                    XCF[:],
                                    XIN[:, base + 5 + j:base + 5 + j + T],
                                    CVWB[:, dm * 4 + j:dm * 4 + j + 1],
                                    XCF[:], ALU.mult, ALU.add)
                            c.done(p + f"cv{dm}", ins, "vv")
                    v_xte(v, 14)
                    v_xte(v, 15)
                    for ph, (c0, c1, lbl) in enumerate(
                            ((0, 16, "B"), (16, 32, "C"), (32, 64, "D"))):
                        for tb in range(TB):
                            c.need(p + f"bdmm{lbl}{tb}")
                            if lbl == "B":
                                dst = BCRB[:, 8 + tb * 512:8 + tb * 512 + 512]
                            elif lbl == "C":
                                dst = BCRC[:, 8 + tb * 512:8 + tb * 512 + 512]
                            else:
                                dst = DTR[:, tb * 512:(tb + 1) * 512]
                            ins = v.tensor_copy(dst, P1[0:c1 - c0, :])
                            c.done(p + f"bdev{lbl}{tb}", ins, "vv")
                    c.need(p + "sil15")
                    v.memset(BCRB[:, 5:8], 0.0)
                    v.memset(BCRC[:, 5:8], 0.0)
                    v.tensor_scalar(XCF[0:16, 0:T], BCRB[:, 5:5 + T],
                                    CVW2B[:, 0:1], CVB2B[:, 0:1],
                                    ALU.mult, ALU.add)
                    ins = None
                    for j in range(1, 4):
                        ins = v.scalar_tensor_tensor(
                            XCF[0:16, 0:T], BCRB[:, 5 + j:5 + j + T],
                            CVW2B[:, j:j + 1], XCF[0:16, 0:T],
                            ALU.mult, ALU.add)
                    c.done(p + "bccvB", ins, "vv")
                    c.need(p + "silB")
                    v.tensor_scalar(XCF[0:16, 0:T], BCRC[:, 5:5 + T],
                                    CVW2C[:, 0:1], CVB2C[:, 0:1],
                                    ALU.mult, ALU.add)
                    for j in range(1, 4):
                        ins = v.scalar_tensor_tensor(
                            XCF[0:16, 0:T], BCRC[:, 5 + j:5 + j + T],
                            CVW2C[:, j:j + 1], XCF[0:16, 0:T],
                            ALU.mult, ALU.add)
                    c.done(p + "bccvC", ins, "vv")
                    v.tensor_scalar(DTR[:], DTR[:], DTBC[:, 0:1], None,
                                    ALU.add)
                    v.tensor_scalar_mul(DTH[:], DTR[:], -1.0)
                    ins = v.tensor_tensor(DTH[:], DTH[:], DTR[:], ALU.max)
                    c.done(p + "dtabs", ins, "vv")
                    c.need(p + "dtln")
                    ins = v.scalar_tensor_tensor(DTH[:], DTR[:], 0.0, DTH[:],
                                                 ALU.max, ALU.add)
                    c.done(p + "dthdone", ins, "vv")
                    c.need(p + "aexp")
                    v.tensor_scalar_mul(ACOL[:], ACOL[:], -1.0)
                    ins = v.tensor_scalar(DTR[:], DTH[:], ACOL[:, 0:1], None,
                                          ALU.mult)
                    c.done(p + "dta", ins, "vv")
                    c.need(p + "o32")
                    ins = None
                    for ch in range(NCH):
                        ins = v.tensor_tensor_scan(
                            SH[:, ch * 128:(ch + 1) * 128], ONES32[:],
                            DTR[:, ch * 128:(ch + 1) * 128], 0.0,
                            ALU.mult, ALU.add)
                    c.done(p + "shdone", ins, "vv")
                    sh3 = SH.ap().rearrange("p (ch t) -> p ch t",
                                            ch=NCH, t=Q)
                    ins = v.tensor_copy(SRO[:, 0:16],
                                        sh3[:, :, Q - 1:Q].squeeze(2))
                    c.done(p + "send", ins, "vv")
                    send_b = SRO.ap()[:, 0:16].unsqueeze(2).broadcast_to(
                        [32, 16, Q])
                    ew3 = EXPW.ap().rearrange("p (ch t) -> p ch t",
                                              ch=NCH, t=Q)
                    ins = v.tensor_tensor(ew3, send_b, sh3, ALU.subtract)
                    c.done(p + "ewsub", ins, "vv")
                    c.need(p + "expsdone")
                    ex3 = EXPS.ap().rearrange("p (ch t) -> p ch t",
                                              ch=NCH, t=Q)
                    ins = v.tensor_copy(SRO[:, 16:32],
                                        ex3[:, :, Q - 1:Q].squeeze(2))
                    c.done(p + "lend", ins, "vv")
                    c.need(p + "ewexp")
                    ins = v.tensor_tensor(EXPW[:], EXPW[:], DTH[:], ALU.mult)
                    c.done(p + "expwdone", ins, "vv")
                    for si, dstt in enumerate((SHT, EXPST, EXPWT, DTT)):
                        for ch in range(NCH):
                            hi = si * NCH + ch
                            c.need(p + f"htp{si}_{ch}")
                            ins = v.tensor_copy(
                                dstt[:, ch * 32:(ch + 1) * 32],
                                PTFs[hi % 2][:])
                            c.done(p + f"hte{si}_{ch}", ins, "vv")
                    for nb in range(4):
                        c.need(p + f"dpbc{nb}")
                        ins = v.tensor_copy(
                            DPREP[:, nb * 512:(nb + 1) * 512], PRW[:])
                        c.done(p + f"dpev{nb}", ins, "vv")
                    for nb in range(4):
                        c.need(p + f"nwbc{nb}")
                        ins = v.tensor_copy(
                            NWREP[:, nb * 512:(nb + 1) * 512], PRW[:])
                        c.done(p + f"nwev{nb}", ins, "vv")
                    c.need(p + "lambc")
                    ins = v.tensor_copy(LAMC[:], PLM[:])
                    c.done(p + "lamev", ins, "vv")
                c.body("vector", v_s1)

                def s_s1(s):
                    for dm in range(16):
                        c.need(p + f"cv{dm}")
                        if dm >= 2:
                            c.need(p + f"xtp{dm - 2}_{NCH - 1}")
                        ins = s.activation(
                            XSB[:, (dm % 2) * T:(dm % 2 + 1) * T],
                            XCF[:], AF.Silu)
                        c.done(p + f"sil{dm}", ins, "ss")
                    c.need(p + "bccvB")
                    ins = s.activation(BSB[:], XCF[0:16, 0:T], AF.Silu)
                    c.done(p + "silB", ins, "ss")
                    c.need(p + "bccvC")
                    ins = s.activation(CSB[:], XCF[0:16, 0:T], AF.Silu)
                    c.done(p + "silC", ins, "ss")
                    c.need(p + "dtabs")
                    s.activation(DTH[:], DTH[:], AF.Exp, scale=-1.0)
                    ins = s.activation(DTH[:], DTH[:], AF.Ln, bias=1.0)
                    c.done(p + "dtln", ins, "ss")
                    c.need(p + "ldcv")
                    ins = s.activation(ACOL[:], ACOL[:], AF.Exp)
                    c.done(p + "aexp", ins, "ss")
                    c.need(p + "shdone")
                    ins = s.activation(EXPS[:], SH[:], AF.Exp)
                    c.done(p + "expsdone", ins, "ss")
                    c.need(p + "ewsub")
                    ins = s.activation(EXPW[:], EXPW[:], AF.Exp)
                    c.done(p + "ewexp", ins, "ss")
                c.body("scalar", s_s1)

            with contextlib.ExitStack() as s2e:
                XSW = s2e.enter_context(
                    nc.sbuf_tensor(p + "xsw", [128, 2048], BF16))
                ZTW = s2e.enter_context(
                    nc.sbuf_tensor(p + "ztw", [128, 2048], BF16))
                SHRC = s2e.enter_context(
                    nc.sbuf_tensor(p + "shrc", [1, 4096], F32))
                SCR8 = s2e.enter_context(
                    nc.sbuf_tensor(p + "scr8", [128, 2048], F32))
                MTALL = s2e.enter_context(
                    nc.sbuf_tensor(p + "mtall", [128, 4096], BF16))
                XDEL = s2e.enter_context(
                    nc.sbuf_tensor(p + "xdel", [128, 2048], BF16))
                XW = s2e.enter_context(
                    nc.sbuf_tensor(p + "xw", [128, 2048], BF16))
                Y1 = s2e.enter_context(
                    nc.sbuf_tensor(p + "y1", [128, 2048], F32))
                GMS = s2e.enter_context(
                    nc.sbuf_tensor(p + "gms", [128, 128], BF16))
                BTOK = s2e.enter_context(
                    nc.sbuf_tensor(p + "btok", [128, 16], BF16))
                SST = s2e.enter_context(
                    nc.sbuf_tensor(p + "sst", [16, 2048], F32))
                SB16 = s2e.enter_context(
                    nc.sbuf_tensor(p + "sb16", [16, 2048], BF16))
                RS = s2e.enter_context(nc.sbuf_tensor(p + "rs", [128, 4], F32))
                YN = s2e.enter_context(
                    nc.sbuf_tensor(p + "ynb", [128, 2048], BF16))
                YNT = s2e.enter_context(
                    nc.sbuf_tensor(p + "yntb", [128, 2048], BF16))
                PM4 = s2e.enter_context(
                    nc.psum_tensor(p + "pm4", [128, 2048], F32))
                PDS2 = s2e.enter_context(
                    nc.psum_tensor(p + "pds2", [16, 512], F32))
                PTB = s2e.enter_context(
                    nc.psum_tensor(p + "ptb", [128, 16], BF16))
                PTY0 = s2e.enter_context(
                    nc.psum_tensor(p + "pty0", [128, 128], BF16))
                PTY1 = s2e.enter_context(
                    nc.psum_tensor(p + "pty1", [128, 128], BF16))
                PTYs = [PTY0, PTY1]

                s1done = [p + "stxt15_14", p + "stxt15_15",
                          p + "stz126", p + "stz127",
                          p + "stsh", p + "lamev", p + "nwev3",
                          p + "hte3_15", p + "silC"]

                def g_s2(g):
                    for ch in range(NCH):
                        q = f"{p}c{ch}_"
                        if ch == 0:
                            c.need(*s1done)
                        else:
                            c.need(f"{p}c{ch - 1}_ynspill")
                        c.dma_batch(q + "ld", [
                            (XSW[:], XSDm[ch * 128:(ch + 1) * 128, :]),
                            (ZTW[:], ZTDm[ch * 128:(ch + 1) * 128, :]),
                            (SHRC[0:1, :].rearrange("p (a b) -> p a b",
                                                    a=32, b=128),
                             SHDm[:, ch * 128:(ch + 1) * 128].unsqueeze(0)),
                        ], sem=f"u{ch % 2}")
                c.body("gpsimd", g_s2)

                def sy_s2(sy):
                    for ch in range(NCH):
                        q = f"{p}c{ch}_"
                        cdst = (T - (ch + 1) * Q) if dd == "b" else ch * Q
                        c.need(q + "ynte15")
                        c.dma_batch(q + "ynspill",
                                    [(YND[dd][dm * 128:(dm + 1) * 128,
                                              cdst:cdst + Q],
                                      YNT[:, dm * 128:(dm + 1) * 128])
                                     for dm in range(16)],
                                    sem=f"y{2 + ch % 2}")
                c.body("sync", sy_s2)

                def t_s2(t):
                    for ch in range(NCH):
                        q = f"{p}c{ch}_"
                        c.need(q + "ld")
                        if ch == 0:
                            c.need(p + "silB", p + "silC")
                        else:
                            c.need(f"{p}c{ch - 1}_y2done")
                        ins = t.matmul(
                            PM4[:, 0:128],
                            BSB[:, ch * 128:(ch + 1) * 128],
                            CSB[:, ch * 128:(ch + 1) * 128],
                            start=True, stop=True)
                        c.done(q + "gmm", ins, "tt")
                        ins = t.transpose(
                            PTB[:], BSB[:, ch * 128:(ch + 1) * 128],
                            EYE[0:16, 0:16])
                        c.done(q + "btokt", ins, "tt")
                        for hh in range(2):
                            c.need(q + ("gmev" if hh == 0 else "m1_0"))
                            ins = None
                            for nb in range(4):
                                ins = t.matmul(
                                    PM4[:, nb * 512:(nb + 1) * 512],
                                    ONESRF[:],
                                    SHRC[0:1, hh * 2048 + nb * 512:
                                         hh * 2048 + (nb + 1) * 512],
                                    start=True, stop=True)
                            c.done(q + f"srow{hh}", ins, "tt")
                        c.need(q + "m1_1")
                        c.need((f"{p}c{ch - 1}_sb16" if ch > 0 else
                                p + "sb16i"))
                        ins = None
                        for nb in range(4):
                            ins = t.matmul(
                                PM4[:, nb * 512:(nb + 1) * 512],
                                CSB[:, ch * 128:(ch + 1) * 128],
                                SB16[:, nb * 512:(nb + 1) * 512],
                                start=True, stop=True)
                        c.done(q + "umm", ins, "tt")
                        c.need(q + "xwdone", q + "btokev")
                        for qb in range(4):
                            if qb >= 1:
                                c.need(q + f"supd{qb - 1}")
                            elif ch > 0:
                                c.need(f"{p}c{ch - 1}_supd3")
                            ins = t.matmul(
                                PDS2[:], BTOK[:],
                                XW[:, qb * 512:(qb + 1) * 512],
                                start=True, stop=True)
                            c.done(q + f"dsmm{qb}", ins, "tt")
                        c.need(q + "y1done", q + "xdeldone")
                        ins = None
                        for h in range(NH):
                            ins = t.matmul(
                                PM4[:, h * 64:(h + 1) * 64],
                                MTALL[:, h * 128:(h + 1) * 128],
                                XDEL[:, h * 64:(h + 1) * 64],
                                start=True, stop=True)
                        c.done(q + "yimm", ins, "tt")
                        c.need(q + "yn")
                        for dm in range(16):
                            if dm >= 2:
                                c.need(q + f"ynte{dm - 2}")
                            elif ch > 0:
                                c.need(f"{p}c{ch - 1}_ynte{14 + dm}")
                            ins = t.transpose(
                                PTYs[dm % 2][:],
                                YN[:, dm * 128:(dm + 1) * 128], EYE[:])
                            c.done(q + f"yntp{dm}", ins, "tt")
                c.body("tensor", t_s2)

                def v_s2(v):
                    v.memset(SST[:], 0.0)
                    ins = v.tensor_copy(SB16[:], SST[:])
                    c.done(p + "sb16i", ins, "vv")
                    for ch in range(NCH):
                        q = f"{p}c{ch}_"
                        c.need(q + "gmm")
                        ins = v.tensor_tensor(GMS[:], PM4[:, 0:128], MSK01[:],
                                              ALU.mult)
                        c.done(q + "gmev", ins, "vv")
                        c.need(q + "btokt")
                        ins = v.tensor_copy(BTOK[:], PTB[:])
                        c.done(q + "btokev", ins, "vv")
                        scr3 = SCR8.ap().rearrange("p (h t) -> p h t",
                                                   h=16, t=Q)
                        gms_b = GMS.ap().unsqueeze(1).broadcast_to(
                            [128, 16, Q])
                        for hh in range(2):
                            c.need(q + f"srow{hh}")
                            if hh == 0 and ch > 0:
                                c.need(f"{p}c{ch - 1}_sq")
                            sht_b = SHT.ap()[:, ch * 32 + hh * 16:
                                             ch * 32 + (hh + 1) * 16] \
                                .unsqueeze(2).broadcast_to([128, 16, Q])
                            pm3 = PM4[:].rearrange("p (h t) -> p h t",
                                                   h=16, t=Q)
                            v.tensor_tensor(scr3, pm3, sht_b, ALU.subtract)
                            ins = v.tensor_single_scalar(SCR8[:], SCR8[:],
                                                         0.0, ALU.min)
                            c.done(q + f"m1_{hh}", ins, "vv")
                            c.need(q + f"mexp{hh}")
                            ins = v.tensor_tensor(
                                MTALL[:, hh * 2048:(hh + 1) * 2048]
                                .rearrange("p (h t) -> p h t", h=16, t=Q),
                                scr3, gms_b, ALU.mult)
                            c.done(q + f"mth{hh}m", ins, "vv")
                        xsw3 = XSW.ap().rearrange("p (h e) -> p h e",
                                                  h=NH, e=HD)
                        dtt_b = DTT.ap()[:, ch * 32:(ch + 1) * 32] \
                            .unsqueeze(2).broadcast_to([128, NH, HD])
                        ins = v.tensor_tensor(
                            XDEL.ap().rearrange("p (h e) -> p h e",
                                                h=NH, e=HD),
                            xsw3, dtt_b, ALU.mult)
                        c.done(q + "xdeldone", ins, "vv")
                        exwt_b = EXPWT.ap()[:, ch * 32:(ch + 1) * 32] \
                            .unsqueeze(2).broadcast_to([128, NH, HD])
                        ins = v.tensor_tensor(
                            XW.ap().rearrange("p (h e) -> p h e",
                                              h=NH, e=HD),
                            xsw3, exwt_b, ALU.mult)
                        c.done(q + "xwdone", ins, "vv")
                        c.need(q + "umm")
                        exst_b = EXPST.ap()[:, ch * 32:(ch + 1) * 32] \
                            .unsqueeze(2).broadcast_to([128, NH, HD])
                        ins = v.tensor_tensor(
                            Y1.ap().rearrange("p (h e) -> p h e",
                                              h=NH, e=HD),
                            PM4[:].rearrange("p (h e) -> p h e",
                                             h=NH, e=HD),
                            exst_b, ALU.mult)
                        c.done(q + "y1done", ins, "vv")
                        lam_b = LAMC.ap().rearrange(
                            "p (h cc) -> p h cc", h=NH, cc=NCH)[
                            :, :, ch:ch + 1].broadcast_to([16, NH, HD])
                        ins = v.tensor_tensor(
                            SST.ap().rearrange("p (h e) -> p h e",
                                               h=NH, e=HD),
                            SST.ap().rearrange("p (h e) -> p h e",
                                               h=NH, e=HD),
                            lam_b, ALU.mult)
                        c.done(q + "lsc", ins, "vv")
                        for qb in range(4):
                            c.need(q + f"dsmm{qb}")
                            ins = v.tensor_tensor(
                                SST[:, qb * 512:(qb + 1) * 512],
                                SST[:, qb * 512:(qb + 1) * 512],
                                PDS2[:], ALU.add)
                            c.done(q + f"supd{qb}", ins, "vv")
                        ins = v.tensor_copy(SB16[:], SST[:])
                        c.done(q + "sb16", ins, "vv")
                        c.need(q + "yimm")
                        ins = v.tensor_tensor(Y1[:], Y1[:], PM4[:], ALU.add)
                        c.done(q + "yiadd", ins, "vv")
                        v.tensor_tensor(SCR8[:], XSW[:], DPREP[:], ALU.mult)
                        ins = v.tensor_tensor(Y1[:], Y1[:], SCR8[:], ALU.add)
                        c.done(q + "y2done", ins, "vv")
                        c.need(q + "zsil")
                        ins = v.tensor_tensor(Y1[:], Y1[:], ZTW[:], ALU.mult)
                        c.done(q + "ygate", ins, "vv")
                        c.need(q + "sq")
                        ins = v.tensor_scalar_mul(RS[:, 1:2], RS[:, 0:1],
                                                  1.0 / 2048.0)
                        c.done(q + "rsmean", ins, "vv")
                        c.need(q + "rstd")
                        v.tensor_scalar(Y1[:], Y1[:], RS[:, 2:3], None,
                                        ALU.mult)
                        ins = v.tensor_tensor(YN[:], Y1[:], NWREP[:],
                                              ALU.mult)
                        c.done(q + "yn", ins, "vv")
                        for dm in range(16):
                            c.need(q + f"yntp{dm}")
                            if ch > 0 and dm < 2:
                                c.need(f"{p}c{ch - 1}_ynspill")
                            src = PTYs[dm % 2][:]
                            if dd == "b":
                                src = src[:, ::-1]
                            ins = v.tensor_copy(
                                YNT[:, dm * 128:(dm + 1) * 128], src)
                            c.done(q + f"ynte{dm}", ins, "vv")
                c.body("vector", v_s2)

                def s_s2(s):
                    for ch in range(NCH):
                        q = f"{p}c{ch}_"
                        for hh in range(2):
                            c.need(q + f"m1_{hh}")
                            ins = s.activation(SCR8[:], SCR8[:], AF.Exp)
                            c.done(q + f"mexp{hh}", ins, "ss")
                        c.need(q + "ld")
                        if ch > 0:
                            c.need(f"{p}c{ch - 1}_ygate")
                        ins = s.activation(ZTW[:], ZTW[:], AF.Silu)
                        c.done(q + "zsil", ins, "ss")
                        c.need(q + "ygate")
                        ins = s.activation(SCR8[:], Y1[:], AF.Square,
                                           accum_out=RS[:, 0:1])
                        c.done(q + "sq", ins, "ss")
                        c.need(q + "rsmean")
                        s.activation(RS[:, 2:3], RS[:, 1:2], AF.Ln, bias=1e-5)
                        ins = s.activation(RS[:, 2:3], RS[:, 2:3], AF.Exp,
                                           scale=-0.5)
                        c.done(q + "rstd", ins, "ss")
                c.body("scalar", s_s2)

            st2.close()
            c.alias(f"{tag}{dd}_dirdone", f"{p}c{NCH - 1}_yn")

        for _di, _dd in enumerate("fb"):
            _dir(_di, _dd)

        with (
            nc.sbuf_tensor(f"{tag}ynt2", [128, 32 * 512], BF16) as YNTB,
            nc.sbuf_tensor(f"{tag}wcb0", [128, 32 * 128], BF16) as WCB0,
            nc.sbuf_tensor(f"{tag}wcb1", [128, 32 * 128], BF16) as WCB1,
            nc.psum_tensor(f"{tag}pf0", [128, 512], F32) as PF0,
            nc.psum_tensor(f"{tag}pf1", [128, 512], F32) as PF1,
        ):
            WCB = [WCB0, WCB1]
            PF = [PF0, PF1]

            def g_fin(g):
                for tb in range(TB):
                    if tb == 0:
                        c.need(f"{tag}b_c15_ynspill", f"{tag}b_c14_ynspill")
                    else:
                        c.need(f"{tag}fmm{tb - 1}_7")
                    c.dma_batch(
                        f"{tag}ldyn{tb}",
                        [(YNTB[:, (di2 * 16 + e) * 512:
                               (di2 * 16 + e + 1) * 512],
                          YND[dd2][e * 128:(e + 1) * 128,
                                   tb * 512:(tb + 1) * 512])
                         for di2, dd2 in enumerate("fb") for e in range(16)],
                        sem="u0")
                    for m in range(KB):
                        bi = tb * KB + m
                        if bi >= 2:
                            c.need(f"{tag}fev{bi - 2}")
                        c.dma_batch(
                            f"{tag}ldwc{tb}_{m}",
                            [(WCB[bi % 2][:, (di2 * 16 + e) * 128:
                                          (di2 * 16 + e + 1) * 128],
                              WCD[dd2][e * 128:(e + 1) * 128,
                                       m * 128:(m + 1) * 128])
                             for di2, dd2 in enumerate("fb")
                             for e in range(16)],
                            sem=f"w{bi % 2}")
            c.body("gpsimd", g_fin)

            def t_fin(t):
                for tb in range(TB):
                    c.need(f"{tag}ldyn{tb}")
                    for m in range(KB):
                        bi = tb * KB + m
                        c.need(f"{tag}ldwc{tb}_{m}")
                        if bi >= 2:
                            c.need(f"{tag}fev{bi - 2}")
                        ins = None
                        for kk in range(32):
                            ins = t.matmul(
                                PF[bi % 2][:],
                                WCB[bi % 2][:, kk * 128:(kk + 1) * 128],
                                YNTB[:, kk * 512:(kk + 1) * 512],
                                start=(kk == 0), stop=(kk == 31))
                        c.done(f"{tag}fmm{tb}_{m}", ins, "tt")
            c.body("tensor", t_fin)

            def v_fin(v):
                for tb in range(TB):
                    for m in range(KB):
                        bi = tb * KB + m
                        c.need(f"{tag}fmm{tb}_{m}")
                        sl = slice(m * T + tb * 512, m * T + tb * 512 + 512)
                        ins = v.scalar_tensor_tensor(
                            X[:, sl], PF[bi % 2][:], LSC[:, m:m + 1],
                            X[:, sl], ALU.mult, ALU.add)
                        c.done(f"{tag}fev{bi}", ins, "vv")
                c.alias(f"{tag}res", f"{tag}fev{TB * KB - 1}")
            c.body("vector", v_fin)

        def st_body(sy):
            c.need(f"{tag}res")
            c.dma_batch(f"{tag}sty",
                        [(W["yT"][k * 128:(k + 1) * 128, :],
                          X[:, k * T:(k + 1) * T]) for k in range(KB)])
            sy.wait_ge(c.sems["qs"].h, c.sems["qs"].n)
        c.body("sync", st_body)


def _prep_s5_inputs(ins, tag):
    import ml_dtypes
    out = {}
    for dd, nm in (("f", "fwd"), ("b", "bwd")):
        out[f"{tag}inT_{dd}"] = np.ascontiguousarray(
            ins[f"L1_attn_a_{nm}_in_proj_w"].T)
        cw = ins[f"L1_attn_a_{nm}_conv_w"].reshape(2080, 4)
        out[f"{tag}cvw_{dd}"] = np.ascontiguousarray(
            cw[:2048].reshape(16, 128, 4).transpose(1, 0, 2).reshape(128, 64))
        out[f"{tag}cvw2b_{dd}"] = np.ascontiguousarray(cw[2048:2064])
        out[f"{tag}cvw2c_{dd}"] = np.ascontiguousarray(cw[2064:2080])
        cb = ins[f"L1_attn_a_{nm}_conv_b"]
        out[f"{tag}cvb_{dd}"] = np.ascontiguousarray(
            cb[:2048].reshape(16, 128).T)
        out[f"{tag}cvb2b_{dd}"] = np.ascontiguousarray(
            cb[2048:2064].reshape(16, 1))
        out[f"{tag}cvb2c_{dd}"] = np.ascontiguousarray(
            cb[2064:2080].reshape(16, 1))
        out[f"{tag}dtb_{dd}"] = np.ascontiguousarray(
            ins[f"L1_attn_a_{nm}_dt_bias"].reshape(32, 1))
        out[f"{tag}alog_{dd}"] = np.ascontiguousarray(
            ins[f"L1_attn_a_{nm}_A_log"].reshape(32, 1))
        out[f"{tag}dprow_{dd}"] = np.ascontiguousarray(
            np.repeat(ins[f"L1_attn_a_{nm}_Dp"], 64).reshape(1, 2048))
        out[f"{tag}nwrow_{dd}"] = np.ascontiguousarray(
            ins[f"L1_attn_a_{nm}_norm_w"].reshape(1, 2048))
        out[f"{tag}op_{dd}"] = np.ascontiguousarray(
            ins[f"L1_attn_a_{nm}_out_proj_w"])
    out[f"{tag}loT"] = np.ascontiguousarray(
        ins["L1_attn_a_layer_out_proj_w"].T)
    out[f"{tag}lsc"] = np.ascontiguousarray(
        ins["L1_attn_a_layer_scale"].reshape(8, 128).T)
    out[f"{tag}eyef"] = np.eye(128, dtype=np.float32)
    m01 = np.zeros((128, 128), np.float32)
    for sg in range(128):
        m01[sg, sg:] = 1.0
    out[f"{tag}msk01"] = m01.astype(ml_dtypes.bfloat16)
    return out


def s5_decls(tag):
    d = [("xT", [D, T], F32, "ExternalInput"),
         ("eye", [128, 128], BF16, "ExternalInput"),
         (f"{tag}eyef", [128, 128], F32, "ExternalInput"),
         (f"{tag}msk01", [128, 128], BF16, "ExternalInput"),
         (f"{tag}lsc", [128, 8], F32, "ExternalInput"),
         (f"{tag}loT", [2048, 1024], F32, "ExternalInput"),
         ("yT", [D, T], F32, "ExternalOutput")]
    for dd in "fb":
        d += [(f"{tag}inT_{dd}", [D, 4160], F32, "ExternalInput"),
              (f"{tag}cvw_{dd}", [128, 64], F32, "ExternalInput"),
              (f"{tag}cvw2b_{dd}", [16, 4], F32, "ExternalInput"),
              (f"{tag}cvw2c_{dd}", [16, 4], F32, "ExternalInput"),
              (f"{tag}cvb_{dd}", [128, 16], F32, "ExternalInput"),
              (f"{tag}cvb2b_{dd}", [16, 1], F32, "ExternalInput"),
              (f"{tag}cvb2c_{dd}", [16, 1], F32, "ExternalInput"),
              (f"{tag}dtb_{dd}", [32, 1], F32, "ExternalInput"),
              (f"{tag}alog_{dd}", [32, 1], F32, "ExternalInput"),
              (f"{tag}dprow_{dd}", [1, 2048], F32, "ExternalInput"),
              (f"{tag}nwrow_{dd}", [1, 2048], F32, "ExternalInput"),
              (f"{tag}op_{dd}", [1024, 2048], F32, "ExternalInput")]
    return d


def run_s5(x, ins):
    import ml_dtypes
    nc = build_single(lambda c, W: emit_s5(c, W, "s5_"), s5_decls("s5_"))
    shared = _prep_s5_inputs(ins, "s5_")
    shared["eye"] = np.eye(128, dtype=np.float32).astype(ml_dtypes.bfloat16)
    in_maps = []
    for b in range(B):
        m = dict(shared)
        m["xT"] = np.ascontiguousarray(x[b].T)
        in_maps.append(m)
    res = run_bass_kernel_spmd(nc, in_maps, core_ids=list(range(NCORES)))
    return np.stack([np.ascontiguousarray(r["yT"].T) for r in res.results])



def chain_decls():
    d = [("xT", [D, T], F32, "ExternalInput"),
         ("yT", [D, T], F32, "ExternalOutput"),
         ("eye", [128, 128], BF16, "ExternalInput")]
    seen = {"xT", "yT", "eye"}
    for lst in (S1_DECLS, S2_DECLS, s3_decls("s3_"), s5_decls("s5_"),
                s6_decls("s6_"), s7_decls("s7_")):
        for ent in lst:
            if ent[0] in seen or ent[3] == "ExternalOutput":
                continue
            seen.add(ent[0])
            d.append(ent)
    return d


def emit_chain(c, W, nchain=None):
    import os
    nc = c.nc
    if nchain is None:
        nchain = int(os.environ.get("NCHAIN", "8"))
    tmids = [nc.dram_tensor(f"chx{i}", [D, T], F32, kind="Internal").ap()
             for i in range(7)]
    bufs = [W["xT"]] + tmids + [W["yT"]]

    def wv(i, names, tag_from=None, tag_to=None):
        d = {}
        for nm in names:
            if nm in ("xT", "yT", "eye"):
                continue
            if tag_from and tag_to and nm.startswith(tag_from):
                d[tag_to + nm[len(tag_from):]] = W[nm]
            else:
                d[nm] = W[nm]
        d["xT"] = bufs[i]
        d["yT"] = W["yT"] if i == nchain - 1 else bufs[i + 1]
        d["eye"] = W["eye"]
        return d

    s1n = [e[0] for e in S1_DECLS]
    s2n = [e[0] for e in S2_DECLS]
    s3n = [e[0] for e in s3_decls("s3_")]
    s5n = [e[0] for e in s5_decls("s5_")]
    s6n = [e[0] for e in s6_decls("s6_")]
    s7n = [e[0] for e in s7_decls("s7_")]
    steps = [
        lambda: emit_s1(c, wv(0, s1n)),
        lambda: emit_s2(c, wv(1, s2n, "s2_", "s2a_"), "s2a_"),
        lambda: emit_s3(c, wv(2, s3n), "s3_"),
        lambda: emit_s2(c, wv(3, s2n, "s2_", "s2b_"), "s2b_"),
        lambda: emit_s5(c, wv(4, s5n), "s5_"),
        lambda: emit_s6(c, wv(5, s6n, "s6_", "s6a_"), "s6a_"),
        lambda: emit_s7(c, wv(6, s7n), "s7_"),
        lambda: emit_s6(c, wv(7, s6n, "s6_", "s6b_"), "s6b_"),
    ]
    gates = [None, "sty", "s2a_sty", "s3_sty", "s2b_sty", "s5_sty",
             "s6a_sty", "s7_sty"]
    for i in range(nchain):
        c.chain_gate = gates[i]
        steps[i]()
    c.chain_gate = None


def chain_inputs(inputs):
    import ml_dtypes
    shared = {}
    shared.update(_prep_s1_inputs(inputs))
    shared.update(_prep_s2_inputs(inputs))
    shared.update(_prep_s3_inputs(inputs, "s3_"))
    shared.update(_prep_s5_inputs(inputs, "s5_"))
    shared.update(_prep_s6_inputs(inputs, "s6_"))
    shared.update(_prep_s7_inputs(inputs, "s7_"))
    shared["eye"] = np.eye(128, dtype=np.float32).astype(ml_dtypes.bfloat16)
    return shared


def kernel(**inputs):
    x = np.asarray(inputs["L0_attn_a_t"])
    nc = build_single(emit_chain, chain_decls())
    shared = chain_inputs(inputs)
    in_maps = []
    for b in range(B):
        m = dict(shared)
        m["xT"] = np.ascontiguousarray(x[b].T)
        in_maps.append(m)
    res = run_bass_kernel_spmd(nc, in_maps, core_ids=list(range(NCORES)))
    return np.stack([np.ascontiguousarray(r["yT"].T) for r in res.results])


# version 06
# speedup vs baseline: 1.3042x; 1.0189x over previous
import contextlib
import numpy as np
import concourse.bass as bass
import concourse.mybir as mybir
from concourse.bass_utils import run_bass_kernel_spmd

F32 = mybir.dt.float32
BF16 = mybir.dt.bfloat16
U32 = mybir.dt.uint32
AF = mybir.ActivationFunctionType
ALU = mybir.AluOpType
AX = mybir.AxisListType

B, T, D = 8, 2048, 1024
KB = D // 128
TB = T // 512
NCORES = 8


class Sem:
    __slots__ = ("h", "n")
    def __init__(self, h):
        self.h = h
        self.n = 0


class _MockInstr:
    def then_inc(self, *a, **k):
        return self

    def annotate(self, *a, **k):
        return self


class _MockEngine:
    def __getattr__(self, name):
        def f(*a, **k):
            return _MockInstr()
        return f


class ChainedInstr:
    __slots__ = ("ins", "sem", "val")

    def __init__(self, ins, sem, val):
        self.ins = ins
        self.sem = sem
        self.val = val

    def then_inc(self, *a, **k):
        raise AssertionError("auto-chained instruction already carries its inc")


_SEQ_OPS = {"wait_ge", "nop", "dma_start", "dma_start_transpose", "sem_inc",
            "drain", "indirect_dma_start", "halt"}


class ChainedEngine:

    def __init__(self, c, eng, semname):
        self._c = c
        self._eng = eng
        self._semname = semname

    def __getattr__(self, name):
        real = getattr(self._eng, name)
        if name in _SEQ_OPS or name.startswith("reg") or name.startswith("_"):
            return real
        c = self._c
        semname = self._semname

        def f(*a, **k):
            s = c.sems[semname]
            if s.n > 0:
                self._eng.wait_ge(s.h, s.n)
            ins = real(*a, **k)
            s.n += 1
            ins.then_inc(s.h, 1)
            key = (semname, semname)
            c.marks[(c.ename, semname)] = s.n
            return ChainedInstr(ins, semname, s.n)
        return f


class Ctx:

    def __init__(self, nc):
        self.nc = nc
        self.sems = {}
        self.events = {}
        self.marks = {}
        self.queue = []
        self.dry = True
        self.pending = set()
        self.eng = None
        self.ename = None
        self.const_pool = None
        self.const_map = {}

    NCONST = 64

    def const(self, val):
        val = float(val)
        if val in self.const_map:
            return
        idx = len(self.const_map)
        assert idx < self.NCONST
        self.const_map[val] = idx
        self.nc.const_aps.register(F32, val,
                                   self.const_pool[:, idx:idx + 1])

    def add_sem(self, name, handle):
        self.sems[name] = Sem(handle)

    def body(self, which, fn):
        self.queue.append((which, fn))

    def done(self, label, instr, semname=None, k=1):
        if isinstance(instr, ChainedInstr):
            ev = (instr.sem, instr.val)
            if self.dry:
                self.events[label] = ev
            else:
                assert self.events.get(label) == ev, (
                    f"nondeterministic emission at {label}: "
                    f"{self.events.get(label)} vs {ev}")
            return instr.val
        s = self.sems[semname]
        s.n += k
        instr.then_inc(s.h, k)
        if self.dry:
            self.events[label] = (semname, s.n)
        else:
            assert self.events.get(label) == (semname, s.n), (
                f"nondeterministic emission at {label}: "
                f"{self.events.get(label)} vs {(semname, s.n)}")
        if semname not in ("qg", "qs", "qa"):
            key = (self.ename, semname)
            self.marks[key] = max(self.marks.get(key, 0), s.n)
        return s.n

    def alias(self, label, other):
        if self.dry:
            if other in self.events:
                self.events[label] = self.events[other]
            else:
                self.pending.add(other)
                self.events[label] = ("__alias__", other)
        else:
            pass

    def need(self, *labels):
        if self.dry:
            for lb in labels:
                if lb not in self.events:
                    self.pending.add(lb)
            return
        for lb in labels:
            semname, v = self.events[lb]
            while semname == "__alias__":
                semname, v = self.events[v]
            key = (self.ename, semname)
            if self.marks.get(key, 0) < v:
                self.eng.wait_ge(self.sems[semname].h, v)
                self.marks[key] = v

    _QSEM = {"gpsimd": "qg", "sync": "qs", "scalar": "qa"}

    def dma_batch(self, label, pairs, sem=None):
        if sem is None:
            sem = self._QSEM[self.ename]
        ins = None
        for out, in_ in pairs:
            ins = self.eng.dma_start(out=out, in_=in_)
            s = self.sems[sem]
            s.n += 16
            ins.then_inc(s.h, 16)
        self.events[label] = (sem, self.sems[sem].n) if self.dry else \
            self.events[label]
        if not self.dry:
            assert self.events[label] == (sem, self.sems[sem].n)
        return self.events[label]

    def run(self, blk):
        if self.const_map:
            cm = dict(self.const_map)

            def cinit(g):
                ins = None
                for val, idx in cm.items():
                    ins = g.memset(self.const_pool[:, idx:idx + 1], val)
                c.done("consts", ins, "gg")
            c = self
            engines_used = {w for w, _ in self.queue} - {"gpsimd"}

            def gate(e):
                c.need("consts")
            self.queue = ([("gpsimd", cinit)]
                          + [(w, gate) for w in sorted(engines_used)]
                          + self.queue)
        _CHAINSEM = {"vector": "vv", "scalar": "ss", "gpsimd": "gg"}

        def wrap_eng(eng, which):
            if which in _CHAINSEM:
                return ChainedEngine(self, eng, _CHAINSEM[which])
            return eng
        self.dry = True
        mock = _MockEngine()
        for which, fn in self.queue:
            self.eng, self.ename = mock, which
            fn(wrap_eng(mock, which))
        missing = self.pending - set(self.events)
        assert not missing, f"events never produced: {sorted(missing)[:10]}"
        self.dry = False
        self.marks = {}
        for s in self.sems.values():
            s.n = 0
        for which, fn in self.queue:
            def wrap(eng, fn=fn, which=which):
                self.eng, self.ename = eng, which
                fn(wrap_eng(eng, which))
            getattr(blk, which)(wrap)


def mm_accum(c, t, psum_ap, lhsT_list, rhs_list):
    n = len(lhsT_list)
    ins = None
    for i, (lh, rh) in enumerate(zip(lhsT_list, rhs_list)):
        ins = t.matmul(psum_ap, lh, rh, start=(i == 0), stop=(i == n - 1))
    return ins


def emit_ln2(c, X, XN, tag, kblocks=KB):
    nc = c.nc
    c.const(1e-5)
    with (
        nc.sbuf_tensor(f"{tag}_sq0", [128, T], F32) as SQ0,
        nc.sbuf_tensor(f"{tag}_sq1", [128, T], F32) as SQ1,
        nc.sbuf_tensor(f"{tag}_rows", [1, 4 * T], F32) as ROWS,
        nc.sbuf_tensor(f"{tag}_oc", [128, 1], F32) as ONESC,
        nc.sbuf_tensor(f"{tag}_or", [1, 128], F32) as ONESR,
        nc.psum_tensor(f"{tag}_pst", [1, T], F32) as PS_STAT,
        nc.psum_tensor(f"{tag}_psa", [128, T], F32) as PS_A,
    ):
        xsq = [SQ0, SQ1]

        def g_body(g):
            g.memset(ONESC[:], 1.0)
            ins = g.memset(ONESR[:], 1.0)
            c.done(f"{tag}ones", ins, "gg")
        c.body("gpsimd", g_body)

        def t_body(t):
            c.need(f"{tag}ones")
            for k in range(kblocks):
                c.need(f"{tag}in{k}")
            for nb in range(TB):
                ins = mm_accum(c, t, PS_STAT[:, nb * 512:nb * 512 + 512],
                               [ONESC[:, 0:1]] * kblocks,
                               [X[:, k * T + nb * 512: k * T + nb * 512 + 512]
                                for k in range(kblocks)])
            c.done(f"{tag}sxmm", ins, "tt")
            c.need(f"{tag}sxev")
            for k in range(kblocks):
                c.need(f"{tag}sq{k}")
                for nb in range(TB):
                    ins = t.matmul(PS_STAT[:, nb * 512:nb * 512 + 512],
                                   ONESC[:, 0:1],
                                   xsq[k % 2][:, nb * 512:nb * 512 + 512],
                                   start=(k == 0), stop=(k == kblocks - 1))
                c.done(f"{tag}qmm{k}", ins, "tt")
            c.need(f"{tag}rowa")
            for nb in range(TB):
                ins = t.matmul(PS_A[:, nb * 512:nb * 512 + 512], ONESR[:],
                               ROWS[:, 2 * T + nb * 512:2 * T + nb * 512 + 512],
                               start=True, stop=True)
            c.done(f"{tag}bca", ins, "tt")
            c.need(f"{tag}aev", f"{tag}rowb")
            for nb in range(TB):
                ins = t.matmul(PS_A[:, nb * 512:nb * 512 + 512], ONESR[:],
                               ROWS[:, 3 * T + nb * 512:3 * T + nb * 512 + 512],
                               start=True, stop=True)
            c.done(f"{tag}bcb", ins, "tt")
        c.body("tensor", t_body)

        def s_body(s):
            for k in range(kblocks):
                c.need(f"{tag}in{k}")
                if k >= 2:
                    c.need(f"{tag}qmm{k - 2}")
                ins = s.activation(xsq[k % 2][:], X[:, k * T:(k + 1) * T], AF.Square)
                c.done(f"{tag}sq{k}", ins, "ss")
            c.need(f"{tag}m")
            ins = s.activation(ROWS[:, 3 * T:4 * T], ROWS[:, T:2 * T], AF.Square)
            c.done(f"{tag}msq", ins, "ss")
            c.need(f"{tag}var")
            ins = s.activation(ROWS[:, 2 * T:3 * T], ROWS[:, 2 * T:3 * T],
                               AF.Sqrt, bias=1e-5)
            c.done(f"{tag}sd", ins, "ss")
        c.body("scalar", s_body)

        def v_body(v):
            c.need(f"{tag}sxmm")
            ins = v.tensor_copy(ROWS[:, 0:T], PS_STAT[:, 0:T])
            c.done(f"{tag}sxev", ins, "vv")
            ins = v.tensor_scalar_mul(ROWS[:, T:2 * T], ROWS[:, 0:T], 1.0 / D)
            c.done(f"{tag}m", ins, "vv")
            c.need(f"{tag}qmm{kblocks - 1}")
            ins = v.tensor_copy(ROWS[:, 0:T], PS_STAT[:, 0:T])
            c.need(f"{tag}msq")
            ins = v.scalar_tensor_tensor(ROWS[:, 2 * T:3 * T], ROWS[:, 0:T],
                                         1.0 / D, ROWS[:, 3 * T:4 * T],
                                         ALU.mult, ALU.subtract)
            c.done(f"{tag}var", ins, "vv")
            c.need(f"{tag}sd")
            ins = v.reciprocal(ROWS[:, 2 * T:3 * T], ROWS[:, 2 * T:3 * T])
            c.done(f"{tag}rowa", ins, "vv")
            ins = v.scalar_tensor_tensor(ROWS[:, 3 * T:4 * T], ROWS[:, T:2 * T],
                                         -1.0, ROWS[:, 2 * T:3 * T],
                                         ALU.mult, ALU.mult)
            c.done(f"{tag}rowb", ins, "vv")
            c.need(f"{tag}bca")
            ins = v.tensor_copy(SQ1[:], PS_A[:, 0:T])
            c.done(f"{tag}aev", ins, "vv")
            c.need(f"{tag}bcb")
            for k in range(kblocks):
                v.tensor_tensor(SQ0[:], X[:, k * T:(k + 1) * T], SQ1[:], ALU.mult)
                ins = v.tensor_tensor(XN[:, k * T:(k + 1) * T], SQ0[:],
                                      PS_A[:, 0:T], ALU.add)
                c.done(f"{tag}xn{k}", ins, "vv")
            c.alias(f"{tag}xn", f"{tag}xn{kblocks - 1}")
        c.body("vector", v_body)


def emit_ln3(c, X, XN, tag, kblocks=KB, in_bf16=False, alpha=None, beta=None,
             ddof1=False, eps=1e-5, eps_outside=False, after=None):
    nc = c.nc
    N = kblocks * 128
    c.const(float(eps))
    dt_in = BF16 if in_bf16 else F32
    with (
        nc.sbuf_tensor(f"{tag}_sq0", [128, T], F32) as SQ0,
        nc.sbuf_tensor(f"{tag}_sq1", [128, T], F32) as SQ1,
        nc.sbuf_tensor(f"{tag}_rows", [1, 4 * T], F32) as ROWS,
        nc.sbuf_tensor(f"{tag}_oc", [128, 1], dt_in) as ONESC,
        nc.sbuf_tensor(f"{tag}_ocf", [128, 1], F32) as ONESCF,
        nc.sbuf_tensor(f"{tag}_or", [1, 128], F32) as ONESR,
        nc.psum_tensor(f"{tag}_pst", [1, T], F32) as PS_STAT,
        nc.psum_tensor(f"{tag}_psa", [128, T], F32) as PS_A,
    ):
        xsq = [SQ0, SQ1]

        def g_body(g):
            if after is not None:
                c.need(after)
            g.memset(ONESC[:], 1.0)
            g.memset(ONESCF[:], 1.0)
            ins = g.memset(ONESR[:], 1.0)
            c.done(f"{tag}ones", ins, "gg")
        c.body("gpsimd", g_body)

        def t_body(t):
            c.need(f"{tag}ones")
            for k in range(kblocks):
                c.need(f"{tag}in{k}")
            for nb in range(TB):
                ins = mm_accum(c, t, PS_STAT[:, nb * 512:nb * 512 + 512],
                               [ONESC[:, 0:1]] * kblocks,
                               [X[:, k * T + nb * 512: k * T + nb * 512 + 512]
                                for k in range(kblocks)])
            c.done(f"{tag}sxmm", ins, "tt")
            c.need(f"{tag}sxev")
            for k in range(kblocks):
                c.need(f"{tag}sq{k}")
                for nb in range(TB):
                    ins = t.matmul(PS_STAT[:, nb * 512:nb * 512 + 512],
                                   ONESCF[:, 0:1],
                                   xsq[k % 2][:, nb * 512:nb * 512 + 512],
                                   start=(k == 0), stop=(k == kblocks - 1))
                c.done(f"{tag}qmm{k}", ins, "tt")
            c.need(f"{tag}rowa")
            for nb in range(TB):
                ins = t.matmul(PS_A[:, nb * 512:nb * 512 + 512], ONESR[:],
                               ROWS[:, 2 * T + nb * 512:2 * T + nb * 512 + 512],
                               start=True, stop=True)
            c.done(f"{tag}bca", ins, "tt")
            c.need(f"{tag}aev", f"{tag}rowb")
            for nb in range(TB):
                ins = t.matmul(PS_A[:, nb * 512:nb * 512 + 512], ONESR[:],
                               ROWS[:, 3 * T + nb * 512:3 * T + nb * 512 + 512],
                               start=True, stop=True)
            c.done(f"{tag}bcb", ins, "tt")
        c.body("tensor", t_body)

        def s_body(s):
            for k in range(kblocks):
                c.need(f"{tag}in{k}")
                if k >= 2:
                    c.need(f"{tag}qmm{k - 2}")
                ins = s.activation(xsq[k % 2][:], X[:, k * T:(k + 1) * T],
                                   AF.Square)
                c.done(f"{tag}sq{k}", ins, "ss")
            c.need(f"{tag}m")
            ins = s.activation(ROWS[:, 3 * T:4 * T], ROWS[:, T:2 * T], AF.Square)
            c.done(f"{tag}msq", ins, "ss")
            c.need(f"{tag}var")
            if eps_outside:
                ins = s.activation(ROWS[:, 2 * T:3 * T], ROWS[:, 2 * T:3 * T],
                                   AF.Sqrt)
            else:
                ins = s.activation(ROWS[:, 2 * T:3 * T], ROWS[:, 2 * T:3 * T],
                                   AF.Sqrt, bias=float(eps))
            c.done(f"{tag}sd", ins, "ss")
        c.body("scalar", s_body)

        def v_body(v):
            c.need(f"{tag}sxmm")
            ins = v.tensor_copy(ROWS[:, 0:T], PS_STAT[:, 0:T])
            c.done(f"{tag}sxev", ins, "vv")
            ins = v.tensor_scalar_mul(ROWS[:, T:2 * T], ROWS[:, 0:T], 1.0 / N)
            c.done(f"{tag}m", ins, "vv")
            c.need(f"{tag}qmm{kblocks - 1}")
            ins = v.tensor_copy(ROWS[:, 0:T], PS_STAT[:, 0:T])
            c.need(f"{tag}msq")
            vscale = (1.0 / N) * (N / (N - 1.0)) if ddof1 else (1.0 / N)
            msc = (N / (N - 1.0)) if ddof1 else 1.0
            v.scalar_tensor_tensor(ROWS[:, 2 * T:3 * T], ROWS[:, 0:T],
                                   vscale, ROWS[:, 3 * T:4 * T],
                                   ALU.mult, ALU.bypass)
            ins = v.scalar_tensor_tensor(ROWS[:, 2 * T:3 * T],
                                         ROWS[:, 3 * T:4 * T], -msc,
                                         ROWS[:, 2 * T:3 * T],
                                         ALU.mult, ALU.add)
            c.done(f"{tag}var", ins, "vv")
            c.need(f"{tag}sd")
            if eps_outside:
                v.tensor_scalar_add(ROWS[:, 2 * T:3 * T], ROWS[:, 2 * T:3 * T],
                                    float(eps))
            ins = v.reciprocal(ROWS[:, 2 * T:3 * T], ROWS[:, 2 * T:3 * T])
            c.done(f"{tag}rowa", ins, "vv")
            ins = v.scalar_tensor_tensor(ROWS[:, 3 * T:4 * T], ROWS[:, T:2 * T],
                                         -1.0, ROWS[:, 2 * T:3 * T],
                                         ALU.mult, ALU.mult)
            c.done(f"{tag}rowb", ins, "vv")
            c.need(f"{tag}bca")
            ins = v.tensor_copy(SQ1[:], PS_A[:, 0:T])
            c.done(f"{tag}aev", ins, "vv")
            c.need(f"{tag}bcb")
            for k in range(kblocks):
                v.tensor_tensor(SQ0[:], X[:, k * T:(k + 1) * T], SQ1[:],
                                ALU.mult)
                if alpha is not None:
                    v.tensor_tensor(SQ0[:], SQ0[:], PS_A[:, 0:T], ALU.add)
                    ins = v.tensor_scalar(XN[:, k * T:(k + 1) * T], SQ0[:],
                                          alpha[:, k:k + 1], beta[:, k:k + 1],
                                          ALU.mult, ALU.add)
                else:
                    ins = v.tensor_tensor(XN[:, k * T:(k + 1) * T], SQ0[:],
                                          PS_A[:, 0:T], ALU.add)
                c.done(f"{tag}xn{k}", ins, "vv")
            c.alias(f"{tag}xn", f"{tag}xn{kblocks - 1}")
        c.body("vector", v_body)


def emit_s1(c, W):
    nc = c.nc
    c.const(1.0 / 32.0)
    est = contextlib.ExitStack()
    with est:
        X = est.enter_context(nc.sbuf_tensor("s1X", [128, KB * T], F32))
        BQ = est.enter_context(nc.sbuf_tensor("s1bq", [128, 16], F32))
        BV = est.enter_context(nc.sbuf_tensor("s1bv", [1, D], BF16))
        EYE = est.enter_context(nc.sbuf_tensor("s1eye", [128, 128], BF16))
        ONESRB = est.enter_context(nc.sbuf_tensor("s1orb", [1, 128], BF16))
        SROW = est.enter_context(nc.sbuf_tensor("s1srow", [128, 8], F32))
        chain_gate = getattr(c, "chain_gate", None)

        def qload0(g):
            if chain_gate:
                c.need(chain_gate)
            pairs = [(X[:, k * T:(k + 1) * T],
                      W["xT"][k * 128:(k + 1) * 128, :]) for k in range(KB)]
            pairs += [(BQ[:], W["bq_col"]), (BV[:], W["bv_row"]),
                      (EYE[:], W["eye"])]
            c.dma_batch("in0", pairs)
            for k in range(KB):
                c.alias(f"s1_in{k}", "in0")
            for lb in ("ldbq", "ldbv", "ldeye"):
                c.alias(lb, "in0")
            ins = g.memset(ONESRB[:], 1.0)
            c.done("onesrb", ins, "gg")
        c.body("gpsimd", qload0)

        with nc.sbuf_tensor("s1xn", [128, KB * T], BF16) as XN:
            emit_ln2(c, X, XN, "s1_")

            QT = est.enter_context(
                nc.sbuf_tensor("s1qt", [128, KB * T], BF16, side="right"))
            KT = est.enter_context(
                nc.sbuf_tensor("s1kt", [128, KB * T], BF16, side="right"))
            VT = est.enter_context(
                nc.sbuf_tensor("s1vt", [128, 16 * 1024], BF16, side="right"))

            with (
                nc.sbuf_tensor("s1wb0", [128, KB * 256], BF16) as WB0,
                nc.sbuf_tensor("s1wb1", [128, KB * 256], BF16) as WB1,
                nc.psum_tensor("s1pj0", [128, 512], F32) as PJ0,
                nc.psum_tensor("s1pj1", [128, 512], F32) as PJ1,
                nc.psum_tensor("s1pj2", [128, 512], F32) as PJ2,
                nc.psum_tensor("s1pj3", [128, 512], F32) as PJ3,
            ):
                WBs = [WB0, WB1]
                PJ = [PJ0, PJ1, PJ2, PJ3]
                NG = 12

                def qload1(g):
                    c.need("s1_xn")
                    for grp in range(NG):
                        if grp >= 2:
                            c.need(f"pgrp{grp - 2}")
                        wb = WBs[grp % 2]
                        c.dma_batch(
                            f"ldw{grp}",
                            [(wb[:, k * 256:(k + 1) * 256],
                              W["wqkvT"][k * 128:(k + 1) * 128,
                                         grp * 256:(grp + 1) * 256])
                             for k in range(KB)],
                            sem=f"w{grp % 2}")
                c.body("gpsimd", qload1)

                def jobs_of(grp):
                    if grp < 8:
                        return [("qk", grp, m, tb) for m in range(2)
                                for tb in range(TB)]
                    else:
                        return [("v", grp, tc, 0) for tc in range(16)]

                def t_body2(t):
                    c.need("s1_xn", "onesrb", "ldbv")
                    gi = 0
                    for grp in range(NG):
                        c.need(f"ldw{grp}")
                        wb = WBs[grp % 2]
                        for job in jobs_of(grp):
                            if gi >= 4:
                                c.need(f"pjev{gi - 4}")
                            ps = PJ[gi % 4]
                            if job[0] == "qk":
                                _, _, m, tb = job
                                ins = mm_accum(
                                    c, t, ps[:],
                                    [wb[:, k * 256 + m * 128:
                                        k * 256 + m * 128 + 128]
                                     for k in range(KB)],
                                    [XN[:, k * T + tb * 512:
                                        k * T + tb * 512 + 512]
                                     for k in range(KB)])
                            else:
                                _, _, tc, _ = job
                                for k in range(KB):
                                    t.matmul(ps[:, 0:256],
                                             XN[:, k * T + tc * 128:
                                                k * T + tc * 128 + 128],
                                             wb[:, k * 256:(k + 1) * 256],
                                             start=(k == 0), stop=False)
                                ins = t.matmul(
                                    ps[:, 0:256], ONESRB[:],
                                    BV[:, (grp - 8) * 256:(grp - 8) * 256 + 256],
                                    start=False, stop=True)
                            c.done(f"pjmm{gi}", ins, "tt")
                            gi += 1
                        c.alias(f"pgrp{grp}", f"pjmm{gi - 1}")
                c.body("tensor", t_body2)

                def s_body2(s):
                    gi = 0
                    for grp in range(NG):
                        for job in jobs_of(grp):
                            if job[0] == "qk":
                                c.need(f"pjmm{gi}")
                                _, _, m, tb = job
                                mm = grp * 2 + m
                                dst = QT if mm < 8 else KT
                                mloc = mm % 8
                                ins = s.activation(
                                    dst[:, mloc * T + tb * 512:
                                        mloc * T + tb * 512 + 512],
                                    PJ[gi % 4][:], AF.Identity,
                                    bias=BQ[:, mm:mm + 1])
                                c.done(f"pjev{gi}", ins, "ss")
                                last_qk = gi
                            gi += 1
                    c.alias("qk_evac_all", f"pjev{last_qk}")
                c.body("scalar", s_body2)

                def v_body2(v):
                    gi = 0
                    ins = None
                    for grp in range(NG):
                        for job in jobs_of(grp):
                            if job[0] == "v":
                                c.need(f"pjmm{gi}")
                                _, _, tc, _ = job
                                eb = grp - 8
                                ins = v.tensor_copy(
                                    VT[:, tc * 1024 + eb * 256:
                                       tc * 1024 + eb * 256 + 256],
                                    PJ[gi % 4][:, 0:256])
                                c.done(f"pjev{gi}", ins, "vv")
                                last_v = gi
                            gi += 1
                    c.alias("v_all", f"pjev{last_v}")
                c.body("vector", v_body2)

        with (
            nc.sbuf_tensor("s1at", [128, 16 * 512], BF16) as ATT,
            nc.sbuf_tensor("s1aa", [128, 4 * T], BF16) as AA,
            nc.psum_tensor("s1ps_s", [128, T], F32) as PS_S,
            nc.psum_tensor("s1ps_t0", [128, 128], BF16) as PS_T0,
            nc.psum_tensor("s1ps_t1", [128, 128], BF16) as PS_T1,
            nc.psum_tensor("s1ps_o0", [128, 512], F32) as PS_O0,
            nc.psum_tensor("s1ps_o1", [128, 512], F32) as PS_O1,
        ):
            PS_T = [PS_T0, PS_T1]
            PS_O = [PS_O0, PS_O1]

            def t_body3(t):
                c.need("qk_evac_all", "v_all")
                c.need("ldeye")
                for qg in range(4):
                    for qb in range(4):
                        qi = qg * 4 + qb
                        if qi > 0:
                            c.need(f"anorm{qi - 1}")
                        for sb in range(TB):
                            ins = mm_accum(c, t,
                                           PS_S[:, sb * 512:sb * 512 + 512],
                                           [QT[:, k * T + qi * 128:
                                               k * T + qi * 128 + 128]
                                            for k in range(KB)],
                                           [KT[:, k * T + sb * 512:
                                               k * T + sb * 512 + 512]
                                            for k in range(KB)])
                        c.done(f"smm{qi}", ins, "tt")
                        c.need(f"anorm{qi}")
                        for sc in range(16):
                            if sc >= 2:
                                c.need(f"tpe{qi}_{sc - 2}")
                            ins = t.transpose(PS_T[sc % 2][:],
                                              AA[:, qb * T + sc * 128:
                                                 qb * T + sc * 128 + 128],
                                              EYE[:])
                            c.done(f"tp{qi}_{sc}", ins, "tt")
                    for e in range(KB):
                        gg = qg * KB + e
                        if gg >= 2:
                            pg, pe = divmod(gg - 2, KB)
                            c.need(f"oev{pg}_{pe}")
                        c.need(f"tpe{qg * 4 + 3}_15")
                        ins = mm_accum(c, t, PS_O[gg % 2][:],
                                       [VT[:, sc * 1024 + e * 128:
                                           sc * 1024 + e * 128 + 128]
                                        for sc in range(16)],
                                       [ATT[:, sc * 512:sc * 512 + 512]
                                        for sc in range(16)])
                        c.done(f"omm{qg}_{e}", ins, "tt")
            c.body("tensor", t_body3)

            def v_body3(v):
                ins = None
                for qg in range(4):
                    for qb in range(4):
                        qi = qg * 4 + qb
                        c.need(f"smm{qi}")
                        v.reduce_max(SROW[:, 0:1], PS_S[:, 0:T], AX.X)
                        ins = v.tensor_scalar_mul(SROW[:, 1:2], SROW[:, 0:1],
                                                  -1.0 / 32.0)
                        c.done(f"srow{qi}", ins, "vv")
                        c.need(f"sexp{qi}")
                        v.reciprocal(SROW[:, 4:5], SROW[:, 3:4])
                        ins = v.tensor_scalar_mul(AA[:, qb * T:(qb + 1) * T],
                                                  AA[:, qb * T:(qb + 1) * T],
                                                  SROW[:, 4:5])
                        c.done(f"anorm{qi}", ins, "vv")
                    for e in range(KB):
                        c.need(f"omm{qg}_{e}")
                        ins = v.tensor_tensor(
                            X[:, e * T + qg * 512: e * T + qg * 512 + 512],
                            X[:, e * T + qg * 512: e * T + qg * 512 + 512],
                            PS_O[(qg * KB + e) % 2][:], ALU.add)
                        c.done(f"oev{qg}_{e}", ins, "vv")
                c.alias("res_done", f"oev3_{KB - 1}")
            c.body("vector", v_body3)

            def s_body3(s):
                for qg in range(4):
                    for qb in range(4):
                        qi = qg * 4 + qb
                        c.need(f"srow{qi}")
                        ins = s.activation(AA[:, qb * T:(qb + 1) * T],
                                           PS_S[:, 0:T], AF.Exp,
                                           bias=SROW[:, 1:2], scale=1.0 / 32.0,
                                           accum_out=SROW[:, 3:4])
                        c.done(f"sexp{qi}", ins, "ss")
                        for sc in range(16):
                            c.need(f"tp{qi}_{sc}")
                            ins = s.activation(
                                ATT[:, sc * 512 + qb * 128:
                                    sc * 512 + qb * 128 + 128],
                                PS_T[sc % 2][:], AF.Copy)
                            c.done(f"tpe{qi}_{sc}", ins, "ss")
            c.body("scalar", s_body3)

        def st_body(sy):
            c.need("res_done")
            c.dma_batch("sty", [(W["yT"][k * 128:(k + 1) * 128, :],
                                 X[:, k * T:(k + 1) * T]) for k in range(KB)])
            sy.wait_ge(c.sems["qs"].h, c.sems["qs"].n)
        c.body("sync", st_body)


def emit_s2(c, W, tag):
    nc = c.nc
    est = contextlib.ExitStack()
    with est:
        X = est.enter_context(nc.sbuf_tensor(f"{tag}X", [128, KB * T], F32))
        GTW = est.enter_context(nc.sbuf_tensor(f"{tag}gt", [128, KB * 8], BF16))
        EYE = est.enter_context(nc.sbuf_tensor(f"{tag}eye", [128, 128], BF16))
        ONESRB = est.enter_context(nc.sbuf_tensor(f"{tag}orb", [1, 128], BF16))
        LG = est.enter_context(nc.sbuf_tensor(f"{tag}lg", [128, 128], F32))
        EXPS = est.enter_context(nc.sbuf_tensor(f"{tag}ex", [128, 128], F32))
        MM_ = est.enter_context(nc.sbuf_tensor(f"{tag}mm", [128, 128], F32))
        MMB = est.enter_context(nc.sbuf_tensor(f"{tag}mmb", [128, 128], BF16))
        TMPB = est.enter_context(nc.sbuf_tensor(f"{tag}tb", [128, 128], BF16))
        ST = est.enter_context(nc.sbuf_tensor(f"{tag}st", [128, 64], F32))
        MROW = est.enter_context(nc.sbuf_tensor(f"{tag}mr", [1, T], BF16))
        B1ROW = est.enter_context(nc.sbuf_tensor(f"{tag}b1r", [1, 4096], BF16))
        B2ROW = est.enter_context(nc.sbuf_tensor(f"{tag}b2r", [1, 1024], BF16))

        chain_gate = getattr(c, "chain_gate", None)

        def qload0(g):
            if chain_gate:
                c.need(chain_gate)
            pairs = [(X[:, k * T:(k + 1) * T],
                      W["xT"][k * 128:(k + 1) * 128, :]) for k in range(KB)]
            pairs += [(GTW[:, k * 8:(k + 1) * 8],
                       W[f"{tag}gateT"][k * 128:(k + 1) * 128, :])
                      for k in range(KB)]
            pairs += [(EYE[:], W["eye"])]
            c.dma_batch(f"{tag}in0", pairs)
            for k in range(KB):
                c.alias(f"{tag}in{k}", f"{tag}in0")
            ins = g.memset(ONESRB[:], 1.0)
            c.done(f"{tag}onesrb", ins, "gg")
        c.body("gpsimd", qload0)

        XN = est.enter_context(
            nc.sbuf_tensor(f"{tag}xnb", [128, KB * T], BF16))
        emit_ln2(c, X, XN, tag)

        with (
            nc.psum_tensor(f"{tag}pr0", [128, 8], F32) as PR0,
            nc.psum_tensor(f"{tag}pr1", [128, 8], F32) as PR1,
            nc.psum_tensor(f"{tag}ptb", [128, 128], BF16) as PTB,
        ):
            PR = [PR0, PR1]

            def t_rt(t):
                c.need(f"{tag}xn", f"{tag}in0")
                for tc in range(16):
                    if tc >= 2:
                        c.need(f"{tag}lgev{tc - 2}")
                    ins = mm_accum(c, t, PR[tc % 2][:],
                                   [XN[:, k * T + tc * 128:
                                       k * T + tc * 128 + 128]
                                    for k in range(KB)],
                                   [GTW[:, k * 8:(k + 1) * 8]
                                    for k in range(KB)])
                    c.done(f"{tag}lgmm{tc}", ins, "tt")
                c.need(f"{tag}mmb")
                ins = t.transpose(PTB[:], MMB[:], EYE[:])
                c.done(f"{tag}ptb", ins, "tt")
            c.body("tensor", t_rt)

            def v_rt(v):
                for tc in range(16):
                    c.need(f"{tag}lgmm{tc}")
                    ins = v.tensor_copy(LG[:, tc * 8:(tc + 1) * 8],
                                        PR[tc % 2][:])
                    c.done(f"{tag}lgev{tc}", ins, "vv")
                lg3 = LG.ap().rearrange("p (a b) -> p a b", a=16, b=8)
                v.tensor_reduce(ST[:, 0:16], lg3, AX.X, ALU.max)
                lmax_b = ST[:, 0:16].unsqueeze(2).broadcast_to([128, 16, 8])
                ex3 = EXPS.ap().rearrange("p (a b) -> p a b", a=16, b=8)
                ins = v.tensor_tensor(ex3, lg3, lmax_b, ALU.subtract)
                c.done(f"{tag}sub", ins, "vv")
                c.need(f"{tag}exp")
                v.tensor_reduce(ST[:, 16:32], ex3, AX.X, ALU.add)
                v.reciprocal(ST[:, 32:48], ST[:, 16:32])
                for e in range(8):
                    lg_sl = lg3[:, :, e:e + 1].squeeze(2)
                    v.tensor_tensor(ST[:, 48:64], lg_sl, ST[:, 0:16],
                                    ALU.is_equal)
                    v.tensor_tensor(MM_[:, e * 16:(e + 1) * 16],
                                    ST[:, 48:64],
                                    ST[:, 32:48], ALU.mult)
                ins = v.tensor_copy(MMB[:], MM_[:])
                c.done(f"{tag}mmb", ins, "vv")
                c.need(f"{tag}ptb")
                ins = v.tensor_copy(TMPB[:], PTB[:])
                c.done(f"{tag}tmpb", ins, "vv")
            c.body("vector", v_rt)

            MED = nc.dram_tensor(f"{tag}med", [128, 128], BF16,
                                 kind="Internal").ap()

            def st_med(sy):
                c.need(f"{tag}tmpb")
                c.dma_batch(f"{tag}medst", [(MED, TMPB[:])])
            c.body("sync", st_med)

            def s_rt(s):
                c.need(f"{tag}sub")
                ins = s.activation(EXPS[:], EXPS[:], AF.Exp)
                c.done(f"{tag}exp", ins, "ss")
            c.body("scalar", s_rt)

        with (
            nc.sbuf_tensor(f"{tag}xg", [128, KB * 512], BF16) as XG,
            nc.sbuf_tensor(f"{tag}w1c", [128, 2 * 8 * 512], BF16) as W1C,
            nc.sbuf_tensor(f"{tag}w2c", [128, 2 * 4 * 1024], BF16) as W2C,
            nc.sbuf_tensor(f"{tag}h1", [128, 32 * 512], BF16) as H1,
            contextlib.ExitStack() as pse,
        ):
            PS = [pse.enter_context(
                nc.psum_tensor(f"{tag}ps{i}", [128, 512], F32))
                for i in range(8)]

            seq = [(th, e) for th in range(4) for e in range(8)]

            def g_stream(g):
                med_flat = MED.rearrange("a b -> (a b)")
                for i, (th, e) in enumerate(seq):
                    p = f"{tag}{th}_{e}_"
                    if i == 0:
                        c.need(f"{tag}medst")
                    else:
                        pth, pe = seq[i - 1]
                        c.need(f"{tag}{pth}_{pe}_yemm7")
                    src = med_flat[e * 2048:(e + 1) * 2048].unsqueeze(0)
                    c.dma_batch(p + "rows", [
                        (MROW[0:1, :], src),
                        (B1ROW[:], W[f"{tag}b1"][e:e + 1, :]),
                        (B2ROW[:], W[f"{tag}b2"][e:e + 1, :]),
                    ])
                    for dc in range(8):
                        bi = i * 8 + dc
                        if bi >= 2:
                            pj = bi - 2
                            pth, pe = seq[pj // 8]
                            pdc = pj % 8
                            c.need(f"{tag}{pth}_{pe}_h1m{pdc * 4 + 3}")
                        else:
                            c.need(f"{tag}xn")
                        c.dma_batch(
                            p + f"ldw1_{dc}",
                            [(W1C[:, (bi % 2) * 4096 + k * 512:
                                  (bi % 2) * 4096 + (k + 1) * 512],
                              W[f"{tag}w1"][e * 1024 + k * 128:
                                            e * 1024 + (k + 1) * 128,
                                            dc * 512:(dc + 1) * 512])
                             for k in range(KB)],
                            sem=f"w{bi % 2}")
                    for sl in range(8):
                        bi = i * 8 + sl
                        if bi >= 2:
                            pj = bi - 2
                            pth, pe = seq[pj // 8]
                            psl = pj % 8
                            c.need(f"{tag}{pth}_{pe}_w2u{psl}")
                        else:
                            c.need(f"{tag}xn")
                        c.dma_batch(
                            p + f"ldw2_{sl}",
                            [(W2C[:, (bi % 2) * 4096 + kk * 1024:
                                  (bi % 2) * 4096 + (kk + 1) * 1024],
                              W[f"{tag}w2"][e * 4096 + sl * 512 + kk * 128:
                                            e * 4096 + sl * 512 + (kk + 1) * 128,
                                            :])
                             for kk in range(4)],
                            sem=f"u{bi % 2}")
            c.body("gpsimd", g_stream)

            def t_exp(t):
                for i, (th, e) in enumerate(seq):
                    p = f"{tag}{th}_{e}_"
                    c.need(p + "rows")
                    if i == 0:
                        c.need(f"{tag}tmpb")
                    else:
                        pth, pe = seq[i - 1]
                        c.need(f"{tag}{pth}_{pe}_oadd0")
                    ins = t.matmul(PS[0][:], ONESRB[:],
                                   MROW[0:1, th * 512:(th + 1) * 512],
                                   start=True, stop=True)
                    c.done(p + "pmbc", ins, "tt")
                    for dc in range(8):
                        c.need(p + f"ldw1_{dc}")
                        if dc == 0:
                            c.need(p + "xg")
                        wb = (i * 8 + dc) % 2
                        for m in range(4):
                            gi = dc * 4 + m
                            if gi >= 2:
                                c.need(p + f"h1e{gi - 2}")
                            elif gi == 1 and i > 0:
                                pth, pe = seq[i - 1]
                                c.need(f"{tag}{pth}_{pe}_oadd1")
                            for k in range(KB):
                                t.matmul(PS[gi % 2][:],
                                         W1C[:, wb * 4096 + k * 512 + m * 128:
                                             wb * 4096 + k * 512 + m * 128 + 128],
                                         XG[:, k * 512:(k + 1) * 512],
                                         start=(k == 0), stop=False)
                            ins = t.matmul(
                                PS[gi % 2][:],
                                B1ROW[0:1, dc * 512 + m * 128:
                                      dc * 512 + m * 128 + 128],
                                MROW[0:1, th * 512:(th + 1) * 512],
                                start=False, stop=True)
                            c.done(p + f"h1m{gi}", ins, "tt")
                    c.need(p + "h1e30", p + "h1e31")
                    if i > 0:
                        pth, pe = seq[i - 1]
                        c.need(f"{tag}{pth}_{pe}_oadd7")
                    for sl in range(8):
                        c.need(p + f"ldw2_{sl}")
                        c.need(p + f"h1e{sl * 4 + 3}")
                        wb = (i * 8 + sl) % 2
                        for kk in range(4):
                            k = sl * 4 + kk
                            for m in range(8):
                                ins = t.matmul(
                                    PS[m][:],
                                    W2C[:, wb * 4096 + kk * 1024 + m * 128:
                                        wb * 4096 + kk * 1024 + m * 128 + 128],
                                    H1[:, k * 512:(k + 1) * 512],
                                    start=(k == 0), stop=False)
                        c.done(p + f"w2u{sl}", ins, "tt")
                    for m in range(8):
                        ins = t.matmul(PS[m][:],
                                       B2ROW[0:1, m * 128:(m + 1) * 128],
                                       MROW[0:1, th * 512:(th + 1) * 512],
                                       start=False, stop=True)
                        c.done(p + f"yemm{m}", ins, "tt")
            c.body("tensor", t_exp)

            def v_exp(v):
                for i, (th, e) in enumerate(seq):
                    p = f"{tag}{th}_{e}_"
                    c.need(p + "pmbc")
                    ins = None
                    for k in range(KB):
                        ins = v.tensor_tensor(
                            XG[:, k * 512:(k + 1) * 512],
                            XN[:, k * T + th * 512: k * T + th * 512 + 512],
                            PS[0][:], ALU.mult)
                    c.done(p + "xg", ins, "vv")
                    for m in range(8):
                        c.need(p + f"yemm{m}")
                        ins = v.tensor_tensor(
                            X[:, m * T + th * 512: m * T + th * 512 + 512],
                            X[:, m * T + th * 512: m * T + th * 512 + 512],
                            PS[m][:], ALU.add)
                        c.done(p + f"oadd{m}", ins, "vv")
            c.body("vector", v_exp)

            def s_exp(s):
                for i, (th, e) in enumerate(seq):
                    p = f"{tag}{th}_{e}_"
                    for gi in range(32):
                        c.need(p + f"h1m{gi}")
                        ins = s.activation(H1[:, gi * 512:(gi + 1) * 512],
                                           PS[gi % 2][:], AF.Relu)
                        c.done(p + f"h1e{gi}", ins, "ss")
            c.body("scalar", s_exp)

        def st_body(sy):
            lth, le = 3, 7
            for m in range(8):
                c.need(f"{tag}{lth}_{le}_oadd{m}")
            c.dma_batch(f"{tag}sty",
                        [(W["yT"][k * 128:(k + 1) * 128, :],
                          X[:, k * T:(k + 1) * T]) for k in range(KB)])
            sy.wait_ge(c.sems["qs"].h, c.sems["qs"].n)
        c.body("sync", st_body)


SEMS = ("qg", "qs", "qa", "tt", "vv", "ss", "gg", "w0", "w1", "u0", "u1", "y0", "y1", "y2", "y3")


def build_single(emit, decls):
    nc = bass.Bass("TRN2", target_bir_lowering=False, debug=False)
    W = {}
    for name, shape, dtype, kind in decls:
        W[name] = nc.dram_tensor(name, shape, dtype, kind=kind).ap()
    with contextlib.ExitStack() as est:
        c = Ctx(nc)
        for name in SEMS:
            c.add_sem(name, est.enter_context(nc.semaphore(name)))
        c.const_pool = est.enter_context(
            nc.sbuf_tensor("cpool", [128, Ctx.NCONST], F32))
        emit(c, W)
        blk = est.enter_context(nc.Block())
        c.run(blk)
    return nc


def _prep_s1_inputs(ins):
    import ml_dtypes
    wqkvT = np.ascontiguousarray(ins["L0_attn_a_W_qkv"].T)
    b = ins["L0_attn_a_b_qkv"]
    bq_col = np.ascontiguousarray(b[:2048].reshape(16, 128).T)
    bv_row = np.ascontiguousarray(b[2048:].reshape(1, D))
    eye = np.eye(128, dtype=np.float32).astype(ml_dtypes.bfloat16)
    return {"wqkvT": wqkvT, "bq_col": bq_col, "bv_row": bv_row, "eye": eye}


S1_DECLS = [
    ("xT", [D, T], F32, "ExternalInput"),
    ("wqkvT", [D, 3072], F32, "ExternalInput"),
    ("bq_col", [128, 16], F32, "ExternalInput"),
    ("bv_row", [1, D], F32, "ExternalInput"),
    ("eye", [128, 128], BF16, "ExternalInput"),
    ("yT", [D, T], F32, "ExternalOutput"),
]


def run_s1(x, ins):
    nc = build_single(emit_s1, S1_DECLS)
    shared = _prep_s1_inputs(ins)
    in_maps = []
    for b in range(B):
        m = dict(shared)
        m["xT"] = np.ascontiguousarray(x[b].T)
        in_maps.append(m)
    res = run_bass_kernel_spmd(nc, in_maps, core_ids=list(range(NCORES)))
    return np.stack([np.ascontiguousarray(r["yT"].T) for r in res.results])



def _prep_s2_inputs(ins, pfx="L0_ffn"):
    gateT = np.ascontiguousarray(ins[f"{pfx}_gate_w"].T)
    w1 = np.ascontiguousarray(ins[f"{pfx}_w1"].reshape(8 * 1024, 4096))
    w2 = np.ascontiguousarray(ins[f"{pfx}_w2"].reshape(8 * 4096, 1024))
    b1 = np.ascontiguousarray(ins[f"{pfx}_b1"])
    b2 = np.ascontiguousarray(ins[f"{pfx}_b2"])
    return {"s2_gateT": gateT, "s2_w1": w1, "s2_w2": w2,
            "s2_b1": b1, "s2_b2": b2}


S2_DECLS = [
    ("xT", [D, T], F32, "ExternalInput"),
    ("s2_gateT", [D, 8], F32, "ExternalInput"),
    ("s2_w1", [8 * 1024, 4096], F32, "ExternalInput"),
    ("s2_w2", [8 * 4096, 1024], F32, "ExternalInput"),
    ("s2_b1", [8, 4096], F32, "ExternalInput"),
    ("s2_b2", [8, 1024], F32, "ExternalInput"),
    ("eye", [128, 128], BF16, "ExternalInput"),
    ("yT", [D, T], F32, "ExternalOutput"),
]


def run_s2(x, ins):
    import ml_dtypes
    nc = build_single(lambda c, W: emit_s2(c, W, "s2_"), S2_DECLS)
    shared = _prep_s2_inputs(ins)
    shared["eye"] = np.eye(128, dtype=np.float32).astype(ml_dtypes.bfloat16)
    in_maps = []
    for b in range(B):
        m = dict(shared)
        m["xT"] = np.ascontiguousarray(x[b].T)
        in_maps.append(m)
    res = run_bass_kernel_spmd(nc, in_maps, core_ids=list(range(NCORES)))
    return np.stack([np.ascontiguousarray(r["yT"].T) for r in res.results])


def emit_s6(c, W, tag):
    nc = c.nc
    est = contextlib.ExitStack()
    with est:
        X = est.enter_context(nc.sbuf_tensor(f"{tag}X", [128, KB * T], F32))
        GTW = est.enter_context(nc.sbuf_tensor(f"{tag}gt", [128, KB * 8], BF16))
        EYE = est.enter_context(nc.sbuf_tensor(f"{tag}eye", [128, 128], BF16))
        ONESRB = est.enter_context(nc.sbuf_tensor(f"{tag}orb", [1, 128], BF16))
        LG = est.enter_context(nc.sbuf_tensor(f"{tag}lg", [128, 128], F32))
        PRB = est.enter_context(nc.sbuf_tensor(f"{tag}prb", [128, 128], F32))
        MM_ = est.enter_context(nc.sbuf_tensor(f"{tag}mm", [128, 128], F32))
        MMB = est.enter_context(nc.sbuf_tensor(f"{tag}mmb", [128, 128], BF16))
        TMPB = est.enter_context(nc.sbuf_tensor(f"{tag}tb", [128, 128], BF16))
        ST = est.enter_context(nc.sbuf_tensor(f"{tag}st", [128, 128], F32))
        MROW = est.enter_context(nc.sbuf_tensor(f"{tag}mr", [1, T], BF16))

        chain_gate = getattr(c, "chain_gate", None)

        def qload0(g):
            if chain_gate:
                c.need(chain_gate)
            pairs = [(X[:, k * T:(k + 1) * T],
                      W["xT"][k * 128:(k + 1) * 128, :]) for k in range(KB)]
            pairs += [(GTW[:, k * 8:(k + 1) * 8],
                       W[f"{tag}gateT"][k * 128:(k + 1) * 128, :])
                      for k in range(KB)]
            pairs += [(EYE[:], W["eye"])]
            c.dma_batch(f"{tag}in0", pairs)
            for k in range(KB):
                c.alias(f"{tag}in{k}", f"{tag}in0")
            ins = g.memset(ONESRB[:], 1.0)
            c.done(f"{tag}onesrb", ins, "gg")
        c.body("gpsimd", qload0)

        XN = est.enter_context(
            nc.sbuf_tensor(f"{tag}xnb", [128, KB * T], BF16))
        emit_ln2(c, X, XN, tag)

        with (
            nc.psum_tensor(f"{tag}pr0", [128, 8], F32) as PR0,
            nc.psum_tensor(f"{tag}pr1", [128, 8], F32) as PR1,
            nc.psum_tensor(f"{tag}ptb", [128, 128], BF16) as PTB,
        ):
            PR = [PR0, PR1]

            def t_rt(t):
                c.need(f"{tag}xn", f"{tag}in0")
                for tc in range(16):
                    if tc >= 2:
                        c.need(f"{tag}lgev{tc - 2}")
                    ins = mm_accum(c, t, PR[tc % 2][:],
                                   [XN[:, k * T + tc * 128:
                                       k * T + tc * 128 + 128]
                                    for k in range(KB)],
                                   [GTW[:, k * 8:(k + 1) * 8]
                                    for k in range(KB)])
                    c.done(f"{tag}lgmm{tc}", ins, "tt")
                c.need(f"{tag}mmb")
                ins = t.transpose(PTB[:], MMB[:], EYE[:])
                c.done(f"{tag}ptb", ins, "tt")
            c.body("tensor", t_rt)

            def v_rt(v):
                for tc in range(16):
                    c.need(f"{tag}lgmm{tc}")
                    ins = v.tensor_copy(LG[:, tc * 8:(tc + 1) * 8],
                                        PR[tc % 2][:])
                    c.done(f"{tag}lgev{tc}", ins, "vv")
                c.need(f"{tag}sig")
                p3 = PRB.ap().rearrange("p (a b) -> p a b", a=16, b=8)
                v.tensor_reduce(ST[:, 0:16], p3, AX.X, ALU.max)
                v1b = ST[:, 0:16].unsqueeze(2).broadcast_to([128, 16, 8])
                mm3 = MM_.ap().rearrange("p (a b) -> p a b", a=16, b=8)
                v.tensor_tensor(mm3, p3, v1b, ALU.is_equal)
                v.scalar_tensor_tensor(mm3, mm3, -1e30, p3, ALU.mult, ALU.add)
                v.tensor_reduce(ST[:, 16:32], mm3, AX.X, ALU.max)
                v.tensor_tensor(ST[:, 32:48], ST[:, 0:16], ST[:, 16:32],
                                ALU.add)
                v.tensor_scalar_add(ST[:, 32:48], ST[:, 32:48], 1e-20)
                v.reciprocal(ST[:, 32:48], ST[:, 32:48])
                v.tensor_tensor(ST[:, 48:64], ST[:, 0:16], ST[:, 32:48],
                                ALU.mult)
                ins = v.tensor_tensor(ST[:, 64:80], ST[:, 16:32], ST[:, 32:48],
                                      ALU.mult)
                c.done(f"{tag}wraw", ins, "vv")
                c.need(f"{tag}sq")
                for e in range(8):
                    pe_sl = p3[:, :, e:e + 1].squeeze(2)
                    v.tensor_tensor(ST[:, 80:96], pe_sl, ST[:, 0:16],
                                    ALU.is_equal)
                    v.tensor_tensor(ST[:, 80:96], ST[:, 80:96], ST[:, 48:64],
                                    ALU.mult)
                    v.tensor_tensor(ST[:, 96:112], pe_sl, ST[:, 16:32],
                                    ALU.is_equal)
                    v.scalar_tensor_tensor(MM_[:, e * 16:(e + 1) * 16],
                                           ST[:, 96:112], 1.0, ST[:, 64:80],
                                           ALU.mult, ALU.mult)
                    v.tensor_tensor(MM_[:, e * 16:(e + 1) * 16],
                                    MM_[:, e * 16:(e + 1) * 16],
                                    ST[:, 80:96], ALU.add)
                ins = v.tensor_copy(MMB[:], MM_[:])
                c.done(f"{tag}mmb", ins, "vv")
                c.need(f"{tag}ptb")
                ins = v.tensor_copy(TMPB[:], PTB[:])
                c.done(f"{tag}tmpb", ins, "vv")
            c.body("vector", v_rt)

            def s_rt(s):
                c.need(f"{tag}lgev15")
                ins = s.activation(PRB[:], LG[:], AF.Sigmoid)
                c.done(f"{tag}sig", ins, "ss")
                c.need(f"{tag}wraw")
                ins = s.activation(ST[:, 48:64], ST[:, 48:64], AF.Sqrt)
                ins = s.activation(ST[:, 64:80], ST[:, 64:80], AF.Sqrt)
                c.done(f"{tag}sq", ins, "ss")
            c.body("scalar", s_rt)

            MED = nc.dram_tensor(f"{tag}med", [128, 128], BF16,
                                 kind="Internal").ap()

            def st_med(sy):
                c.need(f"{tag}tmpb")
                c.dma_batch(f"{tag}medst", [(MED, TMPB[:])])
            c.body("sync", st_med)

        with (
            nc.sbuf_tensor(f"{tag}xg", [128, KB * 512], BF16) as XG,
            nc.sbuf_tensor(f"{tag}w1c", [128, 2 * 8 * 512], BF16) as W1C,
            nc.sbuf_tensor(f"{tag}w2c", [128, 2 * 4 * 1024], BF16) as W2C,
            nc.sbuf_tensor(f"{tag}h1", [128, 8 * 512], BF16) as H1,
            nc.sbuf_tensor(f"{tag}hs0", [128, 512], F32) as HS0,
            nc.sbuf_tensor(f"{tag}hs1", [128, 512], F32) as HS1,
            contextlib.ExitStack() as pse,
        ):
            PS = [pse.enter_context(
                nc.psum_tensor(f"{tag}ps{i}", [128, 512], F32))
                for i in range(8)]
            HS = [HS0, HS1]

            seq = [(th, e) for th in range(4) for e in range(8)]

            def g_stream(g):
                med_flat = MED.rearrange("a b -> (a b)")
                for i, (th, e) in enumerate(seq):
                    p = f"{tag}{th}_{e}_"
                    if i == 0:
                        c.need(f"{tag}medst")
                    else:
                        pth, pe = seq[i - 1]
                        c.need(f"{tag}{pth}_{pe}_yemm7")
                    src = med_flat[e * 2048:(e + 1) * 2048].unsqueeze(0)
                    c.dma_batch(p + "rows", [(MROW[0:1, :], src)])
                    for dc in range(2):
                        bi = i * 2 + dc
                        if bi >= 2:
                            pj = bi - 2
                            pth, pe = seq[pj // 2]
                            pdc = pj % 2
                            c.need(f"{tag}{pth}_{pe}_h1m{pdc * 4 + 3}")
                        else:
                            c.need(f"{tag}xn")
                        c.dma_batch(
                            p + f"ldw1_{dc}",
                            [(W1C[:, (bi % 2) * 4096 + k * 512:
                                  (bi % 2) * 4096 + (k + 1) * 512],
                              W[f"{tag}w1"][k * 128:(k + 1) * 128,
                                            e * 1024 + dc * 512:
                                            e * 1024 + (dc + 1) * 512])
                             for k in range(KB)],
                            sem=f"w{bi % 2}")
                    for sl in range(2):
                        bi = i * 2 + sl
                        if bi >= 2:
                            pj = bi - 2
                            pth, pe = seq[pj // 2]
                            psl = pj % 2
                            c.need(f"{tag}{pth}_{pe}_w2u{psl}")
                        else:
                            c.need(f"{tag}xn")
                        c.dma_batch(
                            p + f"ldw2_{sl}",
                            [(W2C[:, (bi % 2) * 4096 + kk * 1024:
                                  (bi % 2) * 4096 + (kk + 1) * 1024],
                              W[f"{tag}w2"][e * 1024 + sl * 512 + kk * 128:
                                            e * 1024 + sl * 512 + (kk + 1) * 128,
                                            :])
                             for kk in range(4)],
                            sem=f"u{bi % 2}")
            c.body("gpsimd", g_stream)

            def t_exp(t):
                for i, (th, e) in enumerate(seq):
                    p = f"{tag}{th}_{e}_"
                    c.need(p + "rows")
                    if i == 0:
                        c.need(f"{tag}tmpb")
                    else:
                        pth, pe = seq[i - 1]
                        c.need(f"{tag}{pth}_{pe}_oadd0")
                    ins = t.matmul(PS[0][:], ONESRB[:],
                                   MROW[0:1, th * 512:(th + 1) * 512],
                                   start=True, stop=True)
                    c.done(p + "pmbc", ins, "tt")
                    for dc in range(2):
                        c.need(p + f"ldw1_{dc}")
                        if dc == 0:
                            c.need(p + "xg")
                        wb = (i * 2 + dc) % 2
                        for m in range(4):
                            gi = dc * 4 + m
                            if gi >= 2:
                                c.need(p + f"h1e{gi - 2}")
                            elif gi == 1 and i > 0:
                                pth, pe = seq[i - 1]
                                c.need(f"{tag}{pth}_{pe}_oadd1")
                            ins = mm_accum(
                                c, t, PS[gi % 2][:],
                                [W1C[:, wb * 4096 + k * 512 + m * 128:
                                     wb * 4096 + k * 512 + m * 128 + 128]
                                 for k in range(KB)],
                                [XG[:, k * 512:(k + 1) * 512]
                                 for k in range(KB)])
                            c.done(p + f"h1m{gi}", ins, "tt")
                    c.need(p + "h1e6", p + "h1e7")
                    if i > 0:
                        pth, pe = seq[i - 1]
                        c.need(f"{tag}{pth}_{pe}_oadd7")
                    for sl in range(2):
                        c.need(p + f"ldw2_{sl}")
                        c.need(p + f"h1e{sl * 4 + 3}")
                        wb = (i * 2 + sl) % 2
                        for kk in range(4):
                            k = sl * 4 + kk
                            for m in range(8):
                                ins = t.matmul(
                                    PS[m][:],
                                    W2C[:, wb * 4096 + kk * 1024 + m * 128:
                                        wb * 4096 + kk * 1024 + m * 128 + 128],
                                    H1[:, k * 512:(k + 1) * 512],
                                    start=(k == 0), stop=(k == 7))
                        c.done(p + f"w2u{sl}", ins, "tt")
                        if sl == 1:
                            for m in range(8):
                                c.alias(p + f"yemm{m}", p + "w2u1")
            c.body("tensor", t_exp)

            def v_exp(v):
                for i, (th, e) in enumerate(seq):
                    p = f"{tag}{th}_{e}_"
                    c.need(p + "pmbc")
                    ins = None
                    for k in range(KB):
                        ins = v.tensor_tensor(
                            XG[:, k * 512:(k + 1) * 512],
                            XN[:, k * T + th * 512: k * T + th * 512 + 512],
                            PS[0][:], ALU.mult)
                    c.done(p + "xg", ins, "vv")
                    for m in range(8):
                        c.need(p + f"yemm{m}")
                        ins = v.tensor_tensor(
                            X[:, m * T + th * 512: m * T + th * 512 + 512],
                            X[:, m * T + th * 512: m * T + th * 512 + 512],
                            PS[m][:], ALU.add)
                        c.done(p + f"oadd{m}", ins, "vv")
            c.body("vector", v_exp)

            def s_exp(s):
                for i, (th, e) in enumerate(seq):
                    p = f"{tag}{th}_{e}_"
                    for gi in range(8):
                        c.need(p + f"h1m{gi}")
                        s.activation(HS[gi % 2][:], PS[gi % 2][:], AF.Relu)
                        ins = s.activation(H1[:, gi * 512:(gi + 1) * 512],
                                           HS[gi % 2][:], AF.Square)
                        c.done(p + f"h1e{gi}", ins, "ss")
            c.body("scalar", s_exp)

        def st_body(sy):
            lth, le = 3, 7
            for m in range(8):
                c.need(f"{tag}{lth}_{le}_oadd{m}")
            c.dma_batch(f"{tag}sty",
                        [(W["yT"][k * 128:(k + 1) * 128, :],
                          X[:, k * T:(k + 1) * T]) for k in range(KB)])
            sy.wait_ge(c.sems["qs"].h, c.sems["qs"].n)
        c.body("sync", st_body)


def _prep_s6_inputs(ins, tag):
    gateT = np.ascontiguousarray(ins["L1_ffn_router_w"].T)
    w1 = np.ascontiguousarray(ins["L1_ffn_w1"])
    w2 = np.ascontiguousarray(ins["L1_ffn_w2"])
    return {f"{tag}gateT": gateT, f"{tag}w1": w1, f"{tag}w2": w2}


def s6_decls(tag):
    return [
        ("xT", [D, T], F32, "ExternalInput"),
        (f"{tag}gateT", [D, 8], F32, "ExternalInput"),
        (f"{tag}w1", [D, 8192], F32, "ExternalInput"),
        (f"{tag}w2", [8192, D], F32, "ExternalInput"),
        ("eye", [128, 128], BF16, "ExternalInput"),
        ("yT", [D, T], F32, "ExternalOutput"),
    ]


def run_s6(x, ins):
    import ml_dtypes
    nc = build_single(lambda c, W: emit_s6(c, W, "s6_"), s6_decls("s6_"))
    shared = _prep_s6_inputs(ins, "s6_")
    shared["eye"] = np.eye(128, dtype=np.float32).astype(ml_dtypes.bfloat16)
    in_maps = []
    for b in range(B):
        m = dict(shared)
        m["xT"] = np.ascontiguousarray(x[b].T)
        in_maps.append(m)
    res = run_bass_kernel_spmd(nc, in_maps, core_ids=list(range(NCORES)))
    return np.stack([np.ascontiguousarray(r["yT"].T) for r in res.results])


def emit_s7(c, W, tag):
    nc = c.nc
    c.const(1e-6)
    PI = float(np.pi)
    for v in (-PI, PI, PI / 2, -1.0, 0.125, -0.5, 1.0, 1.0 / 64.0):
        c.const(v)
    PROJS = ("qa", "ka", "qp", "kp")

    def qi_label(g):
        hg, r = divmod(g, 32)
        h2, qi = divmod(r, 16)
        return f"{tag}h{hg}_anorm{h2}_{qi}"

    def og_label(g):
        hg, r = divmod(g, 8)
        h2, qg = divmod(r, 4)
        return f"{tag}h{hg}_oev{h2}_{qg}"

    TPL = []
    for hg in range(8):
        for h2 in range(2):
            for qg in range(4):
                for qi4 in range(4):
                    qi = qg * 4 + qi4
                    for sc in range((qg + 1) * 4):
                        TPL.append(f"{tag}h{hg}_tpe{h2}_{qi}_{sc}")
    TPOF = {}
    for hg in range(8):
        TPOF[hg] = 320 * hg

    est = contextlib.ExitStack()
    with est:
        X = est.enter_context(nc.sbuf_tensor(f"{tag}X", [128, KB * T], F32))
        EYE = est.enter_context(nc.sbuf_tensor(f"{tag}eye", [128, 128], BF16))
        IND = est.enter_context(nc.sbuf_tensor(f"{tag}inds", [128, 2], BF16))
        INDT = est.enter_context(nc.sbuf_tensor(f"{tag}indts", [2, 128], BF16))
        MSK = est.enter_context(nc.sbuf_tensor(f"{tag}msks", [128, 4 * 512], BF16))
        SLS = est.enter_context(nc.sbuf_tensor(f"{tag}slss", [1, 16], F32))
        ESC = est.enter_context(nc.sbuf_tensor(f"{tag}esc", [1, 16], F32))
        ESCC = est.enter_context(nc.sbuf_tensor(f"{tag}escc", [128, 16], F32))
        ONESR = est.enter_context(nc.sbuf_tensor(f"{tag}onr", [1, 128], F32))
        SROW = est.enter_context(nc.sbuf_tensor(f"{tag}srow", [128, 8], F32))
        RST = est.enter_context(nc.sbuf_tensor(f"{tag}rst", [2, 2 * T], BF16))
        OD = nc.dram_tensor(f"{tag}od", [D, T], BF16, kind="Internal").ap()

        chain_gate = getattr(c, "chain_gate", None)

        def qload0(g):
            if chain_gate:
                c.need(chain_gate)
            pairs = [(X[:, k * T:(k + 1) * T],
                      W["xT"][k * 128:(k + 1) * 128, :]) for k in range(KB)]
            pairs += [(EYE[:], W["eye"]), (IND[:], W[f"{tag}ind"]),
                      (INDT[:], W[f"{tag}indt"]), (MSK[:], W[f"{tag}msk"]),
                      (SLS[:], W[f"{tag}sls"])]
            c.dma_batch(f"{tag}in0", pairs)
            for k in range(KB):
                c.alias(f"{tag}in{k}", f"{tag}in0")
            ins = g.memset(ONESR[:], 1.0)
            c.done(f"{tag}ones2", ins, "gg")
        c.body("gpsimd", qload0)

        stack2 = contextlib.ExitStack()
        XN = stack2.enter_context(nc.sbuf_tensor(f"{tag}xnb", [128, KB * T], BF16))
        emit_ln2(c, X, XN, tag)
        VT = stack2.enter_context(
            nc.sbuf_tensor(f"{tag}vt", [128, 16 * 1024], BF16, side="right"))

        with nc.psum_tensor(f"{tag}pesc", [128, 16], F32) as PESC:
            def s_esc(s):
                c.need(f"{tag}in0")
                ins = s.activation(ESC[:], SLS[:], AF.Exp, scale=1.0)
                c.done(f"{tag}esc1", ins, "ss")
            c.body("scalar", s_esc)

            def v_esc(v):
                c.need(f"{tag}esc1")
                ins = v.tensor_scalar_mul(ESC[:], ESC[:], 0.125)
                c.done(f"{tag}esc2", ins, "vv")
                c.need(f"{tag}escmm")
                ins = v.tensor_copy(ESCC[:], PESC[:])
                c.done(f"{tag}escc", ins, "vv")
            c.body("vector", v_esc)

            def t_esc(t):
                c.need(f"{tag}esc2", f"{tag}ones2")
                ins = t.matmul(PESC[:], ONESR[:], ESC[:], start=True, stop=True)
                c.done(f"{tag}escmm", ins, "tt")
            c.body("tensor", t_esc)

        with (
            nc.sbuf_tensor(f"{tag}wv0", [128, KB * 256], BF16) as WV0,
            nc.sbuf_tensor(f"{tag}wv1", [128, KB * 256], BF16) as WV1,
            nc.psum_tensor(f"{tag}pj0", [128, 256], F32) as VPJ0,
            nc.psum_tensor(f"{tag}pj1", [128, 256], F32) as VPJ1,
        ):
            WVs = [WV0, WV1]
            VPJ = [VPJ0, VPJ1]

            def g_v(g):
                c.need(f"{tag}xn")
                for grp in range(4):
                    if grp >= 2:
                        c.need(f"{tag}vgrp{grp - 2}")
                    c.dma_batch(
                        f"{tag}ldv{grp}",
                        [(WVs[grp % 2][:, k * 256:(k + 1) * 256],
                          W[f"{tag}wvT"][k * 128:(k + 1) * 128,
                                         grp * 256:(grp + 1) * 256])
                         for k in range(KB)],
                        sem=f"w{grp % 2}")
            c.body("gpsimd", g_v)

            def t_v(t):
                c.need(f"{tag}xn", f"{tag}escc")
                gi = 0
                for grp in range(4):
                    c.need(f"{tag}ldv{grp}")
                    for tc in range(16):
                        if gi >= 2:
                            c.need(f"{tag}vev{gi - 2}")
                        ins = mm_accum(c, t, VPJ[gi % 2][:],
                                       [XN[:, k * T + tc * 128:
                                           k * T + tc * 128 + 128]
                                        for k in range(KB)],
                                       [WVs[grp % 2][:, k * 256:(k + 1) * 256]
                                        for k in range(KB)])
                        c.done(f"{tag}vmm{gi}", ins, "tt")
                        gi += 1
                    c.alias(f"{tag}vgrp{grp}", f"{tag}vmm{gi - 1}")
            c.body("tensor", t_v)

            def v_v(v):
                gi = 0
                for grp in range(4):
                    for tc in range(16):
                        c.need(f"{tag}vmm{gi}")
                        ins = v.tensor_copy(
                            VT[:, tc * 1024 + grp * 256:
                               tc * 1024 + grp * 256 + 256],
                            VPJ[gi % 2][:])
                        c.done(f"{tag}vev{gi}", ins, "vv")
                        gi += 1
                c.alias(f"{tag}vall", f"{tag}vev{gi - 1}")
            c.body("vector", v_v)

        QA = stack2.enter_context(nc.sbuf_tensor(f"{tag}qa", [128, T], BF16))
        KA = stack2.enter_context(nc.sbuf_tensor(f"{tag}ka", [128, T], BF16))
        CQ = stack2.enter_context(nc.sbuf_tensor(f"{tag}cq", [128, T], BF16))
        SQ = stack2.enter_context(nc.sbuf_tensor(f"{tag}sq", [128, T], BF16))
        CK = stack2.enter_context(nc.sbuf_tensor(f"{tag}ck", [128, T], BF16))
        SK = stack2.enter_context(nc.sbuf_tensor(f"{tag}sk", [128, T], BF16))
        AA = stack2.enter_context(nc.sbuf_tensor(f"{tag}aa", [128, T], BF16))
        ATB = stack2.enter_context(
            nc.sbuf_tensor(f"{tag}atb", [128, 16 * 512], BF16, side="right"))
        OTH = stack2.enter_context(nc.sbuf_tensor(f"{tag}oth", [128, T], BF16))
        WH = stack2.enter_context(
            nc.sbuf_tensor(f"{tag}wh", [128, 2 * KB * 128], BF16))
        SCR = [stack2.enter_context(
            nc.sbuf_tensor(f"{tag}scr{jj}", [128, 2 * 512], F32))
            for jj in range(2)]
        SQB = stack2.enter_context(
            nc.sbuf_tensor(f"{tag}sqb", [128, 4 * 512], BF16))

        def g_hg(g):
            for hg in range(8):
                p = f"{tag}h{hg}_"
                for pi, nm in enumerate(PROJS):
                    bi = hg * 4 + pi
                    if bi >= 2:
                        pbi = bi - 2
                        phg, ppi = divmod(pbi, 4)
                        c.need(f"{tag}h{phg}_{PROJS[ppi]}mm3")
                    else:
                        c.need(f"{tag}vall")
                    wn = {"qa": "wqaT", "ka": "wkaT",
                          "qp": "wqpT", "kp": "wkpT"}[nm]
                    c.dma_batch(
                        p + f"ld{nm}",
                        [(WH[:, (bi % 2) * 1024 + k * 128:
                             (bi % 2) * 1024 + (k + 1) * 128],
                          W[f"{tag}{wn}"][k * 128:(k + 1) * 128,
                                          hg * 128:(hg + 1) * 128])
                         for k in range(KB)],
                        sem=f"w{bi % 2}")
                c.need(p + "oall")
                c.dma_batch(p + "ostore", [(OD[hg * 128:(hg + 1) * 128, :],
                                            OTH[:])])
        c.body("gpsimd", g_hg)

        PS_S_guard = nc.psum_tensor(f"{tag}ps_s", [128, T], F32)
        PS_S = stack2.enter_context(PS_S_guard)

        for hg in range(8):
            p = f"{tag}h{hg}_"
            with (
                nc.psum_tensor(p + "pja", [128, 512], F32) as PJA,
                nc.psum_tensor(p + "pjb", [128, 512], F32) as PJB,
                nc.psum_tensor(p + "prs", [2, 512], F32) as PRS,
            ):
                PJs = [PJA, PJB]

                def t_proj(t, p=p, hg=hg, PJs=PJs, PRS=PRS):
                    for pi, nm in enumerate(PROJS):
                        c.need(p + f"ld{nm}")
                        bi = hg * 4 + pi
                        for tb in range(TB):
                            gi = pi * 4 + tb
                            if gi >= 2:
                                pnm = PROJS[(gi - 2) // 4]
                                c.need(p + f"{pnm}done{(gi - 2) % 4}")
                            elif hg > 0:
                                c.need(f"{tag}h{hg - 1}_oall")
                            else:
                                c.need(f"{tag}vall")
                            ins = mm_accum(
                                c, t, PJs[gi % 2][:],
                                [WH[:, (bi % 2) * 1024 + k * 128:
                                    (bi % 2) * 1024 + (k + 1) * 128]
                                 for k in range(KB)],
                                [XN[:, k * T + tb * 512:
                                    k * T + tb * 512 + 512]
                                 for k in range(KB)])
                            c.done(p + f"{nm}mm{tb}", ins, "tt")
                    for si, side in enumerate(("q", "k")):
                        for tb in range(TB):
                            c.need(p + f"sq{side}{tb}")
                            idx = si * 4 + tb
                            if idx >= 1:
                                ps_, pt_ = divmod(idx - 1, 4)
                                c.need(p + "rsev" +
                                       ("q" if ps_ == 0 else "k") + str(pt_))
                            ins = t.matmul(PRS[:], IND[:, 0:2],
                                           SQB[:, tb * 512:(tb + 1) * 512],
                                           start=True, stop=True)
                            c.done(p + f"rsmm{side}{tb}", ins, "tt")
                    bi2 = 0
                    for si, side in enumerate(("q", "k")):
                        co = 0 if side == "q" else T
                        c.need(p + f"rstrow{side}")
                        if bi2 == 0:
                            c.need(p + "kpdone3", p + "qpdone3")
                        for tb in range(TB):
                            if bi2 >= 2:
                                pb = bi2 - 2
                                c.need(p + "bcev" +
                                       ("q" if pb < 4 else "k") + str(pb % 4))
                            ins = t.matmul(
                                PJs[bi2 % 2][:], INDT[0:2, :],
                                RST[0:2, co + tb * 512:co + (tb + 1) * 512],
                                start=True, stop=True)
                            c.done(p + f"bcmm{side}{tb}", ins, "tt")
                            bi2 += 1
                c.body("tensor", t_proj)

                def v_proj(v, p=p, hg=hg, PJs=PJs, PRS=PRS):
                    for pi, nm in enumerate(PROJS):
                        kind = "amp" if pi < 2 else "phase"
                        for tb in range(TB):
                            gi = pi * 4 + tb
                            jj = gi % 2
                            sc = SCR[jj]
                            c.need(p + f"{nm}mm{tb}")
                            if gi >= 2:
                                pnm = PROJS[(gi - 2) // 4]
                                c.need(p + f"{pnm}sdone{(gi - 2) % 4}")
                            v.tensor_scalar_mul(
                                sc[:, 0:512], PJs[jj][:], -1.0)
                            ins = v.tensor_tensor(
                                sc[:, 0:512], sc[:, 0:512], PJs[jj][:],
                                ALU.max)
                            c.done(p + f"{nm}abs{tb}", ins, "vv")
                            if kind == "amp":
                                c.need(p + f"{nm}ln{tb}")
                                dst_t = QA if nm == "qa" else KA
                                ins = v.scalar_tensor_tensor(
                                    dst_t[:, tb * 512:(tb + 1) * 512],
                                    PJs[jj][:], 0.0, sc[:, 512:1024],
                                    ALU.max, ALU.add)
                                c.done(p + f"{nm}done{tb}", ins, "vv")
                            else:
                                c.need(p + f"{nm}sin{tb}", p + f"{nm}sgn{tb}")
                                st_ = SQ if nm == "qp" else SK
                                ins = v.tensor_tensor(
                                    st_[:, tb * 512:(tb + 1) * 512],
                                    sc[:, 0:512], sc[:, 512:1024], ALU.mult)
                                c.done(p + f"{nm}done{tb}", ins, "vv")
                    for si, side in enumerate(("q", "k")):
                        co = 0 if side == "q" else T
                        for tb in range(TB):
                            c.need(p + f"rsmm{side}{tb}")
                            ins = v.tensor_copy(
                                RST[0:2, co + tb * 512:co + (tb + 1) * 512],
                                PRS[:])
                            c.done(p + f"rsev{side}{tb}", ins, "vv")
                        ins = v.tensor_scalar_mul(
                            RST[0:2, co:co + T], RST[0:2, co:co + T],
                            1.0 / 64.0)
                        c.done(p + f"rsmean{side}", ins, "vv")
                    bi2 = 0
                    lastev = None
                    for si, side in enumerate(("q", "k")):
                        amp = QA if side == "q" else KA
                        ct = CQ if side == "q" else CK
                        st_ = SQ if side == "q" else SK
                        for tb in range(TB):
                            c.need(p + f"bcmm{side}{tb}")
                            sl = slice(tb * 512, (tb + 1) * 512)
                            v.tensor_tensor(SCR[bi2 % 2][:, 0:512], amp[:, sl],
                                            PJs[bi2 % 2][:], ALU.mult)
                            v.tensor_tensor(ct[:, sl], ct[:, sl],
                                            SCR[bi2 % 2][:, 0:512], ALU.mult)
                            ins = v.tensor_tensor(st_[:, sl], st_[:, sl],
                                                  SCR[bi2 % 2][:, 0:512],
                                                  ALU.mult)
                            c.done(p + f"bcev{side}{tb}", ins, "vv")
                            lastev = p + f"bcev{side}{tb}"
                            bi2 += 1
                    c.alias(p + "csdone", lastev)
                c.body("vector", v_proj)

                def s_proj(s, p=p, hg=hg, PJs=PJs):
                    for pi, nm in enumerate(PROJS):
                        kind = "amp" if pi < 2 else "phase"
                        for tb in range(TB):
                            gi = pi * 4 + tb
                            jj = gi % 2
                            sc = SCR[jj]
                            c.need(p + f"{nm}abs{tb}")
                            if kind == "amp":
                                s.activation(sc[:, 512:1024], sc[:, 0:512],
                                             AF.Exp, scale=-1.0)
                                ins = s.activation(sc[:, 512:1024],
                                                   sc[:, 512:1024],
                                                   AF.Ln, bias=1.0)
                                c.done(p + f"{nm}ln{tb}", ins, "ss")
                                c.alias(p + f"{nm}sdone{tb}",
                                        p + f"{nm}ln{tb}")
                            else:
                                ct = CQ if nm == "qp" else CK
                                s.activation(sc[:, 512:1024], sc[:, 0:512],
                                             AF.Tanh)
                                ins = s.activation(
                                    ct[:, tb * 512:(tb + 1) * 512],
                                    sc[:, 512:1024], AF.Sin,
                                    scale=-PI, bias=PI / 2)
                                c.done(p + f"{nm}cos{tb}", ins, "ss")
                                ins = s.activation(sc[:, 512:1024],
                                                   sc[:, 512:1024],
                                                   AF.Sin, scale=PI)
                                c.done(p + f"{nm}sin{tb}", ins, "ss")
                                ins = s.activation(sc[:, 0:512], PJs[jj][:],
                                                   AF.Sign)
                                c.done(p + f"{nm}sgn{tb}", ins, "ss")
                                c.alias(p + f"{nm}sdone{tb}",
                                        p + f"{nm}sgn{tb}")
                    for si, side in enumerate(("q", "k")):
                        amp_t = QA if side == "q" else KA
                        for tb in range(TB):
                            c.need(p + ("qa" if side == "q" else "ka") +
                                   f"done{tb}")
                            if side == "k":
                                c.need(p + f"rsmmq{tb}")
                            elif hg > 0:
                                c.need(f"{tag}h{hg - 1}_rsmmk{tb}")
                            ins = s.activation(
                                SQB[:, tb * 512:(tb + 1) * 512],
                                amp_t[:, tb * 512:(tb + 1) * 512], AF.Square)
                            c.done(p + f"sq{side}{tb}", ins, "ss")
                    for side in ("q", "k"):
                        co = 0 if side == "q" else T
                        c.need(p + f"rsmean{side}")
                        s.activation(RST[0:2, co:co + T], RST[0:2, co:co + T],
                                     AF.Ln, bias=1e-6)
                        ins = s.activation(RST[0:2, co:co + T],
                                           RST[0:2, co:co + T],
                                           AF.Exp, scale=-0.5)
                        c.done(p + f"rstrow{side}", ins, "ss")
                c.body("scalar", s_proj)

            with (
                nc.psum_tensor(p + "pt0", [128, 128], BF16) as PT0,
                nc.psum_tensor(p + "pt1", [128, 128], BF16) as PT1,
                nc.psum_tensor(p + "po", [64, 512], F32) as PO,
            ):
                PTs = [PT0, PT1]

                def t_att(t, p=p, hg=hg, PTs=PTs, PO=PO):
                    c.need(p + "csdone")
                    ti = TPOF[hg]
                    for h2 in range(2):
                        hoff = h2 * 64
                        for qg in range(4):
                            nsc = (qg + 1) * 4
                            for qi4 in range(4):
                                qi = qg * 4 + qi4
                                q0 = qi * 128
                                g = hg * 32 + h2 * 16 + qi
                                if g > 0:
                                    c.need(qi_label(g - 1))
                                ins = None
                                for sb in range(qg + 1):
                                    ssl = slice(sb * 512, sb * 512 + 512)
                                    t.matmul(PS_S[:, ssl],
                                             CQ[hoff:hoff + 64, q0:q0 + 128],
                                             CK[hoff:hoff + 64, ssl],
                                             start=True, stop=False)
                                    ins = t.matmul(
                                        PS_S[:, ssl],
                                        SQ[hoff:hoff + 64, q0:q0 + 128],
                                        SK[hoff:hoff + 64, ssl],
                                        start=False, stop=True)
                                c.done(p + f"smm{h2}_{qi}", ins, "tt")
                                c.need(p + f"anorm{h2}_{qi}")
                                for sc in range(nsc):
                                    if ti >= 2:
                                        c.need(TPL[ti - 2])
                                    ins = t.transpose(
                                        PTs[sc % 2][:],
                                        AA[:, sc * 128:(sc + 1) * 128],
                                        EYE[:])
                                    c.done(p + f"tp{h2}_{qi}_{sc}", ins, "tt")
                                    ti += 1
                            c.need(p + f"tpq{h2}_{qg}")
                            og = hg * 8 + h2 * 4 + qg
                            if og > 0:
                                c.need(og_label(og - 1))
                            ins = mm_accum(
                                c, t, PO[:],
                                [VT[:, sc2 * 1024 + hg * 128 + hoff:
                                    sc2 * 1024 + hg * 128 + hoff + 64]
                                 for sc2 in range(nsc)],
                                [ATB[:, sc2 * 512:sc2 * 512 + 512]
                                 for sc2 in range(nsc)])
                            c.done(p + f"omm{h2}_{qg}", ins, "tt")
                c.body("tensor", t_att)

                def v_att(v, p=p, hg=hg, PO=PO):
                    for h2 in range(2):
                        h = hg * 2 + h2
                        hoff = h2 * 64
                        for qg in range(4):
                            ns = (qg + 1) * 512
                            for qi4 in range(4):
                                qi = qg * 4 + qi4
                                c.need(p + f"smm{h2}_{qi}")
                                v.tensor_tensor(
                                    PS_S[:, qg * 512:qg * 512 + 512],
                                    PS_S[:, qg * 512:qg * 512 + 512],
                                    MSK[:, qi4 * 512:qi4 * 512 + 512],
                                    ALU.add)
                                v.tensor_reduce(SROW[:, 0:1], PS_S[:, 0:ns],
                                                AX.X, ALU.max)
                                ins = v.scalar_tensor_tensor(
                                    SROW[:, 1:2], SROW[:, 0:1], -1.0,
                                    ESCC[:, h:h + 1], ALU.mult, ALU.mult)
                                c.done(p + f"srow{h2}_{qi}", ins, "vv")
                                c.need(p + f"sexp{h2}_{qi}")
                                v.reciprocal(SROW[:, 4:5], SROW[:, 3:4])
                                ins = v.tensor_scalar_mul(
                                    AA[:, 0:ns], AA[:, 0:ns], SROW[:, 4:5])
                                c.done(p + f"anorm{h2}_{qi}", ins, "vv")
                            c.need(p + f"omm{h2}_{qg}")
                            if h2 == 0 and qg == 0 and hg > 0:
                                c.need(f"{tag}h{hg - 1}_ostore")
                            ins = v.tensor_copy(
                                OTH[hoff:hoff + 64, qg * 512:(qg + 1) * 512],
                                PO[:])
                            c.done(p + f"oev{h2}_{qg}", ins, "vv")
                    c.alias(p + "oall", p + "oev1_3")
                c.body("vector", v_att)

                def s_att(s, p=p, hg=hg, PTs=PTs):
                    for h2 in range(2):
                        h = hg * 2 + h2
                        for qg in range(4):
                            ns = (qg + 1) * 512
                            for qi4 in range(4):
                                qi = qg * 4 + qi4
                                c.need(p + f"srow{h2}_{qi}")
                                ins = s.activation(
                                    AA[:, 0:ns], PS_S[:, 0:ns], AF.Exp,
                                    bias=SROW[:, 1:2], scale=ESCC[:, h:h + 1],
                                    accum_out=SROW[:, 3:4])
                                c.done(p + f"sexp{h2}_{qi}", ins, "ss")
                                for sc in range((qg + 1) * 4):
                                    c.need(p + f"tp{h2}_{qi}_{sc}")
                                    ins = s.activation(
                                        ATB[:, sc * 512 + qi4 * 128:
                                            sc * 512 + qi4 * 128 + 128],
                                        PTs[sc % 2][:], AF.Copy)
                                    c.done(p + f"tpe{h2}_{qi}_{sc}", ins, "ss")
                            c.alias(p + f"tpq{h2}_{qg}",
                                    p + f"tpe{h2}_{qg * 4 + 3}"
                                    f"_{(qg + 1) * 4 - 1}")
                c.body("scalar", s_att)

        def v_close(v):
            c.need(f"{tag}h7_oall")
        c.body("vector", v_close)
        stack2.close()

        with (
            nc.sbuf_tensor(f"{tag}ob", [128, KB * T], BF16) as OB,
            nc.sbuf_tensor(f"{tag}wo0", [128, KB * 256], BF16) as WO0,
            nc.sbuf_tensor(f"{tag}wo1", [128, KB * 256], BF16) as WO1,
            nc.psum_tensor(f"{tag}pf0", [128, 512], F32) as PF0,
            nc.psum_tensor(f"{tag}pf1", [128, 512], F32) as PF1,
        ):
            WOs = [WO0, WO1]
            PF = [PF0, PF1]

            def g_wo(g):
                for grp in range(4):
                    if grp == 0:
                        c.need(f"{tag}h7_ostore", f"{tag}h7_oall")
                    if grp >= 2:
                        c.need(f"{tag}fgrp{grp - 2}")
                    c.dma_batch(
                        f"{tag}ldo{grp}",
                        [(WOs[grp % 2][:, k * 256:(k + 1) * 256],
                          W[f"{tag}woT"][k * 128:(k + 1) * 128,
                                         grp * 256:(grp + 1) * 256])
                         for k in range(KB)],
                        sem=f"u{grp % 2}")
            c.body("gpsimd", g_wo)

            def g_ob(sy):
                c.need(f"{tag}h7_ostore")
                c.dma_batch(f"{tag}ldob",
                            [(OB[:, k * T:(k + 1) * T],
                              OD[k * 128:(k + 1) * 128, :])
                             for k in range(KB)], sem="qs")
            c.body("sync", g_ob)

            def t_wo(t):
                c.need(f"{tag}ldob")
                gi = 0
                for grp in range(4):
                    c.need(f"{tag}ldo{grp}")
                    for m2 in range(2):
                        for tb in range(TB):
                            if gi >= 2:
                                c.need(f"{tag}fev{gi - 2}")
                            ins = mm_accum(
                                c, t, PF[gi % 2][:],
                                [WOs[grp % 2][:, k * 256 + m2 * 128:
                                     k * 256 + m2 * 128 + 128]
                                 for k in range(KB)],
                                [OB[:, k * T + tb * 512: k * T + tb * 512 + 512]
                                 for k in range(KB)])
                            c.done(f"{tag}fmm{gi}", ins, "tt")
                            gi += 1
                    c.alias(f"{tag}fgrp{grp}", f"{tag}fmm{gi - 1}")
            c.body("tensor", t_wo)

            def v_wo(v):
                gi = 0
                ins = None
                for grp in range(4):
                    for m2 in range(2):
                        m = grp * 2 + m2
                        for tb in range(TB):
                            c.need(f"{tag}fmm{gi}")
                            sl = slice(m * T + tb * 512, m * T + tb * 512 + 512)
                            ins = v.tensor_tensor(X[:, sl], X[:, sl],
                                                  PF[gi % 2][:], ALU.add)
                            c.done(f"{tag}fev{gi}", ins, "vv")
                            gi += 1
                c.alias(f"{tag}res", f"{tag}fev{gi - 1}")
            c.body("vector", v_wo)

        def st_body(sy):
            c.need(f"{tag}res")
            c.dma_batch(f"{tag}sty",
                        [(W["yT"][k * 128:(k + 1) * 128, :],
                          X[:, k * T:(k + 1) * T]) for k in range(KB)])
            sy.wait_ge(c.sems["qs"].h, c.sems["qs"].n)
        c.body("sync", st_body)


def _prep_s7_inputs(ins, tag):
    out = {}
    for nm, key in (("wqaT", "L1_attn_b_Wq_amp"), ("wkaT", "L1_attn_b_Wk_amp"),
                    ("wqpT", "L1_attn_b_Wq_phi"), ("wkpT", "L1_attn_b_Wk_phi"),
                    ("wvT", "L1_attn_b_Wv"), ("woT", "L1_attn_b_Wo")):
        out[f"{tag}{nm}"] = np.ascontiguousarray(ins[key].T)
    out[f"{tag}sls"] = np.ascontiguousarray(
        ins["L1_attn_b_score_log_scale"].reshape(1, 16))
    ind = np.zeros((128, 2), np.float32)
    ind[0:64, 0] = 1.0
    ind[64:128, 1] = 1.0
    out[f"{tag}ind"] = ind
    out[f"{tag}indt"] = np.ascontiguousarray(ind.T)
    msk = np.zeros((128, 4 * 512), np.float32)
    for v in range(4):
        for p in range(128):
            msk[p, v * 512 + v * 128 + p + 1: (v + 1) * 512] = -1e30
    out[f"{tag}msk"] = msk
    return out


def s7_decls(tag):
    d = [("xT", [D, T], F32, "ExternalInput"),
         ("eye", [128, 128], BF16, "ExternalInput"),
         (f"{tag}sls", [1, 16], F32, "ExternalInput"),
         (f"{tag}ind", [128, 2], F32, "ExternalInput"),
         (f"{tag}indt", [2, 128], F32, "ExternalInput"),
         (f"{tag}msk", [128, 4 * 512], F32, "ExternalInput"),
         ("yT", [D, T], F32, "ExternalOutput")]
    for nm in ("wqaT", "wkaT", "wqpT", "wkpT", "wvT", "woT"):
        d.append((f"{tag}{nm}", [D, D], F32, "ExternalInput"))
    return d


def run_s7(x, ins):
    import ml_dtypes
    nc = build_single(lambda c, W: emit_s7(c, W, "s7_"), s7_decls("s7_"))
    shared = _prep_s7_inputs(ins, "s7_")
    shared["eye"] = np.eye(128, dtype=np.float32).astype(ml_dtypes.bfloat16)
    in_maps = []
    for b in range(B):
        m = dict(shared)
        m["xT"] = np.ascontiguousarray(x[b].T)
        in_maps.append(m)
    res = run_bass_kernel_spmd(nc, in_maps, core_ids=list(range(NCORES)))
    return np.stack([np.ascontiguousarray(r["yT"].T) for r in res.results])



def emit_s3(c, W, tag):
    nc = c.nc
    NBK = 16
    NS = 16
    SEG = 16
    WIN = 128
    NWIN = T // WIN
    NSEG = WIN // SEG
    for v in (1e-4, 10.0, 9.357622968840175e-14, 1e-6, -1.0, 1.0):
        c.const(v)
    est = contextlib.ExitStack()
    with est:
        X = est.enter_context(nc.sbuf_tensor(f"{tag}X", [128, KB * T], F32))
        INBX = est.enter_context(nc.sbuf_tensor(f"{tag}ibxc", [128, 16], F32))
        INBZ = est.enter_context(nc.sbuf_tensor(f"{tag}ibzc", [128, 16], F32))
        CVW = est.enter_context(nc.sbuf_tensor(f"{tag}cvwc", [128, 64], F32))
        CVB = est.enter_context(nc.sbuf_tensor(f"{tag}cvbc", [128, 16], F32))
        XPBC = est.enter_context(nc.sbuf_tensor(f"{tag}xpbc", [32, 1], F32))
        DTBC = est.enter_context(nc.sbuf_tensor(f"{tag}dtbc", [128, 16], F32))
        ALOG = est.enter_context(nc.sbuf_tensor(f"{tag}alogc", [128, 256], F32))
        AMAT = est.enter_context(nc.sbuf_tensor(f"{tag}amatc", [128, 256], F32))
        DCOL = est.enter_context(nc.sbuf_tensor(f"{tag}dcolc", [128, 16], F32))
        OPBC = est.enter_context(nc.sbuf_tensor(f"{tag}opbc", [128, 8], F32))
        INAC = est.enter_context(nc.sbuf_tensor(f"{tag}inac", [128, 8], F32))
        INB2 = est.enter_context(nc.sbuf_tensor(f"{tag}inb2c", [128, 8], F32))
        NAC = est.enter_context(nc.sbuf_tensor(f"{tag}nac", [128, 8], F32))
        NBCC = est.enter_context(nc.sbuf_tensor(f"{tag}nbcc", [128, 8], F32))
        ONESRB = est.enter_context(nc.sbuf_tensor(f"{tag}orb", [1, 128], BF16))
        ONESCF = est.enter_context(nc.sbuf_tensor(f"{tag}ocf", [128, 1], F32))
        ONESRF = est.enter_context(nc.sbuf_tensor(f"{tag}orf", [1, 128], F32))
        XCD = nc.dram_tensor(f"{tag}xcd", [NBK * 128, T], BF16,
                             kind="Internal").ap()
        ZD = nc.dram_tensor(f"{tag}zd", [NBK * 128, T], BF16,
                            kind="Internal").ap()
        DTD = nc.dram_tensor(f"{tag}dtd", [NBK * 128, T], F32,
                             kind="Internal").ap()
        BCD = nc.dram_tensor(f"{tag}bcd", [32, T], F32, kind="Internal").ap()
        X0D = nc.dram_tensor(f"{tag}x0d", [D, T], BF16, kind="Internal").ap()

        chain_gate = getattr(c, "chain_gate", None)

        def qload0(g):
            if chain_gate:
                c.need(chain_gate)
            pairs = [(X[:, k * T:(k + 1) * T],
                      W["xT"][k * 128:(k + 1) * 128, :]) for k in range(KB)]
            pairs += [(INBX[:], W[f"{tag}ibx"]), (INBZ[:], W[f"{tag}ibz"]),
                      (CVW[:], W[f"{tag}cvw"]), (CVB[:], W[f"{tag}cvb"]),
                      (XPBC[:], W[f"{tag}xpb"]), (DTBC[:], W[f"{tag}dtb"]),
                      (ALOG[:], W[f"{tag}alog"]), (DCOL[:], W[f"{tag}dcol"]),
                      (OPBC[:], W[f"{tag}opb"]), (INAC[:], W[f"{tag}ina"]),
                      (INB2[:], W[f"{tag}inb"]), (NAC[:], W[f"{tag}na"]),
                      (NBCC[:], W[f"{tag}nb"])]
            c.dma_batch(f"{tag}in0", pairs)
            for k in range(KB):
                c.alias(f"{tag}o_in{k}", f"{tag}in0")
            g.memset(ONESRB[:], 1.0)
            g.memset(ONESCF[:], 1.0)
            ins = g.memset(ONESRF[:], 1.0)
            c.done(f"{tag}ones1", ins, "gg")
        c.body("gpsimd", qload0)

        def s_amat(s):
            c.need(f"{tag}in0")
            ins = s.activation(AMAT[:], ALOG[:], AF.Exp)
            c.done(f"{tag}aexp", ins, "ss")
        c.body("scalar", s_amat)

        def v_amat(v):
            c.need(f"{tag}aexp")
            ins = v.tensor_scalar_mul(AMAT[:], AMAT[:], -1.0)
            c.done(f"{tag}amat", ins, "vv")
        c.body("vector", v_amat)

        stAB = contextlib.ExitStack()
        with contextlib.ExitStack() as stA:
            XN0 = stA.enter_context(
                nc.sbuf_tensor(f"{tag}xn0", [128, KB * T], BF16))
            XN = stAB.enter_context(
                nc.sbuf_tensor(f"{tag}xnn", [128, KB * T], BF16, side="right"))
            emit_ln3(c, X, XN0, f"{tag}o_")
            for k in range(KB):
                c.alias(f"{tag}i_in{k}", f"{tag}o_xn")
            emit_ln3(c, XN0, XN, f"{tag}i_", in_bf16=True, alpha=INAC,
                     beta=INB2, ddof1=True, eps=1e-6, eps_outside=True,
                     after=f"{tag}o_xn")

            def st_xn0(sy):
                c.need(f"{tag}i_xn")
                c.dma_batch(f"{tag}stx0",
                            [(X0D[k * 128:(k + 1) * 128, :],
                              XN0[:, k * T:(k + 1) * T]) for k in range(KB)])
            c.body("sync", st_xn0)

        with (
            nc.sbuf_tensor(f"{tag}wg0", [128, KB * 256], BF16) as WG0,
            nc.sbuf_tensor(f"{tag}wg1", [128, KB * 256], BF16) as WG1,
            nc.sbuf_tensor(f"{tag}xin", [128, 2 * (8 + T)], F32) as XIN,
            nc.sbuf_tensor(f"{tag}xcf", [128, T], F32) as XCF,
            nc.sbuf_tensor(f"{tag}xcbb", [128, 2 * T], BF16) as XCB,
            nc.sbuf_tensor(f"{tag}zbb", [128, 2 * T], BF16) as ZB,
            nc.sbuf_tensor(f"{tag}xpw", [128, 32], BF16) as XPW,
            nc.sbuf_tensor(f"{tag}bcs", [32, T], F32) as BCS,
            nc.psum_tensor(f"{tag}pg0", [128, 512], F32) as PG0,
            nc.psum_tensor(f"{tag}pg1", [128, 512], F32) as PG1,
            nc.psum_tensor(f"{tag}pbc0", [32, 512], F32) as PBC0,
            nc.psum_tensor(f"{tag}pbc1", [32, 512], F32) as PBC1,
            nc.psum_tensor(f"{tag}pbc2", [32, 512], F32) as PBC2,
            nc.psum_tensor(f"{tag}pbc3", [32, 512], F32) as PBC3,
        ):
            WGs = [WG0, WG1]
            PG = [PG0, PG1]
            PBC = [PBC0, PBC1, PBC2, PBC3]

            def g_b(g):
                c.need(f"{tag}i_xn")
                g.wait_ge(c.sems["qs"].h, c.events[f"{tag}stx0"][1])
                for grp in range(16):
                    if grp >= 2:
                        c.need(f"{tag}bgrp{grp - 2}")
                    c.dma_batch(
                        f"{tag}ldb{grp}",
                        [(WGs[grp % 2][:, k * 256:(k + 1) * 256],
                          W[f"{tag}winT"][k * 128:(k + 1) * 128,
                                          grp * 256:(grp + 1) * 256])
                         for k in range(KB)],
                        sem=f"w{grp % 2}")
                    if grp < 8:
                        for m2 in range(2):
                            dm = grp * 2 + m2
                            if dm > 0:
                                c.need(f"{tag}bcmm{dm - 1}")
                            c.dma_batch(f"{tag}ldxp{dm}",
                                        [(XPW[:], W[f"{tag}xprojT"]
                                          [dm * 128:(dm + 1) * 128, :])],
                                        sem="u0")
            c.body("gpsimd", g_b)

            def g_bst(sy):
                for dm in range(NBK):
                    c.need(f"{tag}xc{dm}")
                    c.dma_batch(f"{tag}stxc{dm}",
                                [(XCD[dm * 128:(dm + 1) * 128, :],
                                  XCB[:, (dm % 2) * T:(dm % 2 + 1) * T])],
                                sem=f"y{dm % 2}")
                for zi in range(NBK):
                    c.need(f"{tag}zev{zi}")
                    c.dma_batch(f"{tag}stz{zi}",
                                [(ZD[zi * 128:(zi + 1) * 128, :],
                                  ZB[:, (zi % 2) * T:(zi % 2 + 1) * T])],
                                sem=f"y{2 + zi % 2}")
                c.need(f"{tag}bcall")
                c.dma_batch(f"{tag}stbc", [(BCD, BCS[:])])
            c.body("sync", g_bst)

            def t_b(t):
                c.need(f"{tag}i_xn")
                gi = 0
                for grp in range(16):
                    c.need(f"{tag}ldb{grp}")
                    for m2 in range(2):
                        for tb in range(TB):
                            if gi >= 2:
                                c.need(f"{tag}bev{gi - 2}")
                            ins = mm_accum(
                                c, t, PG[gi % 2][:],
                                [WGs[grp % 2][:, k * 256 + m2 * 128:
                                    k * 256 + m2 * 128 + 128]
                                 for k in range(KB)],
                                [XN[:, k * T + tb * 512:
                                    k * T + tb * 512 + 512]
                                 for k in range(KB)])
                            c.done(f"{tag}bmm{gi}", ins, "tt")
                            gi += 1
                    c.alias(f"{tag}bgrp{grp}", f"{tag}bmm{gi - 1}")
                    if grp < 8:
                        for m2 in range(2):
                            dm = grp * 2 + m2
                            c.need(f"{tag}xc{dm}", f"{tag}ldxp{dm}")
                            for tb in range(TB):
                                ins = t.matmul(
                                    PBC[tb][:], XPW[:],
                                    XCB[:, (dm % 2) * T + tb * 512:
                                        (dm % 2) * T + tb * 512 + 512],
                                    start=(dm == 0), stop=(dm == NBK - 1))
                            c.done(f"{tag}bcmm{dm}", ins, "tt")
            c.body("tensor", t_b)

            def v_b(v):
                gi = 0
                for grp in range(16):
                    for m2 in range(2):
                        for tb in range(TB):
                            c.need(f"{tag}bmm{gi}")
                            if grp < 8:
                                dm = grp * 2 + m2
                                base = (dm % 2) * (8 + T)
                                if tb == 0 and dm >= 2:
                                    c.need(f"{tag}cv{dm - 2}")
                                ins = v.tensor_scalar(
                                    XIN[:, base + 8 + tb * 512:
                                        base + 8 + tb * 512 + 512],
                                    PG[gi % 2][:],
                                    INBX[:, dm:dm + 1], None, ALU.add)
                            else:
                                zi = (grp - 8) * 2 + m2
                                if tb == 0 and zi >= 2:
                                    c.need(f"{tag}stz{zi - 2}")
                                ins = v.tensor_scalar(
                                    ZB[:, (zi % 2) * T + tb * 512:
                                       (zi % 2) * T + tb * 512 + 512],
                                    PG[gi % 2][:],
                                    INBZ[:, zi:zi + 1], None, ALU.add)
                                if tb == TB - 1:
                                    c.done(f"{tag}zev{zi}", ins, "vv")
                            c.done(f"{tag}bev{gi}", ins, "vv")
                            gi += 1
                    if grp < 8:
                        for m2 in range(2):
                            dm = grp * 2 + m2
                            base = (dm % 2) * (8 + T)
                            if dm >= 2:
                                c.need(f"{tag}stxc{dm - 2}")
                            if dm < 2:
                                v.memset(XIN[:, base + 5:base + 8], 0.0)
                            v.tensor_scalar(
                                XCF[:], XIN[:, base + 5:base + 5 + T],
                                CVW[:, dm * 4:dm * 4 + 1],
                                CVB[:, dm:dm + 1], ALU.mult, ALU.add)
                            for j in range(1, 4):
                                ins = v.scalar_tensor_tensor(
                                    XCF[:],
                                    XIN[:, base + 5 + j:base + 5 + j + T],
                                    CVW[:, dm * 4 + j:dm * 4 + j + 1],
                                    XCF[:], ALU.mult, ALU.add)
                            c.done(f"{tag}cv{dm}", ins, "vv")
                            c.need(f"{tag}sil{dm}")
                    if grp == 15:
                        c.need(f"{tag}bcmm{NBK - 1}")
                        for tb in range(TB):
                            ins = v.tensor_scalar(
                                BCS[:, tb * 512:tb * 512 + 512],
                                PBC[tb][:], XPBC[:, 0:1], None, ALU.add)
                        c.done(f"{tag}bcall", ins, "vv")
            c.body("vector", v_b)

            def s_b(s):
                for dm in range(NBK):
                    c.need(f"{tag}cv{dm}")
                    if dm >= 2:
                        c.need(f"{tag}bcmm{dm - 2}", f"{tag}stxc{dm - 2}")
                    ins = s.activation(XCB[:, (dm % 2) * T:(dm % 2 + 1) * T],
                                       XCF[:], AF.Silu)
                    c.done(f"{tag}sil{dm}", ins, "ss")
                    c.alias(f"{tag}xc{dm}", f"{tag}sil{dm}")
            c.body("scalar", s_b)

        stAB.close()

        with (
            nc.sbuf_tensor(f"{tag}dwa", [128, 2 * 16 * 128], BF16) as DWB,
            nc.sbuf_tensor(f"{tag}xct", [128, NBK * 512], BF16) as XCT,
            nc.sbuf_tensor(f"{tag}dtt", [128, 2 * 512], F32) as DTT,
            nc.sbuf_tensor(f"{tag}dts", [128, 2 * 512], F32) as DTS,
            nc.psum_tensor(f"{tag}pd0", [128, 512], F32) as PD0,
            nc.psum_tensor(f"{tag}pd1", [128, 512], F32) as PD1,
        ):
            PD = [PD0, PD1]

            def g_c(g):
                bi = 0
                for tb in range(TB):
                    if tb > 0:
                        c.need(f"{tag}dmm{tb - 1}_15")
                    else:
                        c.need(f"{tag}stbc", f"{tag}stxc14", f"{tag}stxc15")
                    c.dma_batch(f"{tag}ldxct{tb}",
                                [(XCT[:, dm * 512:(dm + 1) * 512],
                                  XCD[dm * 128:(dm + 1) * 128,
                                      tb * 512:(tb + 1) * 512])
                                 for dm in range(NBK)], sem="u1")
                    for mg in range(16):
                        if bi >= 2:
                            pj = bi - 2
                            c.need(f"{tag}dmm{pj // 16}_{pj % 16}")
                        c.dma_batch(
                            f"{tag}lddw{tb}_{mg}",
                            [(DWB[:, (bi % 2) * 2048 + k * 128:
                                  (bi % 2) * 2048 + (k + 1) * 128],
                              W[f"{tag}dtT"][k * 128:(k + 1) * 128,
                                             mg * 128:(mg + 1) * 128])
                             for k in range(NBK)],
                            sem=f"w{bi % 2}")
                        bi += 1
            c.body("gpsimd", g_c)

            def t_c(t):
                bi = 0
                for tb in range(TB):
                    c.need(f"{tag}ldxct{tb}")
                    for mg in range(16):
                        c.need(f"{tag}lddw{tb}_{mg}")
                        if bi >= 2:
                            pj = bi - 2
                            c.need(f"{tag}dtab{pj // 16}_{pj % 16}")
                        ins = mm_accum(
                            c, t, PD[bi % 2][:],
                            [DWB[:, (bi % 2) * 2048 + k * 128:
                                 (bi % 2) * 2048 + (k + 1) * 128]
                             for k in range(NBK)],
                            [XCT[:, k * 512:(k + 1) * 512]
                             for k in range(NBK)])
                        c.done(f"{tag}dmm{tb}_{mg}", ins, "tt")
                        bi += 1
            c.body("tensor", t_c)

            def v_c(v):
                bi = 0
                for tb in range(TB):
                    for mg in range(16):
                        c.need(f"{tag}dmm{tb}_{mg}")
                        jj = bi % 2
                        if bi >= 2:
                            c.need(f"{tag}stdt{(bi - 2) // 16}_{(bi - 2) % 16}")
                        v.tensor_scalar(DTS[:, jj * 512:(jj + 1) * 512],
                                        PD[jj][:], DTBC[:, mg:mg + 1], None,
                                        ALU.add)
                        v.tensor_scalar_mul(
                            DTT[:, jj * 512:(jj + 1) * 512],
                            DTS[:, jj * 512:(jj + 1) * 512], -1.0)
                        ins = v.tensor_tensor(
                            DTT[:, jj * 512:(jj + 1) * 512],
                            DTT[:, jj * 512:(jj + 1) * 512],
                            DTS[:, jj * 512:(jj + 1) * 512], ALU.max)
                        c.done(f"{tag}dtab{tb}_{mg}", ins, "vv")
                        c.need(f"{tag}dtln{tb}_{mg}")
                        v.scalar_tensor_tensor(
                            DTS[:, jj * 512:(jj + 1) * 512],
                            DTS[:, jj * 512:(jj + 1) * 512], 0.0,
                            DTT[:, jj * 512:(jj + 1) * 512],
                            ALU.max, ALU.add)
                        ins = v.tensor_scalar(
                            DTS[:, jj * 512:(jj + 1) * 512],
                            DTS[:, jj * 512:(jj + 1) * 512],
                            1e-4, 10.0, ALU.max, ALU.min)
                        c.done(f"{tag}dtcl{tb}_{mg}", ins, "vv")
                        bi += 1
            c.body("vector", v_c)

            def s_c(s):
                for tb in range(TB):
                    for mg in range(16):
                        jj = (tb * 16 + mg) % 2
                        c.need(f"{tag}dtab{tb}_{mg}")
                        s.activation(DTT[:, jj * 512:(jj + 1) * 512],
                                     DTT[:, jj * 512:(jj + 1) * 512],
                                     AF.Exp, scale=-1.0)
                        ins = s.activation(DTT[:, jj * 512:(jj + 1) * 512],
                                           DTT[:, jj * 512:(jj + 1) * 512],
                                           AF.Ln, bias=1.0)
                        c.done(f"{tag}dtln{tb}_{mg}", ins, "ss")
            c.body("scalar", s_c)

            def g_cst(sy):
                for tb in range(TB):
                    for mg in range(16):
                        jj = (tb * 16 + mg) % 2
                        c.need(f"{tag}dtcl{tb}_{mg}")
                        c.dma_batch(f"{tag}stdt{tb}_{mg}",
                                    [(DTD[mg * 128:(mg + 1) * 128,
                                          tb * 512:(tb + 1) * 512],
                                      DTS[:, jj * 512:(jj + 1) * 512])],
                                    sem=f"y{(tb * 16 + mg) % 2}")
            c.body("sync", g_cst)

        OPWF = est.enter_context(
            nc.sbuf_tensor(f"{tag}opwf", [128, NBK * 1024], BF16,
                           side="right"))

        def g_dop(g):
            c.need(f"{tag}bgrp15")
            c.dma_batch(f"{tag}ldopf",
                        [(OPWF[:, k * 1024:(k + 1) * 1024],
                          W[f"{tag}outT"][k * 128:(k + 1) * 128, :])
                         for k in range(NBK)])
        c.body("gpsimd", g_dop)

        with contextlib.ExitStack() as psD:
            DTW = psD.enter_context(
                nc.sbuf_tensor(f"{tag}dtw", [128, NBK * WIN], F32))
            XCW = psD.enter_context(
                nc.sbuf_tensor(f"{tag}xcw", [128, NBK * WIN], BF16))
            ZW = psD.enter_context(
                nc.sbuf_tensor(f"{tag}zw", [128, NBK * WIN], BF16))
            BROW = psD.enter_context(
                nc.sbuf_tensor(f"{tag}brw", [1, NS * WIN], BF16))
            CROW = psD.enter_context(
                nc.sbuf_tensor(f"{tag}crw", [1, NS * WIN], BF16))
            DAT = psD.enter_context(
                nc.sbuf_tensor(f"{tag}dat", [128, 4096], F32))
            DBX = psD.enter_context(
                nc.sbuf_tensor(f"{tag}dbx", [128, 4096], BF16))
            HOUT = psD.enter_context(
                nc.sbuf_tensor(f"{tag}hout", [128, 4096], F32))
            YW = psD.enter_context(
                nc.sbuf_tensor(f"{tag}yw", [128, NBK * WIN], F32))
            BRS = psD.enter_context(
                nc.sbuf_tensor(f"{tag}brs", [128, 256], F32))
            CRS = psD.enter_context(
                nc.sbuf_tensor(f"{tag}crs", [128, 256], F32))
            CAR = psD.enter_context(
                nc.sbuf_tensor(f"{tag}car", [128, 256], F32))
            DXS = psD.enter_context(
                nc.sbuf_tensor(f"{tag}dxs", [128, 256], F32))
            HSQ = psD.enter_context(
                nc.sbuf_tensor(f"{tag}hsq", [128, 256], F32))
            TMPB = psD.enter_context(
                nc.sbuf_tensor(f"{tag}tmp", [128, 256], F32))
            HNR = psD.enter_context(
                nc.sbuf_tensor(f"{tag}hnr", [128, 16], F32))
            YG = psD.enter_context(
                nc.sbuf_tensor(f"{tag}ygb", [128, NBK * WIN], BF16))
            X0W = psD.enter_context(
                nc.sbuf_tensor(f"{tag}x0w", [128, KB * WIN], BF16))
            SMB = psD.enter_context(
                nc.sbuf_tensor(f"{tag}smb", [128, KB * WIN], F32))
            SQW = psD.enter_context(
                nc.sbuf_tensor(f"{tag}sqw", [128, 2 * WIN], F32))
            LNR = psD.enter_context(
                nc.sbuf_tensor(f"{tag}lnr", [1, 4 * WIN], F32))
            PS = [psD.enter_context(
                nc.psum_tensor(f"{tag}psd{i}", [128, 256], F32))
                for i in range(8)]
            d3 = DAT.ap().rearrange("p (dm n t) -> p dm n t",
                                    dm=16, n=16, t=16)
            b3 = DBX.ap().rearrange("p (dm n t) -> p dm n t",
                                    dm=16, n=16, t=16)
            h3 = HOUT.ap().rearrange("p (dm n t) -> p dm n t",
                                     dm=16, n=16, t=16)
            brow3 = BROW.ap().rearrange("p (n t) -> p n t", n=NS, t=WIN)
            crow3 = CROW.ap().rearrange("p (n t) -> p n t", n=NS, t=WIN)

            def g_d(g):
                for w in range(NWIN):
                    lb = f"{tag}w{w}_"
                    if w == 0:
                        c.need(f"{tag}stdt{TB - 1}_14", f"{tag}stdt{TB - 1}_15",
                               f"{tag}stz14", f"{tag}stz15",
                               f"{tag}stxc14", f"{tag}stxc15",
                               f"{tag}stbc", f"{tag}stx0")
                    else:
                        c.need(f"{tag}w{w - 1}_sumb")
                    wc = slice(w * WIN, (w + 1) * WIN)
                    pairs = [(DTW[:, dm * WIN:(dm + 1) * WIN],
                              DTD[dm * 128:(dm + 1) * 128, wc])
                             for dm in range(NBK)]
                    pairs += [(XCW[:, dm * WIN:(dm + 1) * WIN],
                               XCD[dm * 128:(dm + 1) * 128, wc])
                              for dm in range(NBK)]
                    pairs += [(ZW[:, dm * WIN:(dm + 1) * WIN],
                               ZD[dm * 128:(dm + 1) * 128, wc])
                              for dm in range(NBK)]
                    pairs += [(brow3, BCD[0:16, wc].unsqueeze(0)),
                              (crow3, BCD[16:32, wc].unsqueeze(0))]
                    pairs += [(X0W[:, k * WIN:(k + 1) * WIN],
                               X0D[k * 128:(k + 1) * 128, wc])
                              for k in range(KB)]
                    c.dma_batch(lb + "ld", pairs, sem=f"u{w % 2}")
            c.body("gpsimd", g_d)

            def t_d(t):
                for w in range(NWIN):
                    lb = f"{tag}w{w}_"
                    c.need(lb + "ld")
                    if w == 0:
                        c.need(f"{tag}amat", f"{tag}ones1", f"{tag}ldopf")
                    for s_ in range(NSEG):
                        sl = f"{lb}s{s_}_"
                        if s_ > 0:
                            c.need(f"{lb}s{s_ - 1}_bcev")
                        elif w > 0:
                            c.need(f"{tag}w{w - 1}_used")
                        ins = t.matmul(
                            PS[0][:].rearrange("p (a b) -> p a b",
                                               a=16, b=16),
                            ONESRB[:],
                            brow3[:, :, s_ * SEG:(s_ + 1) * SEG],
                            start=True, stop=True)
                        c.done(sl + "brep", ins, "tt")
                        ins = t.matmul(
                            PS[1][:].rearrange("p (a b) -> p a b",
                                               a=16, b=16),
                            ONESRB[:],
                            crow3[:, :, s_ * SEG:(s_ + 1) * SEG],
                            start=True, stop=True)
                        c.done(sl + "crep", ins, "tt")
                    c.need(lb + "yg")
                    for m in range(KB):
                        if w > 0:
                            c.need(f"{tag}w{w - 1}_oev{m}")
                        ins = mm_accum(
                            c, t, PS[m][:, 0:WIN],
                            [OPWF[:, k * 1024 + m * 128:
                                  k * 1024 + m * 128 + 128]
                             for k in range(NBK)],
                            [YG[:, k * WIN:(k + 1) * WIN]
                             for k in range(NBK)])
                        c.done(lb + f"omm{m}", ins, "tt")
                    c.need(lb + "sumb")
                    for k in range(KB):
                        ins = t.matmul(PS[0][0:1, 0:WIN], ONESCF[:, 0:1],
                                       SMB[:, k * WIN:(k + 1) * WIN],
                                       start=(k == 0), stop=(k == KB - 1))
                    c.done(lb + "lnsx", ins, "tt")
                    c.need(lb + "sxev")
                    for k in range(KB):
                        c.need(lb + f"lsq{k}")
                        ins = t.matmul(PS[0][0:1, 0:WIN], ONESCF[:, 0:1],
                                       SQW[:, (k % 2) * WIN:
                                           (k % 2 + 1) * WIN],
                                       start=(k == 0), stop=(k == KB - 1))
                        c.done(lb + f"lnsqk{k}", ins, "tt")
                    c.alias(lb + "lnsq", lb + f"lnsqk{KB - 1}")
                    c.need(lb + "rowab")
                    ins = t.matmul(PS[0][:, 0:WIN], ONESRF[:],
                                   LNR[:, 2 * WIN:3 * WIN],
                                   start=True, stop=True)
                    c.done(lb + "bca", ins, "tt")
                    ins = t.matmul(PS[1][:, 0:WIN], ONESRF[:],
                                   LNR[:, 3 * WIN:4 * WIN],
                                   start=True, stop=True)
                    c.done(lb + "bcb", ins, "tt")
            c.body("tensor", t_d)

            def v_d(v):
                v.memset(CAR[:], 0.0)
                for w in range(NWIN):
                    lb = f"{tag}w{w}_"
                    for s_ in range(NSEG):
                        sl = f"{lb}s{s_}_"
                        ts0 = s_ * SEG
                        if s_ > 0:
                            c.need(f"{lb}s{s_ - 1}_dbx0",
                                   f"{lb}s{s_ - 1}_cmul")
                        elif w > 0:
                            c.need(f"{tag}w{w - 1}_s{NSEG - 1}_dbx0",
                                   f"{tag}w{w - 1}_s{NSEG - 1}_cmul")
                        c.need(sl + "brep")
                        v.tensor_copy(BRS[:], PS[0][:])
                        c.need(sl + "crep")
                        ins = v.tensor_copy(CRS[:], PS[1][:])
                        c.done(sl + "bcev", ins, "vv")
                        dtb_ap = DTW.ap().rearrange(
                            "p (dm t) -> p dm t", dm=16, t=WIN)[
                            :, :, ts0:ts0 + SEG].unsqueeze(2).broadcast_to(
                            [128, 16, 16, 16])
                        amat_ap = AMAT.ap().rearrange(
                            "p (dm n) -> p dm n", dm=16,
                            n=16).unsqueeze(3).broadcast_to([128, 16, 16, 16])
                        ins = v.tensor_tensor(d3, dtb_ap, amat_ap, ALU.mult)
                        c.done(sl + "da0", ins, "vv")
                        xcw_ap = XCW.ap().rearrange(
                            "p (dm t) -> p dm t", dm=16, t=WIN)[
                            :, :, ts0:ts0 + SEG]
                        dtw_ap = DTW.ap().rearrange(
                            "p (dm t) -> p dm t", dm=16, t=WIN)[
                            :, :, ts0:ts0 + SEG]
                        dxs3 = DXS.ap().rearrange("p (a b) -> p a b",
                                                  a=16, b=16)
                        ins = v.tensor_tensor(dxs3, dtw_ap, xcw_ap, ALU.mult)
                        c.done(sl + "dx", ins, "vv")
                        dx_ap = DXS.ap().rearrange(
                            "p (dm t) -> p dm t", dm=16, t=16).unsqueeze(
                            2).broadcast_to([128, 16, 16, 16])
                        brs_ap = BRS.ap().rearrange(
                            "p (n t) -> p n t", n=16, t=16).unsqueeze(
                            1).broadcast_to([128, 16, 16, 16])
                        ins = v.tensor_tensor(b3, dx_ap, brs_ap, ALU.mult)
                        c.done(sl + "dbx0", ins, "vv")
                        c.need(sl + "daexp")
                        v.tensor_tensor(TMPB.ap().rearrange(
                            "p (a b) -> p a b", a=16, b=16),
                            d3[:, :, :, 0:1].squeeze(3),
                            CAR.ap().rearrange("p (a b) -> p a b",
                                               a=16, b=16),
                            ALU.mult)
                        v.tensor_tensor(
                            b3[:, :, :, 0:1].squeeze(3),
                            b3[:, :, :, 0:1].squeeze(3),
                            TMPB.ap().rearrange("p (a b) -> p a b",
                                                a=16, b=16),
                            ALU.add)
                        ins = v.memset(d3[:, :, :, 0:1].squeeze(3), 0.0)
                        c.done(sl + "da_z", ins, "vv")
                        ins = v.tensor_tensor_scan(HOUT[:], DAT[:], DBX[:],
                                                   0.0, ALU.mult, ALU.add)
                        c.done(sl + "scan", ins, "vv")
                        hend = h3[:, :, :, 15:16].squeeze(3)
                        hsq3 = HSQ.ap().rearrange("p (dm n) -> p dm n",
                                                  dm=16, n=16)
                        v.tensor_tensor(hsq3, hend, hend, ALU.mult)
                        ins = v.tensor_reduce(HNR[:, 0:16], hsq3, AX.X,
                                              ALU.add)
                        c.done(sl + "hn2", ins, "vv")
                        c.need(sl + "hnsq")
                        v.tensor_scalar(HNR[:, 0:16], HNR[:, 0:16], 10.0, 1.0,
                                        ALU.mult, ALU.min)
                        scl_ap = HNR.ap()[:, 0:16].unsqueeze(2).broadcast_to(
                            [128, 16, 16])
                        ins = v.tensor_tensor(
                            CAR.ap().rearrange("p (a b) -> p a b",
                                               a=16, b=16),
                            h3[:, :, :, 15:16].squeeze(3),
                            scl_ap, ALU.mult)
                        c.done(sl + "car", ins, "vv")
                        crs_ap = CRS.ap().rearrange(
                            "p (n t) -> p n t", n=16, t=16).unsqueeze(
                            1).broadcast_to([128, 16, 16, 16])
                        ins = v.tensor_tensor(b3, h3, crs_ap, ALU.mult)
                        c.done(sl + "cmul", ins, "vv")
                        v.tensor_tensor(b3[:, :, 0:8, :], b3[:, :, 0:8, :],
                                        b3[:, :, 8:16, :], ALU.add)
                        v.tensor_tensor(b3[:, :, 0:4, :], b3[:, :, 0:4, :],
                                        b3[:, :, 4:8, :], ALU.add)
                        v.tensor_tensor(b3[:, :, 0:2, :], b3[:, :, 0:2, :],
                                        b3[:, :, 2:4, :], ALU.add)
                        v.tensor_tensor(b3[:, :, 0:1, :], b3[:, :, 0:1, :],
                                        b3[:, :, 1:2, :], ALU.add)
                        yw_ap = YW.ap().rearrange("p (dm t) -> p dm t",
                                                  dm=16, t=WIN)[
                            :, :, ts0:ts0 + SEG]
                        dcol_ap = DCOL.ap().unsqueeze(2).broadcast_to(
                            [128, 16, 16])
                        v.tensor_tensor(TMPB.ap().rearrange(
                            "p (a b) -> p a b", a=16, b=16),
                            xcw_ap, dcol_ap, ALU.mult)
                        ins = v.tensor_tensor(
                            yw_ap, b3[:, :, 0:1, :].squeeze(2),
                            TMPB.ap().rearrange("p (a b) -> p a b",
                                                a=16, b=16),
                            ALU.add)
                        c.done(sl + "y", ins, "vv")
                    c.need(lb + "zsil")
                    ins = v.tensor_tensor(YG[:], YW[:], ZW[:], ALU.mult)
                    c.done(lb + "yg", ins, "vv")
                    for m in range(KB):
                        c.need(lb + f"omm{m}")
                        v.tensor_scalar(SMB[:, m * WIN:(m + 1) * WIN],
                                        PS[m][:, 0:WIN], OPBC[:, m:m + 1],
                                        None, ALU.add)
                        ins = v.tensor_tensor(SMB[:, m * WIN:(m + 1) * WIN],
                                              SMB[:, m * WIN:(m + 1) * WIN],
                                              X0W[:, m * WIN:(m + 1) * WIN],
                                              ALU.add)
                        c.done(lb + f"oev{m}", ins, "vv")
                    c.alias(lb + "sumb", lb + f"oev{KB - 1}")
                    c.need(lb + "lnsx")
                    ins = v.tensor_copy(LNR[:, 0:WIN], PS[0][0:1, 0:WIN])
                    c.done(lb + "sxev", ins, "vv")
                    ins = v.tensor_scalar_mul(LNR[:, WIN:2 * WIN],
                                              LNR[:, 0:WIN], 1.0 / D)
                    c.done(lb + "lnm", ins, "vv")
                    c.need(lb + "lnsq", lb + "lmsq")
                    ins = v.scalar_tensor_tensor(
                        LNR[:, 2 * WIN:3 * WIN], PS[0][0:1, 0:WIN], 1.0 / D,
                        LNR[:, 3 * WIN:4 * WIN], ALU.mult, ALU.subtract)
                    c.done(lb + "lvar", ins, "vv")
                    c.need(lb + "lsd")
                    ins = v.reciprocal(LNR[:, 2 * WIN:3 * WIN],
                                       LNR[:, 2 * WIN:3 * WIN])
                    c.done(lb + "lra", ins, "vv")
                    ins = v.scalar_tensor_tensor(
                        LNR[:, 3 * WIN:4 * WIN], LNR[:, WIN:2 * WIN], -1.0,
                        LNR[:, 2 * WIN:3 * WIN], ALU.mult, ALU.mult)
                    c.done(lb + "rowab", ins, "vv")
                    c.need(lb + "bca", lb + "bcb")
                    for k in range(KB):
                        v.tensor_tensor(SQW[:, 0:WIN],
                                        SMB[:, k * WIN:(k + 1) * WIN],
                                        PS[0][:, 0:WIN], ALU.mult)
                        v.tensor_tensor(SQW[:, 0:WIN], SQW[:, 0:WIN],
                                        PS[1][:, 0:WIN], ALU.add)
                        v.tensor_scalar(SQW[:, 0:WIN], SQW[:, 0:WIN],
                                        NAC[:, k:k + 1], NBCC[:, k:k + 1],
                                        ALU.mult, ALU.add)
                        ins = v.tensor_tensor(
                            X[:, k * T + w * WIN:k * T + w * WIN + WIN],
                            X[:, k * T + w * WIN:k * T + w * WIN + WIN],
                            SQW[:, 0:WIN], ALU.add)
                        c.done(lb + f"xadd{k}", ins, "vv")
                    c.alias(lb + "used", lb + f"xadd{KB - 1}")
            c.body("vector", v_d)



            def s_d(s):
                for w in range(NWIN):
                    lb = f"{tag}w{w}_"
                    for s_ in range(NSEG):
                        sl = f"{lb}s{s_}_"
                        c.need(sl + "da0")
                        ins = s.activation(DAT[:], DAT[:], AF.Exp)
                        c.done(sl + "daexp", ins, "ss")
                        c.need(sl + "hn2")
                        s.activation(HNR[:, 0:16], HNR[:, 0:16], AF.Ln)
                        ins = s.activation(HNR[:, 0:16], HNR[:, 0:16],
                                           AF.Exp, scale=-0.5)
                        c.done(sl + "hnsq", ins, "ss")
                    c.need(lb + "ld")
                    if w > 0:
                        c.need(f"{tag}w{w - 1}_yg")
                    ins = s.activation(ZW[:], ZW[:], AF.Silu)
                    c.done(lb + "zsil", ins, "ss")
                    for k in range(KB):
                        c.need(lb + f"oev{k}")
                        if k >= 2:
                            c.need(lb + f"lnsqk{k - 2}")
                        ins = s.activation(SQW[:, (k % 2) * WIN:
                                               (k % 2 + 1) * WIN],
                                           SMB[:, k * WIN:(k + 1) * WIN],
                                           AF.Square)
                        c.done(lb + f"lsq{k}", ins, "ss")
                    c.need(lb + "lnm")
                    ins = s.activation(LNR[:, 3 * WIN:4 * WIN],
                                       LNR[:, WIN:2 * WIN], AF.Square)
                    c.done(lb + "lmsq", ins, "ss")
                    c.need(lb + "lvar")
                    ins = s.activation(LNR[:, 2 * WIN:3 * WIN],
                                       LNR[:, 2 * WIN:3 * WIN], AF.Sqrt,
                                       bias=1e-5)
                    c.done(lb + "lsd", ins, "ss")
            c.body("scalar", s_d)

        def st_body(sy):
            for w in range(NWIN):
                for k in range(KB):
                    c.need(f"{tag}w{w}_xadd{k}")
            c.dma_batch(f"{tag}sty",
                        [(W["yT"][k * 128:(k + 1) * 128, :],
                          X[:, k * T:(k + 1) * T]) for k in range(KB)])
            sy.wait_ge(c.sems["qs"].h, c.sems["qs"].n)
        c.body("sync", st_body)


def _prep_s3_inputs(ins, tag):
    def colt(v, nb):
        return np.ascontiguousarray(v.reshape(nb, 128).T)
    out = {
        f"{tag}winT": np.ascontiguousarray(ins["L0_attn_b_in_proj_w"].T),
        f"{tag}ibx": colt(ins["L0_attn_b_in_proj_b"][:2048], 16),
        f"{tag}ibz": colt(ins["L0_attn_b_in_proj_b"][2048:], 16),
        f"{tag}cvw": np.ascontiguousarray(
            ins["L0_attn_b_conv_w"].reshape(16, 128, 4).transpose(1, 0, 2)
            .reshape(128, 64)),
        f"{tag}cvb": colt(ins["L0_attn_b_conv_b"], 16),
        f"{tag}xprojT": np.ascontiguousarray(ins["L0_attn_b_x_proj_w"].T),
        f"{tag}xpb": np.ascontiguousarray(
            ins["L0_attn_b_x_proj_b"].reshape(32, 1)),
        f"{tag}dtT": np.ascontiguousarray(ins["L0_attn_b_dt_proj_w"].T),
        f"{tag}dtb": colt(ins["L0_attn_b_dt_proj_b"], 16),
        f"{tag}alog": np.ascontiguousarray(
            ins["L0_attn_b_A_log"].reshape(16, 128, 16).transpose(1, 0, 2)
            .reshape(128, 256)),
        f"{tag}dcol": colt(ins["L0_attn_b_D"], 16),
        f"{tag}outT": np.ascontiguousarray(ins["L0_attn_b_out_proj_w"].T),
        f"{tag}opb": colt(ins["L0_attn_b_out_proj_b"], 8),
        f"{tag}ina": colt(ins["L0_attn_b_in_norm_alpha"], 8),
        f"{tag}inb": colt(ins["L0_attn_b_in_norm_bias"], 8),
        f"{tag}na": colt(ins["L0_attn_b_norm_alpha"], 8),
        f"{tag}nb": colt(ins["L0_attn_b_norm_bias"], 8),
    }
    return out


def s3_decls(tag):
    return [
        ("xT", [D, T], F32, "ExternalInput"),
        (f"{tag}winT", [D, 4096], F32, "ExternalInput"),
        (f"{tag}ibx", [128, 16], F32, "ExternalInput"),
        (f"{tag}ibz", [128, 16], F32, "ExternalInput"),
        (f"{tag}cvw", [128, 64], F32, "ExternalInput"),
        (f"{tag}cvb", [128, 16], F32, "ExternalInput"),
        (f"{tag}xprojT", [2048, 32], F32, "ExternalInput"),
        (f"{tag}xpb", [32, 1], F32, "ExternalInput"),
        (f"{tag}dtT", [2048, 2048], F32, "ExternalInput"),
        (f"{tag}dtb", [128, 16], F32, "ExternalInput"),
        (f"{tag}alog", [128, 256], F32, "ExternalInput"),
        (f"{tag}dcol", [128, 16], F32, "ExternalInput"),
        (f"{tag}outT", [2048, 1024], F32, "ExternalInput"),
        (f"{tag}opb", [128, 8], F32, "ExternalInput"),
        (f"{tag}ina", [128, 8], F32, "ExternalInput"),
        (f"{tag}inb", [128, 8], F32, "ExternalInput"),
        (f"{tag}na", [128, 8], F32, "ExternalInput"),
        (f"{tag}nb", [128, 8], F32, "ExternalInput"),
        ("yT", [D, T], F32, "ExternalOutput"),
    ]


def run_s3(x, ins):
    nc = build_single(lambda c, W: emit_s3(c, W, "s3_"), s3_decls("s3_"))
    shared = _prep_s3_inputs(ins, "s3_")
    in_maps = []
    for b in range(B):
        m = dict(shared)
        m["xT"] = np.ascontiguousarray(x[b].T)
        in_maps.append(m)
    res = run_bass_kernel_spmd(nc, in_maps, core_ids=list(range(NCORES)))
    return np.stack([np.ascontiguousarray(r["yT"].T) for r in res.results])



def emit_s5(c, W, tag):
    nc = c.nc
    NH, HD, NS, Q, NCH = 32, 64, 16, 128, 16
    for v in (-1.0, 1.0, 1e-5, -0.5, 1.0 / 2048.0):
        c.const(v)
    est = contextlib.ExitStack()
    with est:
        X = est.enter_context(nc.sbuf_tensor(f"{tag}X", [128, KB * T], F32))
        EYE = est.enter_context(nc.sbuf_tensor(f"{tag}eyeb", [128, 128], BF16))
        EYEF = est.enter_context(nc.sbuf_tensor(f"{tag}eyefs", [128, 128], F32))
        MSK01 = est.enter_context(nc.sbuf_tensor(f"{tag}m01", [128, 128], BF16))
        ONESRB = est.enter_context(nc.sbuf_tensor(f"{tag}orb", [1, 128], BF16))
        ONESRF = est.enter_context(nc.sbuf_tensor(f"{tag}orf", [1, 128], F32))
        LSC = est.enter_context(nc.sbuf_tensor(f"{tag}lscs", [128, 8], F32))
        XN = est.enter_context(
            nc.sbuf_tensor(f"{tag}xnb", [128, KB * T], BF16, side="right"))
        WCD = {dd: nc.dram_tensor(f"{tag}wc{dd}", [2048, 1024], BF16,
                                  kind="Internal").ap() for dd in "fb"}
        YND = {dd: nc.dram_tensor(f"{tag}yn{dd}", [2048, T], BF16,
                                  kind="Internal").ap() for dd in "fb"}

        chain_gate = getattr(c, "chain_gate", None)

        def qload0(g):
            if chain_gate:
                c.need(chain_gate)
            pairs = [(X[:, k * T:(k + 1) * T],
                      W["xT"][k * 128:(k + 1) * 128, :]) for k in range(KB)]
            pairs += [(EYE[:], W["eye"]), (EYEF[:], W[f"{tag}eyef"]),
                      (MSK01[:], W[f"{tag}msk01"]), (LSC[:], W[f"{tag}lsc"])]
            c.dma_batch(f"{tag}in0", pairs)
            for k in range(KB):
                c.alias(f"{tag}ln_in{k}", f"{tag}in0")
            g.memset(ONESRB[:], 1.0)
            ins = g.memset(ONESRF[:], 1.0)
            c.done(f"{tag}ones", ins, "gg")
        c.body("gpsimd", qload0)

        emit_ln3(c, X, XN, f"{tag}ln_")

        def v_addxn(v):
            c.need(f"{tag}ln_xn")
            ins = None
            for k in range(KB):
                ins = v.tensor_tensor(X[:, k * T:(k + 1) * T],
                                      X[:, k * T:(k + 1) * T],
                                      XN[:, k * T:(k + 1) * T], ALU.add)
            c.done(f"{tag}xadd", ins, "vv")
        c.body("vector", v_addxn)

        def wcn(ge):
            dd2, e2 = ("f", ge) if ge < 16 else ("b", ge - 16)
            return f"{tag}wcev_{dd2}{e2}", f"{tag}stwc_{dd2}{e2}"

        with contextlib.ExitStack() as wcs:
            OPN = wcs.enter_context(
                nc.sbuf_tensor(f"{tag}opn", [128, KB * 2048], BF16))
            LOT = wcs.enter_context(
                nc.sbuf_tensor(f"{tag}lot", [128, KB * 1024], BF16))
            WCS = wcs.enter_context(
                nc.sbuf_tensor(f"{tag}wcs", [128, 2 * 1024], BF16))
            PWC0 = wcs.enter_context(
                nc.psum_tensor(f"{tag}pwc0", [128, 1024], F32))
            PWC1 = wcs.enter_context(
                nc.psum_tensor(f"{tag}pwc1", [128, 1024], F32))
            PWC = [PWC0, PWC1]

            def g_wc(g):
                for di, dd in enumerate("fb"):
                    if di == 0:
                        c.need(f"{tag}ln_xn")
                    else:
                        c.need(f"{tag}wcev_f15")
                    c.dma_batch(f"{tag}ldop_{dd}",
                                [(OPN[:, k * 2048:(k + 1) * 2048],
                                  W[f"{tag}op_{dd}"][k * 128:(k + 1) * 128, :])
                                 for k in range(KB)], sem="u0")
                    c.dma_batch(f"{tag}ldlo_{dd}",
                                [(LOT[:, k * 1024:(k + 1) * 1024],
                                  W[f"{tag}loT"][di * 1024 + k * 128:
                                                 di * 1024 + (k + 1) * 128, :])
                                 for k in range(KB)], sem="u1")
            c.body("gpsimd", g_wc)

            def sy_wc(sy):
                for ge in range(32):
                    dd, e = ("f", ge) if ge < 16 else ("b", ge - 16)
                    c.need(f"{tag}wcev_{dd}{e}")
                    c.dma_batch(f"{tag}stwc_{dd}{e}",
                                [(WCD[dd][e * 128:(e + 1) * 128, :],
                                  WCS[:, (ge % 2) * 1024:
                                      (ge % 2 + 1) * 1024])],
                                sem=f"y{ge % 2}")
            c.body("sync", sy_wc)

            def t_wc(t):
                for ge in range(32):
                    dd, e = ("f", ge) if ge < 16 else ("b", ge - 16)
                    if e == 0:
                        c.need(f"{tag}ldop_{dd}", f"{tag}ldlo_{dd}")
                    if ge >= 2:
                        c.need(wcn(ge - 2)[0])
                    ins = None
                    for nb in range(2):
                        for k in range(KB):
                            ins = t.matmul(
                                PWC[ge % 2][:, nb * 512:(nb + 1) * 512],
                                OPN[:, k * 2048 + e * 128:
                                    k * 2048 + e * 128 + 128],
                                LOT[:, k * 1024 + nb * 512:
                                    k * 1024 + nb * 512 + 512],
                                start=(k == 0), stop=(k == KB - 1))
                    c.done(f"{tag}wcmm_{dd}{e}", ins, "tt")
            c.body("tensor", t_wc)

            def v_wc(v):
                for ge in range(32):
                    dd, e = ("f", ge) if ge < 16 else ("b", ge - 16)
                    c.need(f"{tag}wcmm_{dd}{e}")
                    if ge >= 2:
                        c.need(wcn(ge - 2)[1])
                    ins = v.tensor_copy(
                        WCS[:, (ge % 2) * 1024:(ge % 2 + 1) * 1024],
                        PWC[ge % 2][:])
                    c.done(f"{tag}wcev_{dd}{e}", ins, "vv")
            c.body("vector", v_wc)

        def _dir(di, dd):
            p = f"{tag}{dd}_"
            SHDm = nc.dram_tensor(p + "shd", [32, T], F32, kind="Internal").ap()
            LEDm = nc.dram_tensor(p + "led", [32, 16], F32, kind="Internal").ap()
            ZTDm = nc.dram_tensor(p + "ztd", [T, 2048], BF16,
                                  kind="Internal").ap()
            XSDm = nc.dram_tensor(p + "xsd", [T, 2048], BF16,
                                  kind="Internal").ap()

            st2 = contextlib.ExitStack()
            SHT = st2.enter_context(
                nc.sbuf_tensor(p + "sht", [128, NCH * 32], F32, side="right"))
            EXPST = st2.enter_context(
                nc.sbuf_tensor(p + "exst", [128, NCH * 32], F32, side="right"))
            EXPWT = st2.enter_context(
                nc.sbuf_tensor(p + "exwt", [128, NCH * 32], F32, side="right"))
            DTT = st2.enter_context(
                nc.sbuf_tensor(p + "dttk", [128, NCH * 32], BF16, side="right"))
            BSB = st2.enter_context(
                nc.sbuf_tensor(p + "bsbr", [16, T], BF16, side="right"))
            CSB = st2.enter_context(
                nc.sbuf_tensor(p + "csbr", [16, T], BF16, side="right"))
            LAMC = st2.enter_context(
                nc.sbuf_tensor(p + "lamc", [16, 512], F32, side="right"))
            DPREP = st2.enter_context(
                nc.sbuf_tensor(p + "dpr", [128, 2048], BF16, side="right"))
            NWREP = st2.enter_context(
                nc.sbuf_tensor(p + "nwr", [128, 2048], BF16, side="right"))

            s1e = contextlib.ExitStack()
            with s1e:
                if dd == "b":
                    REV = s1e.enter_context(
                        nc.sbuf_tensor(p + "rev", [128, 2048], BF16))

                    def v_rev(v):
                        c.need(f"{tag}f_dirdone", f"{tag}f_c15_ynspill")
                        ins = None
                        for k in range(KB):
                            v.tensor_copy(REV[:], XN[:, k * T:(k + 1) * T])
                            ins = v.tensor_copy(XN[:, k * T:(k + 1) * T],
                                                REV[:, ::-1])
                        c.done(f"{tag}revdone", ins, "vv")
                    c.body("vector", v_rev)
                WG0 = s1e.enter_context(
                    nc.sbuf_tensor(p + "wg0", [128, KB * 256], BF16))
                WG1 = s1e.enter_context(
                    nc.sbuf_tensor(p + "wg1", [128, KB * 256], BF16))
                XIN = s1e.enter_context(
                    nc.sbuf_tensor(p + "xin", [128, 2 * (8 + T)], BF16))
                XCF = s1e.enter_context(nc.sbuf_tensor(p + "xcf", [128, T], F32))
                XSB = s1e.enter_context(
                    nc.sbuf_tensor(p + "xsb", [128, 2 * T], BF16))
                ZTB = s1e.enter_context(
                    nc.sbuf_tensor(p + "ztb", [128, 2 * 256], BF16))
                XTT = s1e.enter_context(
                    nc.sbuf_tensor(p + "xtt", [128, 2 * 128], BF16))
                BCRB = s1e.enter_context(
                    nc.sbuf_tensor(p + "bcrb", [16, 8 + T], BF16))
                BCRC = s1e.enter_context(
                    nc.sbuf_tensor(p + "bcrc", [16, 8 + T], BF16))
                DTR = s1e.enter_context(nc.sbuf_tensor(p + "dtr", [32, T], F32))
                DTH = s1e.enter_context(nc.sbuf_tensor(p + "dth", [32, T], F32))
                SH = s1e.enter_context(nc.sbuf_tensor(p + "sh", [32, T], F32))
                EXPS = s1e.enter_context(nc.sbuf_tensor(p + "exs", [32, T], F32))
                EXPW = DTR
                ONES32 = s1e.enter_context(
                    nc.sbuf_tensor(p + "o32", [32, 128], F32))
                SRO = s1e.enter_context(nc.sbuf_tensor(p + "sro", [32, 32], F32))
                ACOL = s1e.enter_context(
                    nc.sbuf_tensor(p + "acol", [32, 1], F32))
                CVWB = s1e.enter_context(
                    nc.sbuf_tensor(p + "cvwb", [128, 64], F32))
                CVBB = s1e.enter_context(
                    nc.sbuf_tensor(p + "cvbb", [128, 16], F32))
                CVW2B = s1e.enter_context(
                    nc.sbuf_tensor(p + "cvw2b", [16, 4], F32))
                CVW2C = s1e.enter_context(
                    nc.sbuf_tensor(p + "cvw2c", [16, 4], F32))
                CVB2B = s1e.enter_context(
                    nc.sbuf_tensor(p + "cvb2b", [16, 1], F32))
                CVB2C = s1e.enter_context(
                    nc.sbuf_tensor(p + "cvb2c", [16, 1], F32))
                DTBC = s1e.enter_context(
                    nc.sbuf_tensor(p + "dtbi", [32, 1], F32))
                PG0 = s1e.enter_context(
                    nc.psum_tensor(p + "pg0", [128, 512], F32))
                PG1 = s1e.enter_context(
                    nc.psum_tensor(p + "pg1", [128, 512], F32))
                PTX0 = s1e.enter_context(
                    nc.psum_tensor(p + "ptx0", [128, 128], BF16))
                PTX1 = s1e.enter_context(
                    nc.psum_tensor(p + "ptx1", [128, 128], BF16))
                PTF0 = s1e.enter_context(
                    nc.psum_tensor(p + "ptf0", [128, 32], F32))
                PTF1 = s1e.enter_context(
                    nc.psum_tensor(p + "ptf1", [128, 32], F32))
                PRW = s1e.enter_context(
                    nc.psum_tensor(p + "prw", [128, 512], F32))
                scA = contextlib.ExitStack()
                P1 = scA.enter_context(nc.psum_tensor(p + "p1", [32, 512], F32))
                scA.close()
                PLM = s1e.enter_context(
                    nc.psum_tensor(p + "plm", [16, 512], F32))
                WGs = [WG0, WG1]
                PG = [PG0, PG1]
                PTXs = [PTX0, PTX1]
                PTFs = [PTF0, PTF1]

                def g_s1(g):
                    if dd == "f":
                        c.need(f"{tag}wcev_b15",
                               f"{tag}stwc_b14", f"{tag}stwc_b15")
                    else:
                        c.need(f"{tag}revdone", f"{tag}f_c15_ynspill")
                    c.dma_batch(p + "ldcv", [
                        (CVWB[:], W[f"{tag}cvw_{dd}"]),
                        (CVBB[:], W[f"{tag}cvb_{dd}"]),
                        (CVW2B[:], W[f"{tag}cvw2b_{dd}"]),
                        (CVW2C[:], W[f"{tag}cvw2c_{dd}"]),
                        (CVB2B[:], W[f"{tag}cvb2b_{dd}"]),
                        (CVB2C[:], W[f"{tag}cvb2c_{dd}"]),
                        (DTBC[:], W[f"{tag}dtb_{dd}"]),
                        (ACOL[:], W[f"{tag}alog_{dd}"]),
                        (DPREP[0:1, 0:2048], W[f"{tag}dprow_{dd}"]),
                        (NWREP[0:1, 0:2048], W[f"{tag}nwrow_{dd}"]),
                    ])
                    ins = g.memset(ONES32[:], 1.0)
                    c.done(p + "o32", ins, "gg")
                    for zg in range(8):
                        if zg >= 2:
                            c.need(p + f"pgev{zg * 16 - 17}")
                        c.dma_batch(
                            p + f"ldz{zg}",
                            [(WGs[zg % 2][:, k * 256:(k + 1) * 256],
                              W[f"{tag}inT_{dd}"][k * 128:(k + 1) * 128,
                                                  zg * 256:(zg + 1) * 256])
                             for k in range(KB)], sem=f"w{zg % 2}")
                    for xg in range(8):
                        bi = 8 + xg
                        if xg == 0:
                            c.need(p + "pgev111")
                        elif xg == 1:
                            c.need(p + "pgev127")
                        else:
                            c.need(p + f"pgev{128 + (xg - 2) * 8 + 7}")
                        c.dma_batch(
                            p + f"ldxs{xg}",
                            [(WGs[bi % 2][:, k * 256:(k + 1) * 256],
                              W[f"{tag}inT_{dd}"][k * 128:(k + 1) * 128,
                                                  2048 + xg * 256:
                                                  2048 + (xg + 1) * 256])
                             for k in range(KB)], sem=f"w{bi % 2}")
                    c.need(p + "pgev183")
                    c.dma_batch(
                        p + "ldbcdt",
                        [(WG0[:, k * 256:k * 256 + 64],
                          W[f"{tag}inT_{dd}"][k * 128:(k + 1) * 128,
                                              4096:4160])
                         for k in range(KB)], sem="w0")
                    c.need(p + "stle")
                    c.dma_batch(p + "ldlam",
                                [(LAMC[0:1, 0:512],
                                  LEDm.rearrange("a b -> (a b)").unsqueeze(0))])
                c.body("gpsimd", g_s1)

                def sy_s1(sy):
                    for zg in range(8):
                        for tc in range(16):
                            zi = zg * 16 + tc
                            c.need(p + f"pgev{zi}")
                            c.dma_batch(
                                p + f"stz{zi}",
                                [(ZTDm[tc * 128:(tc + 1) * 128,
                                       zg * 256:(zg + 1) * 256],
                                  ZTB[:, (zi % 2) * 256:(zi % 2 + 1) * 256])],
                                sem=f"y{zi % 2}")
                    for dm in range(16):
                        for ch in range(NCH):
                            xi = dm * NCH + ch
                            c.need(p + f"xte{dm}_{ch}")
                            c.dma_batch(
                                p + f"stxt{dm}_{ch}",
                                [(XSDm[ch * 128:(ch + 1) * 128,
                                       dm * 128:(dm + 1) * 128],
                                  XTT[:, (xi % 2) * 128:(xi % 2 + 1) * 128])],
                                sem=f"y{2 + xi % 2}")
                    c.need(p + "shdone")
                    c.dma_batch(p + "stsh", [(SHDm, SH[:])])
                    c.need(p + "lend")
                    c.dma_batch(p + "stle", [(LEDm, SRO[:, 16:32])])
                c.body("sync", sy_s1)

                def emit_xtp(t, dm):
                    c.need(p + f"sil{dm}")
                    for ch in range(NCH):
                        xi = dm * NCH + ch
                        if xi >= 2:
                            pi2 = xi - 2
                            c.need(p + f"xte{pi2 // NCH}_{pi2 % NCH}")
                        ins = t.transpose(
                            PTXs[xi % 2][:],
                            XSB[:, (dm % 2) * T + ch * 128:
                                (dm % 2) * T + ch * 128 + 128],
                            EYE[:])
                        c.done(p + f"xtp{dm}_{ch}", ins, "tt")

                def t_s1(t):
                    if dd == "f":
                        c.need(f"{tag}ln_xn", f"{tag}wcev_b15")
                    else:
                        c.need(f"{tag}revdone")
                    gi = 0
                    for zg in range(8):
                        c.need(p + f"ldz{zg}")
                        for tc in range(16):
                            if gi >= 2:
                                c.need(p + f"pgev{gi - 2}")
                            ins = mm_accum(
                                c, t, PG[gi % 2][:, 0:256],
                                [XN[:, k * T + tc * 128:
                                    k * T + tc * 128 + 128]
                                 for k in range(KB)],
                                [WGs[zg % 2][:, k * 256:(k + 1) * 256]
                                 for k in range(KB)])
                            c.done(p + f"pgr{gi}", ins, "tt")
                            gi += 1
                    for xg in range(8):
                        bi = 8 + xg
                        c.need(p + f"ldxs{xg}")
                        for m2 in range(2):
                            dm = xg * 2 + m2
                            for tb in range(TB):
                                if gi >= 2:
                                    c.need(p + f"pgev{gi - 2}")
                                ins = mm_accum(
                                    c, t, PG[gi % 2][:],
                                    [WGs[bi % 2][:, k * 256 + m2 * 128:
                                        k * 256 + m2 * 128 + 128]
                                     for k in range(KB)],
                                    [XN[:, k * T + tb * 512:
                                        k * T + tb * 512 + 512]
                                     for k in range(KB)])
                                c.done(p + f"pgr{gi}", ins, "tt")
                                gi += 1
                            if dm >= 2:
                                emit_xtp(t, dm - 2)
                    emit_xtp(t, 14)
                    emit_xtp(t, 15)
                    c.need(p + "ldbcdt")
                    for ph, (c0, c1, lbl) in enumerate(
                            ((0, 16, "B"), (16, 32, "C"), (32, 64, "D"))):
                        for tb in range(TB):
                            if tb >= 1:
                                c.need(p + f"bdev{lbl}{tb - 1}")
                            elif ph >= 1:
                                plbl = ("B", "C", "D")[ph - 1]
                                c.need(p + f"bdev{plbl}{TB - 1}")
                            ins = mm_accum(
                                c, t, P1[0:c1 - c0, :],
                                [WG0[:, k * 256 + c0:k * 256 + c1]
                                 for k in range(KB)],
                                [XN[:, k * T + tb * 512:
                                    k * T + tb * 512 + 512]
                                 for k in range(KB)])
                            c.done(p + f"bdmm{lbl}{tb}", ins, "tt")
                    for si, srcname in enumerate(("SH", "EXPS", "EXPW", "DTH")):
                        src = {"SH": SH, "EXPS": EXPS, "EXPW": EXPW,
                               "DTH": DTH}[srcname]
                        ev = {"SH": "shdone", "EXPS": "expsdone",
                              "EXPW": "expwdone", "DTH": "dthdone"}[srcname]
                        c.need(p + ev)
                        for ch in range(NCH):
                            hi = si * NCH + ch
                            if hi >= 2:
                                pi2 = hi - 2
                                c.need(p + f"hte{pi2 // NCH}_{pi2 % NCH}")
                            ins = t.transpose(
                                PTFs[hi % 2][:],
                                src[:, ch * 128:(ch + 1) * 128],
                                EYEF[0:32, 0:32])
                            c.done(p + f"htp{si}_{ch}", ins, "tt")
                    c.need(p + "ldcv", f"{tag}ones")
                    for nb in range(4):
                        if nb >= 1:
                            c.need(p + f"dpev{nb - 1}")
                        ins = t.matmul(PRW[:], ONESRB[:],
                                       DPREP[0:1, nb * 512:(nb + 1) * 512],
                                       start=True, stop=True)
                        c.done(p + f"dpbc{nb}", ins, "tt")
                    for nb in range(4):
                        c.need(p + (f"nwev{nb - 1}" if nb >= 1 else "dpev3"))
                        ins = t.matmul(PRW[:], ONESRB[:],
                                       NWREP[0:1, nb * 512:(nb + 1) * 512],
                                       start=True, stop=True)
                        c.done(p + f"nwbc{nb}", ins, "tt")
                    c.need(p + "ldlam")
                    ins = t.matmul(PLM[:], ONESRF[0:1, 0:16],
                                   LAMC[0:1, 0:512], start=True, stop=True)
                    c.done(p + "lambc", ins, "tt")
                c.body("tensor", t_s1)

                def v_xte(v, dm):
                    for ch in range(NCH):
                        xi = dm * NCH + ch
                        c.need(p + f"xtp{dm}_{ch}")
                        if xi >= 2:
                            pi2 = xi - 2
                            c.need(p + f"stxt{pi2 // NCH}_{pi2 % NCH}")
                        ins = v.tensor_copy(
                            XTT[:, (xi % 2) * 128:(xi % 2 + 1) * 128],
                            PTXs[xi % 2][:])
                        c.done(p + f"xte{dm}_{ch}", ins, "vv")

                def v_s1(v):
                    c.need(p + "ldcv")
                    gi = 0
                    for zg in range(8):
                        for tc in range(16):
                            zi = zg * 16 + tc
                            c.need(p + f"pgr{gi}")
                            if zi >= 2:
                                c.need(p + f"stz{zi - 2}")
                            ins = v.tensor_copy(
                                ZTB[:, (zi % 2) * 256:(zi % 2 + 1) * 256],
                                PG[gi % 2][:, 0:256])
                            c.done(p + f"pgev{gi}", ins, "vv")
                            gi += 1
                    for xg in range(8):
                        for m2 in range(2):
                            dm = xg * 2 + m2
                            base = (dm % 2) * (8 + T)
                            for tb in range(TB):
                                c.need(p + f"pgr{gi}")
                                ins = v.tensor_copy(
                                    XIN[:, base + 8 + tb * 512:
                                        base + 8 + tb * 512 + 512],
                                    PG[gi % 2][:])
                                c.done(p + f"pgev{gi}", ins, "vv")
                                gi += 1
                            if dm >= 2:
                                v_xte(v, dm - 2)
                            if dm >= 1:
                                c.need(p + f"sil{dm - 1}")
                            if dm < 2:
                                v.memset(XIN[:, base + 5:base + 8], 0.0)
                            v.tensor_scalar(
                                XCF[:], XIN[:, base + 5:base + 5 + T],
                                CVWB[:, dm * 4:dm * 4 + 1],
                                CVBB[:, dm:dm + 1], ALU.mult, ALU.add)
                            ins = None
                            for j in range(1, 4):
                                ins = v.scalar_tensor_tensor(
                                    XCF[:],
                                    XIN[:, base + 5 + j:base + 5 + j + T],
                                    CVWB[:, dm * 4 + j:dm * 4 + j + 1],
                                    XCF[:], ALU.mult, ALU.add)
                            c.done(p + f"cv{dm}", ins, "vv")
                    v_xte(v, 14)
                    v_xte(v, 15)
                    for ph, (c0, c1, lbl) in enumerate(
                            ((0, 16, "B"), (16, 32, "C"), (32, 64, "D"))):
                        for tb in range(TB):
                            c.need(p + f"bdmm{lbl}{tb}")
                            if lbl == "B":
                                dst = BCRB[:, 8 + tb * 512:8 + tb * 512 + 512]
                            elif lbl == "C":
                                dst = BCRC[:, 8 + tb * 512:8 + tb * 512 + 512]
                            else:
                                dst = DTR[:, tb * 512:(tb + 1) * 512]
                            ins = v.tensor_copy(dst, P1[0:c1 - c0, :])
                            c.done(p + f"bdev{lbl}{tb}", ins, "vv")
                    c.need(p + "sil15")
                    v.memset(BCRB[:, 5:8], 0.0)
                    v.memset(BCRC[:, 5:8], 0.0)
                    v.tensor_scalar(XCF[0:16, 0:T], BCRB[:, 5:5 + T],
                                    CVW2B[:, 0:1], CVB2B[:, 0:1],
                                    ALU.mult, ALU.add)
                    ins = None
                    for j in range(1, 4):
                        ins = v.scalar_tensor_tensor(
                            XCF[0:16, 0:T], BCRB[:, 5 + j:5 + j + T],
                            CVW2B[:, j:j + 1], XCF[0:16, 0:T],
                            ALU.mult, ALU.add)
                    c.done(p + "bccvB", ins, "vv")
                    c.need(p + "silB")
                    v.tensor_scalar(XCF[0:16, 0:T], BCRC[:, 5:5 + T],
                                    CVW2C[:, 0:1], CVB2C[:, 0:1],
                                    ALU.mult, ALU.add)
                    for j in range(1, 4):
                        ins = v.scalar_tensor_tensor(
                            XCF[0:16, 0:T], BCRC[:, 5 + j:5 + j + T],
                            CVW2C[:, j:j + 1], XCF[0:16, 0:T],
                            ALU.mult, ALU.add)
                    c.done(p + "bccvC", ins, "vv")
                    v.tensor_scalar(DTR[:], DTR[:], DTBC[:, 0:1], None,
                                    ALU.add)
                    v.tensor_scalar_mul(DTH[:], DTR[:], -1.0)
                    ins = v.tensor_tensor(DTH[:], DTH[:], DTR[:], ALU.max)
                    c.done(p + "dtabs", ins, "vv")
                    c.need(p + "dtln")
                    ins = v.scalar_tensor_tensor(DTH[:], DTR[:], 0.0, DTH[:],
                                                 ALU.max, ALU.add)
                    c.done(p + "dthdone", ins, "vv")
                    c.need(p + "aexp")
                    v.tensor_scalar_mul(ACOL[:], ACOL[:], -1.0)
                    ins = v.tensor_scalar(DTR[:], DTH[:], ACOL[:, 0:1], None,
                                          ALU.mult)
                    c.done(p + "dta", ins, "vv")
                    c.need(p + "o32")
                    ins = None
                    for ch in range(NCH):
                        ins = v.tensor_tensor_scan(
                            SH[:, ch * 128:(ch + 1) * 128], ONES32[:],
                            DTR[:, ch * 128:(ch + 1) * 128], 0.0,
                            ALU.mult, ALU.add)
                    c.done(p + "shdone", ins, "vv")
                    sh3 = SH.ap().rearrange("p (ch t) -> p ch t",
                                            ch=NCH, t=Q)
                    ins = v.tensor_copy(SRO[:, 0:16],
                                        sh3[:, :, Q - 1:Q].squeeze(2))
                    c.done(p + "send", ins, "vv")
                    send_b = SRO.ap()[:, 0:16].unsqueeze(2).broadcast_to(
                        [32, 16, Q])
                    ew3 = EXPW.ap().rearrange("p (ch t) -> p ch t",
                                              ch=NCH, t=Q)
                    ins = v.tensor_tensor(ew3, send_b, sh3, ALU.subtract)
                    c.done(p + "ewsub", ins, "vv")
                    c.need(p + "expsdone")
                    ex3 = EXPS.ap().rearrange("p (ch t) -> p ch t",
                                              ch=NCH, t=Q)
                    ins = v.tensor_copy(SRO[:, 16:32],
                                        ex3[:, :, Q - 1:Q].squeeze(2))
                    c.done(p + "lend", ins, "vv")
                    c.need(p + "ewexp")
                    ins = v.tensor_tensor(EXPW[:], EXPW[:], DTH[:], ALU.mult)
                    c.done(p + "expwdone", ins, "vv")
                    for si, dstt in enumerate((SHT, EXPST, EXPWT, DTT)):
                        for ch in range(NCH):
                            hi = si * NCH + ch
                            c.need(p + f"htp{si}_{ch}")
                            ins = v.tensor_copy(
                                dstt[:, ch * 32:(ch + 1) * 32],
                                PTFs[hi % 2][:])
                            c.done(p + f"hte{si}_{ch}", ins, "vv")
                    for nb in range(4):
                        c.need(p + f"dpbc{nb}")
                        ins = v.tensor_copy(
                            DPREP[:, nb * 512:(nb + 1) * 512], PRW[:])
                        c.done(p + f"dpev{nb}", ins, "vv")
                    for nb in range(4):
                        c.need(p + f"nwbc{nb}")
                        ins = v.tensor_copy(
                            NWREP[:, nb * 512:(nb + 1) * 512], PRW[:])
                        c.done(p + f"nwev{nb}", ins, "vv")
                    c.need(p + "lambc")
                    ins = v.tensor_copy(LAMC[:], PLM[:])
                    c.done(p + "lamev", ins, "vv")
                c.body("vector", v_s1)

                def s_s1(s):
                    for dm in range(16):
                        c.need(p + f"cv{dm}")
                        if dm >= 2:
                            c.need(p + f"xtp{dm - 2}_{NCH - 1}")
                        ins = s.activation(
                            XSB[:, (dm % 2) * T:(dm % 2 + 1) * T],
                            XCF[:], AF.Silu)
                        c.done(p + f"sil{dm}", ins, "ss")
                    c.need(p + "bccvB")
                    ins = s.activation(BSB[:], XCF[0:16, 0:T], AF.Silu)
                    c.done(p + "silB", ins, "ss")
                    c.need(p + "bccvC")
                    ins = s.activation(CSB[:], XCF[0:16, 0:T], AF.Silu)
                    c.done(p + "silC", ins, "ss")
                    c.need(p + "dtabs")
                    s.activation(DTH[:], DTH[:], AF.Exp, scale=-1.0)
                    ins = s.activation(DTH[:], DTH[:], AF.Ln, bias=1.0)
                    c.done(p + "dtln", ins, "ss")
                    c.need(p + "ldcv")
                    ins = s.activation(ACOL[:], ACOL[:], AF.Exp)
                    c.done(p + "aexp", ins, "ss")
                    c.need(p + "shdone")
                    ins = s.activation(EXPS[:], SH[:], AF.Exp)
                    c.done(p + "expsdone", ins, "ss")
                    c.need(p + "ewsub")
                    ins = s.activation(EXPW[:], EXPW[:], AF.Exp)
                    c.done(p + "ewexp", ins, "ss")
                c.body("scalar", s_s1)

            with contextlib.ExitStack() as s2e:
                XSW = s2e.enter_context(
                    nc.sbuf_tensor(p + "xsw", [128, 2048], BF16))
                ZTW = s2e.enter_context(
                    nc.sbuf_tensor(p + "ztw", [128, 2048], BF16))
                SHRC = s2e.enter_context(
                    nc.sbuf_tensor(p + "shrc", [1, 4096], F32))
                SCR8 = s2e.enter_context(
                    nc.sbuf_tensor(p + "scr8", [128, 2048], F32))
                MTALL = s2e.enter_context(
                    nc.sbuf_tensor(p + "mtall", [128, 4096], BF16))
                XDEL = s2e.enter_context(
                    nc.sbuf_tensor(p + "xdel", [128, 2048], BF16))
                XW = s2e.enter_context(
                    nc.sbuf_tensor(p + "xw", [128, 2048], BF16))
                Y1 = s2e.enter_context(
                    nc.sbuf_tensor(p + "y1", [128, 2048], F32))
                GMS = s2e.enter_context(
                    nc.sbuf_tensor(p + "gms", [128, 128], BF16))
                BTOK = s2e.enter_context(
                    nc.sbuf_tensor(p + "btok", [128, 16], BF16))
                SST = s2e.enter_context(
                    nc.sbuf_tensor(p + "sst", [16, 2048], F32))
                SB16 = s2e.enter_context(
                    nc.sbuf_tensor(p + "sb16", [16, 2048], BF16))
                RS = s2e.enter_context(nc.sbuf_tensor(p + "rs", [128, 4], F32))
                YN = s2e.enter_context(
                    nc.sbuf_tensor(p + "ynb", [128, 2048], BF16))
                YNT = s2e.enter_context(
                    nc.sbuf_tensor(p + "yntb", [128, 2048], BF16))
                PM4 = s2e.enter_context(
                    nc.psum_tensor(p + "pm4", [128, 2048], F32))
                PDS2 = s2e.enter_context(
                    nc.psum_tensor(p + "pds2", [16, 512], F32))
                PTB = s2e.enter_context(
                    nc.psum_tensor(p + "ptb", [128, 16], BF16))
                PTY0 = s2e.enter_context(
                    nc.psum_tensor(p + "pty0", [128, 128], BF16))
                PTY1 = s2e.enter_context(
                    nc.psum_tensor(p + "pty1", [128, 128], BF16))
                PTYs = [PTY0, PTY1]

                s1done = [p + "stxt15_14", p + "stxt15_15",
                          p + "stz126", p + "stz127",
                          p + "stsh", p + "lamev", p + "nwev3",
                          p + "hte3_15", p + "silC"]

                def g_s2(g):
                    for ch in range(NCH):
                        q = f"{p}c{ch}_"
                        if ch == 0:
                            c.need(*s1done)
                        else:
                            c.need(f"{p}c{ch - 1}_ynspill")
                        c.dma_batch(q + "ld", [
                            (XSW[:], XSDm[ch * 128:(ch + 1) * 128, :]),
                            (ZTW[:], ZTDm[ch * 128:(ch + 1) * 128, :]),
                            (SHRC[0:1, :].rearrange("p (a b) -> p a b",
                                                    a=32, b=128),
                             SHDm[:, ch * 128:(ch + 1) * 128].unsqueeze(0)),
                        ], sem=f"u{ch % 2}")
                c.body("gpsimd", g_s2)

                def sy_s2(sy):
                    for ch in range(NCH):
                        q = f"{p}c{ch}_"
                        cdst = (T - (ch + 1) * Q) if dd == "b" else ch * Q
                        c.need(q + "ynte15")
                        c.dma_batch(q + "ynspill",
                                    [(YND[dd][dm * 128:(dm + 1) * 128,
                                              cdst:cdst + Q],
                                      YNT[:, dm * 128:(dm + 1) * 128])
                                     for dm in range(16)],
                                    sem=f"y{2 + ch % 2}")
                c.body("sync", sy_s2)

                def t_s2(t):
                    for ch in range(NCH):
                        q = f"{p}c{ch}_"
                        c.need(q + "ld")
                        if ch == 0:
                            c.need(p + "silB", p + "silC")
                        else:
                            c.need(f"{p}c{ch - 1}_y2done")
                        ins = t.matmul(
                            PM4[:, 0:128],
                            BSB[:, ch * 128:(ch + 1) * 128],
                            CSB[:, ch * 128:(ch + 1) * 128],
                            start=True, stop=True)
                        c.done(q + "gmm", ins, "tt")
                        ins = t.transpose(
                            PTB[:], BSB[:, ch * 128:(ch + 1) * 128],
                            EYE[0:16, 0:16])
                        c.done(q + "btokt", ins, "tt")
                        for hh in range(2):
                            c.need(q + ("gmev" if hh == 0 else "m1_0"))
                            ins = None
                            for nb in range(4):
                                ins = t.matmul(
                                    PM4[:, nb * 512:(nb + 1) * 512],
                                    ONESRF[:],
                                    SHRC[0:1, hh * 2048 + nb * 512:
                                         hh * 2048 + (nb + 1) * 512],
                                    start=True, stop=True)
                            c.done(q + f"srow{hh}", ins, "tt")
                        c.need(q + "m1_1")
                        c.need((f"{p}c{ch - 1}_sb16" if ch > 0 else
                                p + "sb16i"))
                        ins = None
                        for nb in range(4):
                            ins = t.matmul(
                                PM4[:, nb * 512:(nb + 1) * 512],
                                CSB[:, ch * 128:(ch + 1) * 128],
                                SB16[:, nb * 512:(nb + 1) * 512],
                                start=True, stop=True)
                        c.done(q + "umm", ins, "tt")
                        c.need(q + "xwdone", q + "btokev")
                        for qb in range(4):
                            if qb >= 1:
                                c.need(q + f"supd{qb - 1}")
                            elif ch > 0:
                                c.need(f"{p}c{ch - 1}_supd3")
                            ins = t.matmul(
                                PDS2[:], BTOK[:],
                                XW[:, qb * 512:(qb + 1) * 512],
                                start=True, stop=True)
                            c.done(q + f"dsmm{qb}", ins, "tt")
                        c.need(q + "y1done", q + "xdeldone")
                        ins = None
                        for h in range(NH):
                            ins = t.matmul(
                                PM4[:, h * 64:(h + 1) * 64],
                                MTALL[:, h * 128:(h + 1) * 128],
                                XDEL[:, h * 64:(h + 1) * 64],
                                start=True, stop=True)
                        c.done(q + "yimm", ins, "tt")
                        c.need(q + "yn")
                        for dm in range(16):
                            if dm >= 2:
                                c.need(q + f"ynte{dm - 2}")
                            elif ch > 0:
                                c.need(f"{p}c{ch - 1}_ynte{14 + dm}")
                            ins = t.transpose(
                                PTYs[dm % 2][:],
                                YN[:, dm * 128:(dm + 1) * 128], EYE[:])
                            c.done(q + f"yntp{dm}", ins, "tt")
                c.body("tensor", t_s2)

                def v_s2(v):
                    v.memset(SST[:], 0.0)
                    ins = v.tensor_copy(SB16[:], SST[:])
                    c.done(p + "sb16i", ins, "vv")
                    for ch in range(NCH):
                        q = f"{p}c{ch}_"
                        c.need(q + "gmm")
                        ins = v.tensor_tensor(GMS[:], PM4[:, 0:128], MSK01[:],
                                              ALU.mult)
                        c.done(q + "gmev", ins, "vv")
                        c.need(q + "btokt")
                        ins = v.tensor_copy(BTOK[:], PTB[:])
                        c.done(q + "btokev", ins, "vv")
                        scr3 = SCR8.ap().rearrange("p (h t) -> p h t",
                                                   h=16, t=Q)
                        gms_b = GMS.ap().unsqueeze(1).broadcast_to(
                            [128, 16, Q])
                        for hh in range(2):
                            c.need(q + f"srow{hh}")
                            if hh == 0 and ch > 0:
                                c.need(f"{p}c{ch - 1}_sq")
                            sht_b = SHT.ap()[:, ch * 32 + hh * 16:
                                             ch * 32 + (hh + 1) * 16] \
                                .unsqueeze(2).broadcast_to([128, 16, Q])
                            pm3 = PM4[:].rearrange("p (h t) -> p h t",
                                                   h=16, t=Q)
                            v.tensor_tensor(scr3, pm3, sht_b, ALU.subtract)
                            ins = v.tensor_single_scalar(SCR8[:], SCR8[:],
                                                         0.0, ALU.min)
                            c.done(q + f"m1_{hh}", ins, "vv")
                            c.need(q + f"mexp{hh}")
                            ins = v.tensor_tensor(
                                MTALL[:, hh * 2048:(hh + 1) * 2048]
                                .rearrange("p (h t) -> p h t", h=16, t=Q),
                                scr3, gms_b, ALU.mult)
                            c.done(q + f"mth{hh}m", ins, "vv")
                        xsw3 = XSW.ap().rearrange("p (h e) -> p h e",
                                                  h=NH, e=HD)
                        dtt_b = DTT.ap()[:, ch * 32:(ch + 1) * 32] \
                            .unsqueeze(2).broadcast_to([128, NH, HD])
                        ins = v.tensor_tensor(
                            XDEL.ap().rearrange("p (h e) -> p h e",
                                                h=NH, e=HD),
                            xsw3, dtt_b, ALU.mult)
                        c.done(q + "xdeldone", ins, "vv")
                        exwt_b = EXPWT.ap()[:, ch * 32:(ch + 1) * 32] \
                            .unsqueeze(2).broadcast_to([128, NH, HD])
                        ins = v.tensor_tensor(
                            XW.ap().rearrange("p (h e) -> p h e",
                                              h=NH, e=HD),
                            xsw3, exwt_b, ALU.mult)
                        c.done(q + "xwdone", ins, "vv")
                        c.need(q + "umm")
                        exst_b = EXPST.ap()[:, ch * 32:(ch + 1) * 32] \
                            .unsqueeze(2).broadcast_to([128, NH, HD])
                        ins = v.tensor_tensor(
                            Y1.ap().rearrange("p (h e) -> p h e",
                                              h=NH, e=HD),
                            PM4[:].rearrange("p (h e) -> p h e",
                                             h=NH, e=HD),
                            exst_b, ALU.mult)
                        c.done(q + "y1done", ins, "vv")
                        lam_b = LAMC.ap().rearrange(
                            "p (h cc) -> p h cc", h=NH, cc=NCH)[
                            :, :, ch:ch + 1].broadcast_to([16, NH, HD])
                        ins = v.tensor_tensor(
                            SST.ap().rearrange("p (h e) -> p h e",
                                               h=NH, e=HD),
                            SST.ap().rearrange("p (h e) -> p h e",
                                               h=NH, e=HD),
                            lam_b, ALU.mult)
                        c.done(q + "lsc", ins, "vv")
                        for qb in range(4):
                            c.need(q + f"dsmm{qb}")
                            ins = v.tensor_tensor(
                                SST[:, qb * 512:(qb + 1) * 512],
                                SST[:, qb * 512:(qb + 1) * 512],
                                PDS2[:], ALU.add)
                            c.done(q + f"supd{qb}", ins, "vv")
                        ins = v.tensor_copy(SB16[:], SST[:])
                        c.done(q + "sb16", ins, "vv")
                        c.need(q + "yimm")
                        ins = v.tensor_tensor(Y1[:], Y1[:], PM4[:], ALU.add)
                        c.done(q + "yiadd", ins, "vv")
                        v.tensor_tensor(SCR8[:], XSW[:], DPREP[:], ALU.mult)
                        ins = v.tensor_tensor(Y1[:], Y1[:], SCR8[:], ALU.add)
                        c.done(q + "y2done", ins, "vv")
                        c.need(q + "zsil")
                        ins = v.tensor_tensor(Y1[:], Y1[:], ZTW[:], ALU.mult)
                        c.done(q + "ygate", ins, "vv")
                        c.need(q + "sq")
                        ins = v.tensor_scalar_mul(RS[:, 1:2], RS[:, 0:1],
                                                  1.0 / 2048.0)
                        c.done(q + "rsmean", ins, "vv")
                        c.need(q + "rstd")
                        v.tensor_scalar(Y1[:], Y1[:], RS[:, 2:3], None,
                                        ALU.mult)
                        ins = v.tensor_tensor(YN[:], Y1[:], NWREP[:],
                                              ALU.mult)
                        c.done(q + "yn", ins, "vv")
                        for dm in range(16):
                            c.need(q + f"yntp{dm}")
                            if ch > 0 and dm < 2:
                                c.need(f"{p}c{ch - 1}_ynspill")
                            src = PTYs[dm % 2][:]
                            if dd == "b":
                                src = src[:, ::-1]
                            ins = v.tensor_copy(
                                YNT[:, dm * 128:(dm + 1) * 128], src)
                            c.done(q + f"ynte{dm}", ins, "vv")
                c.body("vector", v_s2)

                def s_s2(s):
                    for ch in range(NCH):
                        q = f"{p}c{ch}_"
                        for hh in range(2):
                            c.need(q + f"m1_{hh}")
                            ins = s.activation(SCR8[:], SCR8[:], AF.Exp)
                            c.done(q + f"mexp{hh}", ins, "ss")
                        c.need(q + "ld")
                        if ch > 0:
                            c.need(f"{p}c{ch - 1}_ygate")
                        ins = s.activation(ZTW[:], ZTW[:], AF.Silu)
                        c.done(q + "zsil", ins, "ss")
                        c.need(q + "ygate")
                        ins = s.activation(SCR8[:], Y1[:], AF.Square,
                                           accum_out=RS[:, 0:1])
                        c.done(q + "sq", ins, "ss")
                        c.need(q + "rsmean")
                        s.activation(RS[:, 2:3], RS[:, 1:2], AF.Ln, bias=1e-5)
                        ins = s.activation(RS[:, 2:3], RS[:, 2:3], AF.Exp,
                                           scale=-0.5)
                        c.done(q + "rstd", ins, "ss")
                c.body("scalar", s_s2)

            st2.close()
            c.alias(f"{tag}{dd}_dirdone", f"{p}c{NCH - 1}_yn")

        for _di, _dd in enumerate("fb"):
            _dir(_di, _dd)

        with (
            nc.sbuf_tensor(f"{tag}ynt2", [128, 32 * 512], BF16) as YNTB,
            nc.sbuf_tensor(f"{tag}wcb0", [128, 32 * 128], BF16) as WCB0,
            nc.sbuf_tensor(f"{tag}wcb1", [128, 32 * 128], BF16) as WCB1,
            nc.psum_tensor(f"{tag}pf0", [128, 512], F32) as PF0,
            nc.psum_tensor(f"{tag}pf1", [128, 512], F32) as PF1,
        ):
            WCB = [WCB0, WCB1]
            PF = [PF0, PF1]

            def g_fin(g):
                for tb in range(TB):
                    if tb == 0:
                        c.need(f"{tag}b_c15_ynspill", f"{tag}b_c14_ynspill")
                    else:
                        c.need(f"{tag}fmm{tb - 1}_7")
                    c.dma_batch(
                        f"{tag}ldyn{tb}",
                        [(YNTB[:, (di2 * 16 + e) * 512:
                               (di2 * 16 + e + 1) * 512],
                          YND[dd2][e * 128:(e + 1) * 128,
                                   tb * 512:(tb + 1) * 512])
                         for di2, dd2 in enumerate("fb") for e in range(16)],
                        sem="u0")
                    for m in range(KB):
                        bi = tb * KB + m
                        if bi >= 2:
                            c.need(f"{tag}fev{bi - 2}")
                        c.dma_batch(
                            f"{tag}ldwc{tb}_{m}",
                            [(WCB[bi % 2][:, (di2 * 16 + e) * 128:
                                          (di2 * 16 + e + 1) * 128],
                              WCD[dd2][e * 128:(e + 1) * 128,
                                       m * 128:(m + 1) * 128])
                             for di2, dd2 in enumerate("fb")
                             for e in range(16)],
                            sem=f"w{bi % 2}")
            c.body("gpsimd", g_fin)

            def t_fin(t):
                for tb in range(TB):
                    c.need(f"{tag}ldyn{tb}")
                    for m in range(KB):
                        bi = tb * KB + m
                        c.need(f"{tag}ldwc{tb}_{m}")
                        if bi >= 2:
                            c.need(f"{tag}fev{bi - 2}")
                        ins = None
                        for kk in range(32):
                            ins = t.matmul(
                                PF[bi % 2][:],
                                WCB[bi % 2][:, kk * 128:(kk + 1) * 128],
                                YNTB[:, kk * 512:(kk + 1) * 512],
                                start=(kk == 0), stop=(kk == 31))
                        c.done(f"{tag}fmm{tb}_{m}", ins, "tt")
            c.body("tensor", t_fin)

            def v_fin(v):
                for tb in range(TB):
                    for m in range(KB):
                        bi = tb * KB + m
                        c.need(f"{tag}fmm{tb}_{m}")
                        sl = slice(m * T + tb * 512, m * T + tb * 512 + 512)
                        ins = v.scalar_tensor_tensor(
                            X[:, sl], PF[bi % 2][:], LSC[:, m:m + 1],
                            X[:, sl], ALU.mult, ALU.add)
                        c.done(f"{tag}fev{bi}", ins, "vv")
                c.alias(f"{tag}res", f"{tag}fev{TB * KB - 1}")
            c.body("vector", v_fin)

        def st_body(sy):
            c.need(f"{tag}res")
            c.dma_batch(f"{tag}sty",
                        [(W["yT"][k * 128:(k + 1) * 128, :],
                          X[:, k * T:(k + 1) * T]) for k in range(KB)])
            sy.wait_ge(c.sems["qs"].h, c.sems["qs"].n)
        c.body("sync", st_body)


def _prep_s5_inputs(ins, tag):
    import ml_dtypes
    out = {}
    for dd, nm in (("f", "fwd"), ("b", "bwd")):
        out[f"{tag}inT_{dd}"] = np.ascontiguousarray(
            ins[f"L1_attn_a_{nm}_in_proj_w"].T)
        cw = ins[f"L1_attn_a_{nm}_conv_w"].reshape(2080, 4)
        out[f"{tag}cvw_{dd}"] = np.ascontiguousarray(
            cw[:2048].reshape(16, 128, 4).transpose(1, 0, 2).reshape(128, 64))
        out[f"{tag}cvw2b_{dd}"] = np.ascontiguousarray(cw[2048:2064])
        out[f"{tag}cvw2c_{dd}"] = np.ascontiguousarray(cw[2064:2080])
        cb = ins[f"L1_attn_a_{nm}_conv_b"]
        out[f"{tag}cvb_{dd}"] = np.ascontiguousarray(
            cb[:2048].reshape(16, 128).T)
        out[f"{tag}cvb2b_{dd}"] = np.ascontiguousarray(
            cb[2048:2064].reshape(16, 1))
        out[f"{tag}cvb2c_{dd}"] = np.ascontiguousarray(
            cb[2064:2080].reshape(16, 1))
        out[f"{tag}dtb_{dd}"] = np.ascontiguousarray(
            ins[f"L1_attn_a_{nm}_dt_bias"].reshape(32, 1))
        out[f"{tag}alog_{dd}"] = np.ascontiguousarray(
            ins[f"L1_attn_a_{nm}_A_log"].reshape(32, 1))
        out[f"{tag}dprow_{dd}"] = np.ascontiguousarray(
            np.repeat(ins[f"L1_attn_a_{nm}_Dp"], 64).reshape(1, 2048))
        out[f"{tag}nwrow_{dd}"] = np.ascontiguousarray(
            ins[f"L1_attn_a_{nm}_norm_w"].reshape(1, 2048))
        out[f"{tag}op_{dd}"] = np.ascontiguousarray(
            ins[f"L1_attn_a_{nm}_out_proj_w"])
    out[f"{tag}loT"] = np.ascontiguousarray(
        ins["L1_attn_a_layer_out_proj_w"].T)
    out[f"{tag}lsc"] = np.ascontiguousarray(
        ins["L1_attn_a_layer_scale"].reshape(8, 128).T)
    out[f"{tag}eyef"] = np.eye(128, dtype=np.float32)
    m01 = np.zeros((128, 128), np.float32)
    for sg in range(128):
        m01[sg, sg:] = 1.0
    out[f"{tag}msk01"] = m01.astype(ml_dtypes.bfloat16)
    return out


def s5_decls(tag):
    d = [("xT", [D, T], F32, "ExternalInput"),
         ("eye", [128, 128], BF16, "ExternalInput"),
         (f"{tag}eyef", [128, 128], F32, "ExternalInput"),
         (f"{tag}msk01", [128, 128], BF16, "ExternalInput"),
         (f"{tag}lsc", [128, 8], F32, "ExternalInput"),
         (f"{tag}loT", [2048, 1024], F32, "ExternalInput"),
         ("yT", [D, T], F32, "ExternalOutput")]
    for dd in "fb":
        d += [(f"{tag}inT_{dd}", [D, 4160], F32, "ExternalInput"),
              (f"{tag}cvw_{dd}", [128, 64], F32, "ExternalInput"),
              (f"{tag}cvw2b_{dd}", [16, 4], F32, "ExternalInput"),
              (f"{tag}cvw2c_{dd}", [16, 4], F32, "ExternalInput"),
              (f"{tag}cvb_{dd}", [128, 16], F32, "ExternalInput"),
              (f"{tag}cvb2b_{dd}", [16, 1], F32, "ExternalInput"),
              (f"{tag}cvb2c_{dd}", [16, 1], F32, "ExternalInput"),
              (f"{tag}dtb_{dd}", [32, 1], F32, "ExternalInput"),
              (f"{tag}alog_{dd}", [32, 1], F32, "ExternalInput"),
              (f"{tag}dprow_{dd}", [1, 2048], F32, "ExternalInput"),
              (f"{tag}nwrow_{dd}", [1, 2048], F32, "ExternalInput"),
              (f"{tag}op_{dd}", [1024, 2048], F32, "ExternalInput")]
    return d


def run_s5(x, ins):
    import ml_dtypes
    nc = build_single(lambda c, W: emit_s5(c, W, "s5_"), s5_decls("s5_"))
    shared = _prep_s5_inputs(ins, "s5_")
    shared["eye"] = np.eye(128, dtype=np.float32).astype(ml_dtypes.bfloat16)
    in_maps = []
    for b in range(B):
        m = dict(shared)
        m["xT"] = np.ascontiguousarray(x[b].T)
        in_maps.append(m)
    res = run_bass_kernel_spmd(nc, in_maps, core_ids=list(range(NCORES)))
    return np.stack([np.ascontiguousarray(r["yT"].T) for r in res.results])



def chain_decls():
    d = [("xT", [D, T], F32, "ExternalInput"),
         ("yT", [D, T], F32, "ExternalOutput"),
         ("eye", [128, 128], BF16, "ExternalInput")]
    seen = {"xT", "yT", "eye"}
    for lst in (S1_DECLS, S2_DECLS, s3_decls("s3_"), s5_decls("s5_"),
                s6_decls("s6_"), s7_decls("s7_")):
        for ent in lst:
            if ent[0] in seen or ent[3] == "ExternalOutput":
                continue
            seen.add(ent[0])
            d.append(ent)
    return d


def emit_chain(c, W, nchain=None):
    import os
    nc = c.nc
    if nchain is None:
        nchain = int(os.environ.get("NCHAIN", "8"))
    tmids = [nc.dram_tensor(f"chx{i}", [D, T], F32, kind="Internal").ap()
             for i in range(7)]
    bufs = [W["xT"]] + tmids + [W["yT"]]

    def wv(i, names, tag_from=None, tag_to=None):
        d = {}
        for nm in names:
            if nm in ("xT", "yT", "eye"):
                continue
            if tag_from and tag_to and nm.startswith(tag_from):
                d[tag_to + nm[len(tag_from):]] = W[nm]
            else:
                d[nm] = W[nm]
        d["xT"] = bufs[i]
        d["yT"] = W["yT"] if i == nchain - 1 else bufs[i + 1]
        d["eye"] = W["eye"]
        return d

    s1n = [e[0] for e in S1_DECLS]
    s2n = [e[0] for e in S2_DECLS]
    s3n = [e[0] for e in s3_decls("s3_")]
    s5n = [e[0] for e in s5_decls("s5_")]
    s6n = [e[0] for e in s6_decls("s6_")]
    s7n = [e[0] for e in s7_decls("s7_")]
    steps = [
        lambda: emit_s1(c, wv(0, s1n)),
        lambda: emit_s2(c, wv(1, s2n, "s2_", "s2a_"), "s2a_"),
        lambda: emit_s3(c, wv(2, s3n), "s3_"),
        lambda: emit_s2(c, wv(3, s2n, "s2_", "s2b_"), "s2b_"),
        lambda: emit_s5(c, wv(4, s5n), "s5_"),
        lambda: emit_s6(c, wv(5, s6n, "s6_", "s6a_"), "s6a_"),
        lambda: emit_s7(c, wv(6, s7n), "s7_"),
        lambda: emit_s6(c, wv(7, s6n, "s6_", "s6b_"), "s6b_"),
    ]
    gates = [None, "sty", "s2a_sty", "s3_sty", "s2b_sty", "s5_sty",
             "s6a_sty", "s7_sty"]
    for i in range(nchain):
        c.chain_gate = gates[i]
        steps[i]()
    c.chain_gate = None


def chain_inputs(inputs):
    import ml_dtypes
    shared = {}
    shared.update(_prep_s1_inputs(inputs))
    shared.update(_prep_s2_inputs(inputs))
    shared.update(_prep_s3_inputs(inputs, "s3_"))
    shared.update(_prep_s5_inputs(inputs, "s5_"))
    shared.update(_prep_s6_inputs(inputs, "s6_"))
    shared.update(_prep_s7_inputs(inputs, "s7_"))
    shared["eye"] = np.eye(128, dtype=np.float32).astype(ml_dtypes.bfloat16)
    return shared


def kernel(**inputs):
    x = np.asarray(inputs["L0_attn_a_t"])
    nc = build_single(emit_chain, chain_decls())
    shared = chain_inputs(inputs)
    in_maps = []
    for b in range(B):
        m = dict(shared)
        m["xT"] = np.ascontiguousarray(x[b].T)
        in_maps.append(m)
    res = run_bass_kernel_spmd(nc, in_maps, core_ids=list(range(NCORES)))
    return np.stack([np.ascontiguousarray(r["yT"].T) for r in res.results])
